# version 04
# baseline (speedup 1.0000x reference)
import numpy as np
import ml_dtypes
from contextlib import ExitStack

import concourse.bass as bass
import concourse.mybir as mybir
from concourse.bass_utils import run_bass_kernel_spmd

F32 = mybir.dt.float32
BF16 = mybir.dt.bfloat16
AF = mybir.ActivationFunctionType
ALU = mybir.AluOpType
AX = mybir.AxisListType
bf16 = ml_dtypes.bfloat16

D = 2048
T = 2048
NDC = 16
NTT = 4
TTW = 512
NH = 8
DK = 256
CC = 128
NCH = T // CC
NVE = 10
NHC = 11
HINTER = 1408
GINTER = 2816
NGC = 22
EPS = 1e-5
SCL = DK ** -0.5
GAMMA = [1.0 - 2.0 ** (-5.0 - h) for h in range(NH)]
GC = [g ** CC for g in GAMMA]


class Sem:
    def __init__(self, nc, name, ctx):
        self.h = ctx.enter_context(nc.semaphore(name))
        self.n = 0


class Prog:
    ENGINES = ("tensor", "vector", "scalar", "gpsimd", "sync")

    def __init__(self, nc):
        self.nc = nc
        self.steps = {e: [] for e in self.ENGINES}

    def on(self, engine, fn):
        self.steps[engine].append(fn)

    def replay(self):
        with self.nc.Block() as block:
            for eng_name in self.ENGINES:
                fns = self.steps[eng_name]
                if not fns:
                    continue

                def body(eng, fns=fns):
                    for f in fns:
                        f(eng)

                getattr(block, eng_name)(body)


def build_nc(phases=(1, 2, 3), dev_p1_out=False, dev_dump=False, dev_scr=False):
    nc = bass.Bass("TRN2", target_bir_lowering=False, detect_race_conditions=False)
    SCR_KIND = "ExternalOutput" if (dev_dump or dev_scr) else "Internal"

    d_xT = nc.dram_tensor("xT", [D, T + 3], F32, kind="ExternalInput").ap()
    d_wq = nc.dram_tensor("wq", [D, D], F32, kind="ExternalInput").ap()
    d_wk = nc.dram_tensor("wk", [D, D], F32, kind="ExternalInput").ap()
    d_wv = nc.dram_tensor("wv", [D, D], F32, kind="ExternalInput").ap()
    d_wgt = nc.dram_tensor("wgt", [D, D], F32, kind="ExternalInput").ap()
    d_wo = nc.dram_tensor("wo", [D, D], F32, kind="ExternalInput").ap()
    d_wgate = nc.dram_tensor("wgate", [D, 2 * GINTER], F32, kind="ExternalInput").ap()
    d_wdown = nc.dram_tensor("wdown", [GINTER, D], F32, kind="ExternalInput").ap()
    d_rwT = nc.dram_tensor("rwT", [D, 8], F32, kind="ExternalInput").ap()
    d_wg8 = nc.dram_tensor("wg8", [8, D, HINTER], F32, kind="ExternalInput").ap()
    d_wu8 = nc.dram_tensor("wu8", [8, D, HINTER], F32, kind="ExternalInput").ap()
    d_wd8 = nc.dram_tensor("wd8", [8, HINTER, D], F32, kind="ExternalInput").ap()
    d_sg = nc.dram_tensor("sg", [D, GINTER], F32, kind="ExternalInput").ap()
    d_su = nc.dram_tensor("su", [D, GINTER], F32, kind="ExternalInput").ap()
    d_sd = nc.dram_tensor("sd", [GINTER, D], F32, kind="ExternalInput").ap()
    d_anw = nc.dram_tensor("anw", [D, 1], F32, kind="ExternalInput").ap()
    d_mnw = nc.dram_tensor("mnw", [D, 1], F32, kind="ExternalInput").ap()
    d_gnw = nc.dram_tensor("gnw", [DK, 1], F32, kind="ExternalInput").ap()
    d_cw = nc.dram_tensor("cw", [3, D, 4], F32, kind="ExternalInput").ap()
    d_cos = nc.dram_tensor("cosd", [128, T], F32, kind="ExternalInput").ap()
    d_sin = nc.dram_tensor("sind", [128, T], F32, kind="ExternalInput").ap()
    d_dmT = nc.dram_tensor("dmTd", [8 * 128, 128], F32, kind="ExternalInput").ap()
    d_diT = nc.dram_tensor("diTd", [8 * 128, 128], F32, kind="ExternalInput").ap()
    d_corr = nc.dram_tensor("corrd", [8, T], F32, kind="ExternalInput").ap()
    d_dkc = nc.dram_tensor("dkcd", [128, 8], F32, kind="ExternalInput").ap()
    d_eye = nc.dram_tensor("eyed", [128, 128], F32, kind="ExternalInput").ap()
    d_hmask = nc.dram_tensor("hmaskd", [128, 1], F32, kind="ExternalInput").ap()

    d_out = nc.dram_tensor("out_fm", [D, T], F32, kind="ExternalOutput").ap()

    d_xscr = nc.dram_tensor("xscr", [D, T], F32, kind=SCR_KIND).ap()
    d_hmid = nc.dram_tensor("hmidscr", [D, T], F32, kind=SCR_KIND).ap()
    d_hscr = nc.dram_tensor("hscr", [NVE * NHC, 128, T], BF16, kind="Internal").ap()
    d_oscr = nc.dram_tensor("oscr", [NDC, 128, T], BF16, kind=SCR_KIND).ap()
    d_zscr = nc.dram_tensor("zscr", [NGC, 128, T], BF16, kind=SCR_KIND).ap()
    d_grows = nc.dram_tensor("growsscr", [8, T], F32, kind="Internal").ap()
    d_rows2 = nc.dram_tensor("rows2scr", [3, T], F32, kind="Internal").ap()
    t_cchi = nc.dram_tensor("cchi", [D, 128], F32, kind="Internal")
    t_ccho = nc.dram_tensor("ccho", [2 * D, 128], F32, kind="Internal")
    t_ccsi = nc.dram_tensor("ccsi", [NH * DK, DK], F32, kind="Internal")
    t_ccso = nc.dram_tensor("ccso", [NH * 2 * DK, DK], F32, kind="Internal")
    d_cchi, d_ccho = t_cchi.ap(), t_ccho.ap()
    d_ccsi, d_ccso = t_ccsi.ap(), t_ccso.ap()
    if dev_dump:
        d_dbgA = nc.dram_tensor("dbgA", [128, 33792], BF16, kind="ExternalOutput").ap()
        d_dbgC = nc.dram_tensor("dbgC", [128, 6, 2048], BF16, kind="ExternalOutput").ap()
        d_dbgD = nc.dram_tensor("dbgD", [128, 2, 2048], F32, kind="ExternalOutput").ap()
        d_dbgS = nc.dram_tensor("dbgS", [128, 2, DK], F32, kind="ExternalOutput").ap()

    GROUPS = [[0, 1], [2, 3], [4, 5], [6, 7]]

    with ExitStack() as ctx:
        ec = ctx.enter_context
        A = ec(nc.sbuf_tensor("A", [128, 33792], BF16))
        B = ec(nc.sbuf_tensor("B", [128, 32768], BF16))
        Cq = ec(nc.sbuf_tensor("Cq", [128, 6, 2048], BF16))
        Dfp = ec(nc.sbuf_tensor("Dfp", [128, 2, 2048], F32))
        cosb = ec(nc.sbuf_tensor("cosb", [128, T], BF16))
        sinb = ec(nc.sbuf_tensor("sinb", [128, T], BF16))
        S_f = ec(nc.sbuf_tensor("S_f", [128, 2, DK], F32))
        S_b = ec(nc.sbuf_tensor("S_b", [128, 2, DK], BF16))
        Sg_b = ec(nc.sbuf_tensor("Sg_b", [128, 2, DK], BF16))
        dmT = ec(nc.sbuf_tensor("dmT", [128, 8, 128], BF16))
        diT = ec(nc.sbuf_tensor("diT", [128, 8, 128], BF16))
        dkc = ec(nc.sbuf_tensor("dkc_sb", [128, 8], F32))
        anw = ec(nc.sbuf_tensor("anw_sb", [128, 16], F32))
        mnw = ec(nc.sbuf_tensor("mnw_sb", [128, 16], F32))
        gnw = ec(nc.sbuf_tensor("gnw_sb", [128, 2], F32))
        cwt = ec(nc.sbuf_tensor("cwt", [128, 3, 16, 4], F32))
        rwT = ec(nc.sbuf_tensor("rwT_sb", [128, 16, 8], BF16))
        eyef = ec(nc.sbuf_tensor("eyef", [128, 128], F32))
        eyeb = ec(nc.sbuf_tensor("eyeb", [128, 128], BF16))
        ones = ec(nc.sbuf_tensor("ones", [128, 128], BF16))
        onesf = ec(nc.sbuf_tensor("onesf", [128, 128], F32))
        cst_eps = ec(nc.sbuf_tensor("cst_eps", [128, 1], F32))
        nc.const_aps.register(F32, EPS, cst_eps[:, 0:1])
        hmask = ec(nc.sbuf_tensor("hmask_sb", [128, 1], F32))
        tp2 = ec(nc.sbuf_tensor("tp2", [128, 64], F32))
        xh3 = ec(nc.sbuf_tensor("xh3", [128, 16, 4], F32))
        hsc = ec(nc.sbuf_tensor("hsc", [128, 96], BF16))
        ktb = ec(nc.sbuf_tensor("ktb", [128, 256], BF16))
        vtb = ec(nc.sbuf_tensor("vtb", [128, 256], BF16))
        adt = ec(nc.sbuf_tensor("adt", [128, 128], BF16))
        qtb = ec(nc.sbuf_tensor("qtb", [128, 256], BF16))

        Bf = B.bitcast(F32)
        Af = A.bitcast(F32)
        Dfb = Dfp.bitcast(BF16)

        PA = ec(nc.psum_tensor("PA", [128, 2048], F32))
        PB = ec(nc.psum_tensor("PB", [128, 2048], F32))
        PBb = PB.bitcast(BF16)

        TS = Sem(nc, "TS", ctx)
        VS = Sem(nc, "VS", ctx)
        AS = Sem(nc, "AS", ctx)
        CLD = Sem(nc, "CLD", ctx)
        XL = Sem(nc, "XL", ctx)
        XLG = Sem(nc, "XLG", ctx)
        HXG = Sem(nc, "HXG", ctx)
        CCC = Sem(nc, "CCC", ctx)
        GL = Sem(nc, "GL", ctx)
        WL = [Sem(nc, f"WL{i}", ctx) for i in range(4)]
        HSs = [Sem(nc, f"HS{i}", ctx) for i in range(2)]
        XHL = Sem(nc, "XHL", ctx)
        HLs = [Sem(nc, f"HL{i}", ctx) for i in range(2)]
        WDs = [Sem(nc, f"WD{i}", ctx) for i in range(2)]
        SX = Sem(nc, "SX", ctx)
        WPs = [Sem(nc, f"WP{i}", ctx) for i in range(3)]
        OS = Sem(nc, "OS", ctx)
        OL = Sem(nc, "OL", ctx)
        WOL = Sem(nc, "WOL", ctx)
        HMS = Sem(nc, "HMS", ctx)
        HMLs = [Sem(nc, f"HML{i}", ctx) for i in range(2)]
        WGs = [Sem(nc, f"WG{i}", ctx) for i in range(3)]
        ZSs = [Sem(nc, f"ZS{i}", ctx) for i in range(2)]
        ZLs = [Sem(nc, f"ZL{i}", ctx) for i in range(2)]
        WNs = [Sem(nc, f"WN{i}", ctx) for i in range(2)]
        CCH = Sem(nc, "CCH", ctx)
        CCSs = [Sem(nc, f"CCS{i}", ctx) for i in range(NH)]
        SDs = [Sem(nc, f"SD{i}", ctx) for i in range(2)]
        SGL = Sem(nc, "SGL", ctx)

        p = Prog(nc)

        def boot(g):
            g.dma_start(out=cosb[:, :], in_=d_cos[:, :]).then_inc(CLD.h, 16)
            g.dma_start(out=sinb[:, :], in_=d_sin[:, :]).then_inc(CLD.h, 16)
            g.dma_start(out=dmT[:, :, :],
                        in_=d_dmT.rearrange("(h p) c -> p h c", p=128)).then_inc(CLD.h, 16)
            g.dma_start(out=diT[:, :, :],
                        in_=d_diT.rearrange("(h p) c -> p h c", p=128)).then_inc(CLD.h, 16)
            g.dma_start(out=dkc[:, :], in_=d_dkc[:, :]).then_inc(CLD.h, 16)
            g.dma_start(out=anw[:, :],
                        in_=d_anw.rearrange("(c p) one -> p (c one)", p=128),
                        allow_slow_non_contiguous=True).then_inc(CLD.h, 16)
            g.dma_start(out=mnw[:, :],
                        in_=d_mnw.rearrange("(c p) one -> p (c one)", p=128),
                        allow_slow_non_contiguous=True).then_inc(CLD.h, 16)
            g.dma_start(out=gnw[:, :],
                        in_=d_gnw.rearrange("(c p) one -> p (c one)", p=128),
                        allow_slow_non_contiguous=True).then_inc(CLD.h, 16)
            g.dma_start(out=cwt[:, :, :, :],
                        in_=d_cw.rearrange("k (c p) s -> p k c s", p=128),
                        allow_slow_non_contiguous=True).then_inc(CLD.h, 16)
            g.dma_start(out=rwT[:, :, :],
                        in_=d_rwT.rearrange("(c p) e -> p c e", p=128)).then_inc(CLD.h, 16)
            g.dma_start(out=eyef[:, :], in_=d_eye[:, :]).then_inc(CLD.h, 16)
            g.dma_start(out=eyeb[:, :], in_=d_eye[:, :]).then_inc(CLD.h, 16)
            g.dma_start(out=hmask[:, :], in_=d_hmask[:, :]).then_inc(CLD.h, 16)
        p.on("gpsimd", boot)
        CLD.n = 13 * 16
        CLD_T = CLD.n

        def boot_ms(v):
            v.memset(ones[:, :], 1.0)
            v.memset(onesf[:, :], 1.0)
            ins = v.memset(cst_eps[:, :], EPS)
            ins.then_inc(VS.h, 1)
        p.on("vector", boot_ms)
        VS.n += 1

        def drow(k):
            return Dfp[0:1, 0, k * 512:(k + 1) * 512]

        def lnA(dc, c0, c1):
            return A[:, dc * 2112 + c0:dc * 2112 + c1]

        def emit_ffn(call):
            if call == 0:
                def in_ap(dc, c0, c1):
                    return d_xT.rearrange("(c p) t -> p c t", p=128)[:, dc, 3 + c0:3 + c1]
            else:
                def in_ap(dc, c0, c1):
                    return d_xscr.rearrange("(c p) t -> p c t", p=128)[:, dc, c0:c1]
            out_d = (d_out if (call == 1 or dev_p1_out) else d_xscr)

            def Xf32(dc):
                return Bf[:, 8192 + dc * 512:8192 + (dc + 1) * 512]

            def Xbf(dc):
                return B[:, dc * 512:(dc + 1) * 512]

            def Xsq(dc):
                return B[:, 8192 + dc * 512:8192 + (dc + 1) * 512]

            sx_gate = SX.n
            for tt in range(NTT):
                c0, c1 = tt * TTW, (tt + 1) * TTW
                vs_g0, ts_g0, as_g0 = VS.n, TS.n, AS.n

                def ld_f(sy, vs_g0=vs_g0, ts_g0=ts_g0, as_g0=as_g0, c0=c0, c1=c1,
                         sx_gate=sx_gate):
                    sy.wait_ge(VS.h, vs_g0)
                    sy.wait_ge(TS.h, ts_g0)
                    sy.wait_ge(AS.h, as_g0)
                    sy.wait_ge(SX.h, sx_gate)
                    for dc in range(NDC):
                        sy.dma_start(out=Xf32(dc), in_=in_ap(dc, c0, c1)).then_inc(XL.h, 16)
                p.on("sync", ld_f)
                XL.n += 16 * NDC

                def ld_b(g, vs_g0=vs_g0, ts_g0=ts_g0, c0=c0, c1=c1, sx_gate=sx_gate):
                    g.wait_ge(VS.h, vs_g0)
                    g.wait_ge(TS.h, ts_g0)
                    g.wait_ge(SX.h, sx_gate)
                    for dc in range(NDC):
                        g.dma_start(out=Xbf(dc), in_=in_ap(dc, c0, c1)).then_inc(XLG.h, 16)
                p.on("gpsimd", ld_b)
                XLG.n += 16 * NDC
                xl_need = XL.n
                xlg_need = XLG.n

                def sq_f(s, xl_need=xl_need, ts_g0=ts_g0):
                    s.wait_ge(XL.h, xl_need)
                    s.wait_ge(TS.h, ts_g0)
                    s.wait_ge(CLD.h, CLD_T)
                    for dc in range(NDC):
                        ins = s.activation(Xsq(dc), Xf32(dc), AF.Square)
                    ins.then_inc(AS.h, 1)
                p.on("scalar", sq_f)
                AS.n += 1
                as_sq = AS.n

                def st_mm(t, xl_need=xl_need, xlg_need=xlg_need,
                          as_sq=as_sq, vs_g0=vs_g0):
                    t.wait_ge(XL.h, xl_need)
                    t.wait_ge(XLG.h, xlg_need)
                    t.wait_ge(AS.h, as_sq)
                    t.wait_ge(CLD.h, CLD_T)
                    t.wait_ge(VS.h, vs_g0)
                    for dc in range(NDC):
                        t.matmul(PB[0:1, 0:512], ones[:, 0:1], Xbf(dc),
                                 start=(dc == 0), stop=(dc == NDC - 1))
                    for dc in range(NDC):
                        ins = t.matmul(PB[0:1, 512:1024], ones[:, 0:1], Xsq(dc),
                                       start=(dc == 0), stop=(dc == NDC - 1))
                    ins.then_inc(TS.h, 1)
                p.on("tensor", st_mm)
                TS.n += 1
                ts_stats = TS.n

                def rows_v(v, ts_stats=ts_stats):
                    v.wait_ge(TS.h, ts_stats)
                    v.tensor_scalar(drow(0), PB[0:1, 0:512], -1.0 / D, None, ALU.mult)
                    v.tensor_scalar(drow(2), PB[0:1, 512:1024], 1.0 / D, None, ALU.mult)
                    v.tensor_tensor(drow(3), drow(0), drow(0), ALU.mult)
                    ins = v.tensor_tensor(drow(2), drow(2), drow(3), ALU.subtract)
                    ins.then_inc(VS.h, 1)
                p.on("vector", rows_v)
                VS.n += 1
                vs_rows = VS.n

                def rows_s(s, vs_rows=vs_rows):
                    s.wait_ge(VS.h, vs_rows)
                    s.activation(drow(3), drow(2), AF.Ln, bias=EPS)
                    ins = s.activation(drow(1), drow(3), AF.Exp, scale=-0.5)
                    ins.then_inc(AS.h, 1)
                p.on("scalar", rows_s)
                AS.n += 1
                as_rows = AS.n

                def bc_mm(t, as_rows=as_rows):
                    t.wait_ge(AS.h, as_rows)
                    t.matmul(PB[:, 1024:1536], onesf[0:1, :], drow(0), start=True, stop=True)
                    ins = t.matmul(PB[:, 1536:2048], onesf[0:1, :], drow(1), start=True, stop=True)
                    ins.then_inc(TS.h, 1)
                p.on("tensor", bc_mm)
                TS.n += 1
                ts_bc = TS.n

                def ln_apply(v, ts_bc=ts_bc, tt=tt):
                    v.wait_ge(TS.h, ts_bc)
                    for dc in range(NDC):
                        t1 = Dfp[:, 1, 0:512]
                        v.tensor_tensor(t1, Xf32(dc), PB[:, 1024:1536], ALU.add)
                        ins = v.tensor_tensor(lnA(dc, tt * 512, tt * 512 + 512),
                                              t1, PB[:, 1536:2048], ALU.mult)
                    ins.then_inc(VS.h, 1)
                p.on("vector", ln_apply)
                VS.n += 1
                vs_ln = VS.n

            vs_ln_all = VS.n

            def emit_top2_unit(tt, tc):
                def rl_mm(t, tt=tt, tc=tc, vs_ln=vs_ln_all, as_prev=AS.n):
                    t.wait_ge(VS.h, vs_ln)
                    t.wait_ge(AS.h, as_prev)
                    for dc in range(NDC):
                        ins = t.matmul(PB[:, 0:8],
                                       lnA(dc, tt * 512 + tc * 128, tt * 512 + tc * 128 + 128),
                                       rwT[:, dc, :], start=(dc == 0), stop=(dc == NDC - 1))
                    ins.then_inc(TS.h, 1)
                p.on("tensor", rl_mm)
                TS.n += 1
                ts_rl = TS.n

                vs0 = VS.n

                def top2_v(v, ts_rl=ts_rl, vs0=vs0):
                    v.wait_ge(TS.h, ts_rl)
                    v.reduce_max(tp2[:, 8:9], PB[:, 0:8], axis=AX.X).then_inc(VS.h, 1)
                    v.wait_ge(VS.h, vs0 + 1)
                    v.tensor_scalar(tp2[:, 9:10], tp2[:, 8:9], -1.0, None,
                                    ALU.mult).then_inc(VS.h, 1)
                p.on("vector", top2_v)
                VS.n += 2
                vs_t2a = VS.n

                def top2_s(s, vs_t2a=vs_t2a):
                    s.wait_ge(VS.h, vs_t2a)
                    ins = s.activation(tp2[:, 16:24], PB[:, 0:8], AF.Exp, bias=tp2[:, 9:10])
                    ins.then_inc(AS.h, 1)
                p.on("scalar", top2_s)
                AS.n += 1
                as_t2 = AS.n
                vs0 = VS.n

                def top2_v3(v, as_t2=as_t2, vs0=vs0):
                    v.wait_ge(AS.h, as_t2)
                    E = tp2[:, 16:24]
                    v.tensor_scalar(tp2[:, 32:40], E, 1.0, None, ALU.is_lt).then_inc(VS.h, 1)
                    v.wait_ge(VS.h, vs0 + 1)
                    v.tensor_tensor(tp2[:, 24:32], E, tp2[:, 32:40], ALU.mult).then_inc(VS.h, 1)
                    v.wait_ge(VS.h, vs0 + 2)
                    v.reduce_max(tp2[:, 15:16], tp2[:, 24:32], axis=AX.X).then_inc(VS.h, 1)
                    v.wait_ge(VS.h, vs0 + 3)
                    v.tensor_scalar(tp2[:, 48:56], E, tp2[:, 15:16], None,
                                    ALU.is_ge).then_inc(VS.h, 1)
                    v.wait_ge(VS.h, vs0 + 4)
                    v.tensor_scalar(tp2[:, 56:57], tp2[:, 15:16], 1.0 + 1e-20, None,
                                    ALU.add).then_inc(VS.h, 1)
                p.on("vector", top2_v3)
                VS.n += 5
                vs_t2c = VS.n

                def top2_s3(s, vs_t2c=vs_t2c):
                    s.wait_ge(VS.h, vs_t2c)
                    s.activation(tp2[:, 57:58], tp2[:, 56:57], AF.Ln)
                    ins = s.activation(tp2[:, 58:59], tp2[:, 57:58], AF.Exp, scale=-1.0)
                    ins.then_inc(AS.h, 1)
                p.on("scalar", top2_s3)
                AS.n += 1
                as_t2c = AS.n
                vs0 = VS.n

                def top2_v4(v, as_t2c=as_t2c, vs0=vs0):
                    v.wait_ge(AS.h, as_t2c)
                    gk = tp2[:, 40:48]
                    v.tensor_tensor(gk, tp2[:, 16:24], tp2[:, 48:56], ALU.mult).then_inc(VS.h, 1)
                    v.wait_ge(VS.h, vs0 + 1)
                    v.tensor_scalar(gk, gk, tp2[:, 58:59], None, ALU.mult).then_inc(VS.h, 1)
                    v.wait_ge(VS.h, vs0 + 2)
                    ins = v.tensor_scalar(gk, gk, 0.5, None, ALU.mult)
                    ins.then_inc(VS.h, 1)
                p.on("vector", top2_v4)
                VS.n += 3
                vs_t2d = VS.n

                def gt_tr(t, vs_t2d=vs_t2d):
                    t.wait_ge(VS.h, vs_t2d)
                    ins = t.transpose(PB[0:8, 128:256], tp2[:, 40:48], eyef[:, :])
                    ins.then_inc(TS.h, 1)
                p.on("tensor", gt_tr)
                TS.n += 1
                ts_gt = TS.n

                def gt_ev(v, ts_gt=ts_gt, gl_g=GL.n):
                    v.wait_ge(TS.h, ts_gt)
                    v.wait_ge(GL.h, gl_g)
                    ins = v.tensor_copy(Dfp[0:8, 1, 512:640], PB[0:8, 128:256])
                    ins.then_inc(VS.h, 1)
                p.on("vector", gt_ev)
                VS.n += 1
                vs_gt = VS.n

                def gt_st(sy, vs_gt=vs_gt, tt=tt, tc=tc):
                    sy.wait_ge(VS.h, vs_gt)
                    sy.dma_start(
                        out=d_grows[:, tt * 512 + tc * 128:tt * 512 + tc * 128 + 128],
                        in_=Dfp[0:8, 1, 512:640]).then_inc(GL.h, 16)
                p.on("sync", gt_st)
                GL.n += 16

            vs_stage0 = VS.n
            ts_stage0 = TS.n
            as_stage0 = AS.n

            def wslot(s, half, dc):
                base = s * 4096 + half * 2048 + dc * 128
                return B[:, base:base + 128]

            def Gsb(c0, c1):
                return Bf[:, 8192 + c0:8192 + c1]

            s1 = Bf[:, 10240:10752]
            s2 = Bf[:, 10752:11264]

            def hslot(j):
                return B[:, 22528 + j * 512:22528 + (j + 1) * 512]

            def wsrc(ve):
                if ve < 8:
                    return (d_wg8[ve], d_wu8[ve])
                if ve == 8:
                    return (d_sg[:, 0:HINTER], d_su[:, 0:HINTER])
                return (d_sg[:, HINTER:GINTER], d_su[:, HINTER:GINTER])

            slot_gate = {}
            p1_ev = []
            gidx = 0
            units_left = [(tt_, tc_) for tt_ in range(NTT) for tc_ in range(4)]
            for kpos, ve in enumerate((8, 9, 0, 1, 2, 3, 4, 5, 6, 7)):
                src_g, src_u = wsrc(ve)
                if ve < 8:
                    for tt in range(NTT):
                        vs_gate = VS.n
                        ts_gate = TS.n

                        def g_ld(sy, ve=ve, tt=tt, vs_gate=vs_gate, ts_gate=ts_gate,
                                 gl_gate=GL.n):
                            sy.wait_ge(GL.h, gl_gate)
                            sy.wait_ge(VS.h, vs_gate)
                            sy.wait_ge(TS.h, ts_gate)
                            sy.dma_start(out=Dfp[0:1, 1, 0:512],
                                         in_=d_grows[ve:ve + 1, tt * 512:(tt + 1) * 512]
                                         ).then_inc(GL.h, 16)
                        p.on("sync", g_ld)
                        GL.n += 16
                        gl_need = GL.n

                        def g_bc(t, gl_need=gl_need, vs_gate=vs_gate, as_g=AS.n):
                            t.wait_ge(GL.h, gl_need)
                            t.wait_ge(VS.h, vs_gate)
                            t.wait_ge(AS.h, as_g)
                            ins = t.matmul(PB[:, 0:512], onesf[0:1, :], Dfp[0:1, 1, 0:512],
                                           start=True, stop=True)
                            ins.then_inc(TS.h, 1)
                        p.on("tensor", g_bc)
                        TS.n += 1
                        ts_gbc = TS.n

                        def g_ev(v, ts_gbc=ts_gbc, tt=tt):
                            v.wait_ge(TS.h, ts_gbc)
                            ins = v.tensor_copy(Gsb(tt * 512, (tt + 1) * 512), PB[:, 0:512])
                            ins.then_inc(VS.h, 1)
                        p.on("vector", g_ev)
                        VS.n += 1
                else:
                    vs_gate = VS.n

                    def g_half(v, vs_gate=vs_gate, as_g=as_stage0, ts_g=ts_stage0):
                        v.wait_ge(VS.h, vs_gate)
                        v.wait_ge(AS.h, as_g)
                        v.wait_ge(TS.h, ts_g)
                        ins = v.memset(Gsb(0, 2048), 0.5)
                        ins.then_inc(VS.h, 1)
                    p.on("vector", g_half)
                    VS.n += 1
                vs_g = VS.n

                for hc in range(NHC):
                    k = kpos * NHC + hc
                    slot = k % 4
                    ts_gate = slot_gate.get(k - 4, ts_stage0)

                    def w_ld(g, src_g=src_g, src_u=src_u, hc=hc, slot=slot,
                             ts_gate=ts_gate, ws=WL[slot]):
                        g.wait_ge(TS.h, ts_gate)
                        g.dma_start(
                            out=B[:, slot * 4096:slot * 4096 + 2048].rearrange(
                                "p (c m) -> p c m", m=128),
                            in_=src_g.rearrange("(c p) m -> p c m", p=128)[
                                :, :, hc * 128:(hc + 1) * 128]).then_inc(ws.h, 16)
                        g.dma_start(
                            out=B[:, slot * 4096 + 2048:slot * 4096 + 4096].rearrange(
                                "p (c m) -> p c m", m=128),
                            in_=src_u.rearrange("(c p) m -> p c m", p=128)[
                                :, :, hc * 128:(hc + 1) * 128]).then_inc(ws.h, 16)
                    p.on("gpsimd", w_ld)
                    WL[slot].n += 32
                    wl_need = WL[slot].n

                    for tt in range(NTT):
                        pset = gidx % 2
                        pg = PA[:, pset * 1024:pset * 1024 + 512]
                        pu = PA[:, pset * 1024 + 512:pset * 1024 + 1024]
                        ev_gate = p1_ev[gidx - 2] if gidx >= 2 else (0, 0)

                        def mm_g(t, hc=hc, tt=tt, slot=slot, pg=pg, pu=pu,
                                 wl_need=wl_need, ws=WL[slot], ev_gate=ev_gate, vs_g=vs_g):
                            t.wait_ge(ws.h, wl_need)
                            t.wait_ge(VS.h, max(ev_gate[0], vs_g))
                            t.wait_ge(AS.h, ev_gate[1])
                            for dc in range(NDC):
                                t.matmul(pg, wslot(slot, 0, dc),
                                         lnA(dc, tt * 512, tt * 512 + 512),
                                         start=(dc == 0), stop=(dc == NDC - 1))
                            for dc in range(NDC):
                                ins = t.matmul(pu, wslot(slot, 1, dc),
                                               lnA(dc, tt * 512, tt * 512 + 512),
                                               start=(dc == 0), stop=(dc == NDC - 1))
                            ins.then_inc(TS.h, 1)
                        p.on("tensor", mm_g)
                        TS.n += 1
                        ts_mm = TS.n

                        def h_silu(s, pg=pg, ts_mm=ts_mm):
                            s.wait_ge(TS.h, ts_mm)
                            ins = s.activation(s1, pg, AF.Silu)
                            ins.then_inc(AS.h, 1)
                        p.on("scalar", h_silu)
                        AS.n += 1
                        as_h = AS.n

                        hj = gidx % 2
                        hs_gate = HSs[hj].n

                        def h_mul(v, pu=pu, as_h=as_h, tt=tt, hj=hj, hs_gate=hs_gate):
                            v.wait_ge(AS.h, as_h)
                            v.wait_ge(HSs[hj].h, hs_gate)
                            v.tensor_tensor(s2, s1, pu, ALU.mult)
                            ins = v.tensor_tensor(hslot(hj), s2,
                                                  Gsb(tt * 512, (tt + 1) * 512), ALU.mult)
                            ins.then_inc(VS.h, 1)
                        p.on("vector", h_mul)
                        VS.n += 1
                        vs_h = VS.n
                        p1_ev.append((vs_h, as_h))

                        def h_st(sy, kh=ve * NHC + hc, tt=tt, hj=hj, vs_h=vs_h):
                            sy.wait_ge(VS.h, vs_h)
                            sy.dma_start(out=d_hscr[kh, :, tt * 512:(tt + 1) * 512],
                                         in_=hslot(hj)).then_inc(HSs[hj].h, 16)
                        p.on("sync", h_st)
                        HSs[hj].n += 16
                        gidx += 1
                        if kpos < 2 and units_left and gidx % 2 == 0:
                            emit_top2_unit(*units_left.pop(0))

                    slot_gate[k] = TS.n
                if kpos == 1:
                    while units_left:
                        emit_top2_unit(*units_left.pop(0))

            hs_tot = (HSs[0].n, HSs[1].n)
            ts_pass1 = TS.n
            vs_pass1 = VS.n
            as_pass1 = AS.n

            def h2slot(j, hc):
                return A[:, j * 5632 + hc * 512:j * 5632 + (hc + 1) * 512]

            def wdbuf(j, hc, dcl):
                base = 11264 + j * 11264 + hc * 1024 + dcl * 128
                return A[:, base:base + 128]

            def xhalf(dcl, c0, c1):
                return Bf[:, dcl * 2048 + c0:dcl * 2048 + c1]

            p2_ev = []
            p2_hg = []
            p2_wg = []
            g2 = 0
            for half in range(2):
                vs_gate = VS.n
                ts_gate = TS.n
                as_gate = AS.n

                def xh_ld(sy, half=half, vs_gate=vs_gate, ts_gate=ts_gate, as_gate=as_gate,
                          sx_gate=sx_gate):
                    sy.wait_ge(VS.h, vs_gate)
                    sy.wait_ge(TS.h, ts_gate)
                    sy.wait_ge(AS.h, as_gate)
                    sy.wait_ge(SX.h, sx_gate)
                    for dcl in range(8):
                        sy.dma_start(out=Bf[:, dcl * 2048:(dcl + 1) * 2048],
                                     in_=in_ap(half * 8 + dcl, 0, T)).then_inc(XHL.h, 16)
                p.on("sync", xh_ld)
                XHL.n += 16 * 8
                xhl_need = XHL.n

                for ve in range(NVE):
                    gi = half * NVE + ve
                    wj = gi % 2
                    if ve < 8:
                        wd_src = d_wd8[ve].rearrange("(c p) m -> p c m", p=128)[
                            :, :, half * 1024:(half + 1) * 1024]
                    elif ve == 8:
                        wd_src = d_sd[0:HINTER, :].rearrange("(c p) m -> p c m", p=128)[
                            :, :, half * 1024:(half + 1) * 1024]
                    else:
                        wd_src = d_sd[HINTER:GINTER, :].rearrange("(c p) m -> p c m", p=128)[
                            :, :, half * 1024:(half + 1) * 1024]
                    ts_gate2 = p2_wg[gi - 2] if gi >= 2 else ts_pass1

                    def wd_ld(g, wd_src=wd_src, wj=wj, ts_gate2=ts_gate2,
                              hs_tot=hs_tot, first=(gi == 0)):
                        g.wait_ge(TS.h, ts_gate2)
                        if first:
                            g.wait_ge(HSs[0].h, hs_tot[0])
                            g.wait_ge(HSs[1].h, hs_tot[1])
                        g.dma_start(
                            out=A[:, 11264 + wj * 11264:11264 + (wj + 1) * 11264].rearrange(
                                "p (c m) -> p c m", m=1024),
                            in_=wd_src).then_inc(WDs[wj].h, 16)
                    p.on("gpsimd", wd_ld)
                    WDs[wj].n += 16
                    wd_need = WDs[wj].n

                    for tt in range(NTT):
                        ii = gi * NTT + tt
                        hi = ii % 2
                        hl_gate = p2_hg[ii - 2] if ii >= 2 else ts_pass1

                        def h_ld(sy, ve=ve, tt=tt, hi=hi, hl_gate=hl_gate,
                                 hs_tot=hs_tot, first=(ii < 2)):
                            sy.wait_ge(TS.h, hl_gate)
                            if first:
                                sy.wait_ge(HSs[0].h, hs_tot[0])
                                sy.wait_ge(HSs[1].h, hs_tot[1])
                            for hc in range(NHC):
                                sy.dma_start(out=h2slot(hi, hc),
                                             in_=d_hscr[ve * NHC + hc, :,
                                                        tt * 512:(tt + 1) * 512]
                                             ).then_inc(HLs[hi].h, 16)
                        p.on("sync", h_ld)
                        HLs[hi].n += 16 * NHC
                        hl_need = HLs[hi].n

                        for dcl in range(8):
                            pb_ = PA[:, (g2 % 2) * 512:(g2 % 2) * 512 + 512]
                            ev_g = p2_ev[g2 - 2] if g2 >= 2 else 0

                            def wd_mm(t, wj=wj, hi=hi, dcl=dcl, pb_=pb_,
                                      wd_need=wd_need, hl_need=hl_need,
                                      ev_g=ev_g, xhl_need=xhl_need):
                                t.wait_ge(WDs[wj].h, wd_need)
                                t.wait_ge(HLs[hi].h, hl_need)
                                t.wait_ge(VS.h, ev_g)
                                t.wait_ge(XHL.h, xhl_need)
                                for hc in range(NHC):
                                    ins = t.matmul(pb_, wdbuf(wj, hc, dcl), h2slot(hi, hc),
                                                   start=(hc == 0), stop=(hc == NHC - 1))
                                ins.then_inc(TS.h, 1)
                            p.on("tensor", wd_mm)
                            TS.n += 1
                            ts_wd = TS.n

                            def x_acc(v, dcl=dcl, tt=tt, pb_=pb_, ts_wd=ts_wd):
                                v.wait_ge(TS.h, ts_wd)
                                xs = xhalf(dcl, tt * 512, (tt + 1) * 512)
                                ins = v.tensor_tensor(xs, xs, pb_, ALU.add)
                                ins.then_inc(VS.h, 1)
                            p.on("vector", x_acc)
                            VS.n += 1
                            p2_ev.append(VS.n)
                            g2 += 1

                        p2_hg.append(TS.n)
                    p2_wg.append(TS.n)

                vs_done = VS.n

                def xh_st(sy, half=half, vs_done=vs_done):
                    sy.wait_ge(VS.h, vs_done)
                    for dcl in range(8):
                        sy.dma_start(
                            out=out_d.rearrange("(c p) t -> p c t", p=128)[:, half * 8 + dcl, :],
                            in_=Bf[:, dcl * 2048:(dcl + 1) * 2048]).then_inc(SX.h, 16)
                p.on("sync", xh_st)
                SX.n += 16 * 8

        def emit_attn():
            sx_p1 = SX.n
            def cch_dma(sy, sx_p1=sx_p1):
                sy.wait_ge(SX.h, sx_p1)
                sy.dma_start(out=d_cchi[:, :], in_=d_xscr[:, 1920:2048]).then_inc(CCH.h, 16)
            p.on("sync", cch_dma)
            CCH.n += 16

            def cch_cc(g, need=CCH.n):
                g.wait_ge(CCH.h, need)
                g.collective_compute("AllGather", ALU.bypass, replica_groups=GROUPS,
                                     ins=[d_cchi], outs=[d_ccho]).then_inc(CCC.h, 1)
            p.on("gpsimd", cch_cc)
            CCC.n += 1
            cch_done = CCC.n

            def Xf32(dc):
                return Bf[:, 8192 + dc * 512:8192 + (dc + 1) * 512]

            def Xbf(dc):
                return B[:, dc * 512:(dc + 1) * 512]

            def Xsq(dc):
                return B[:, 8192 + dc * 512:8192 + (dc + 1) * 512]

            for tt in range(NTT):
                c0, c1 = tt * TTW, (tt + 1) * TTW
                vs_g0, ts_g0, as_g0 = VS.n, TS.n, AS.n

                def ld_f(sy, vs_g0=vs_g0, ts_g0=ts_g0, as_g0=as_g0, c0=c0, c1=c1,
                         sx_p1=sx_p1):
                    sy.wait_ge(VS.h, vs_g0)
                    sy.wait_ge(TS.h, ts_g0)
                    sy.wait_ge(AS.h, as_g0)
                    sy.wait_ge(SX.h, sx_p1)
                    for dc in range(NDC):
                        sy.dma_start(
                            out=Xf32(dc),
                            in_=d_xscr.rearrange("(c p) t -> p c t", p=128)[:, dc, c0:c1]
                        ).then_inc(XL.h, 16)
                p.on("sync", ld_f)
                XL.n += 16 * NDC

                def ld_b(g, vs_g0=vs_g0, ts_g0=ts_g0, c0=c0, c1=c1, sx_p1=sx_p1):
                    g.wait_ge(VS.h, vs_g0)
                    g.wait_ge(TS.h, ts_g0)
                    g.wait_ge(SX.h, sx_p1)
                    for dc in range(NDC):
                        g.dma_start(
                            out=Xbf(dc),
                            in_=d_xscr.rearrange("(c p) t -> p c t", p=128)[:, dc, c0:c1]
                        ).then_inc(XLG.h, 16)
                p.on("gpsimd", ld_b)
                XLG.n += 16 * NDC
                xl_need = XL.n
                xlg_need = XLG.n

                def sq_f(s, xl_need=xl_need, ts_g0=ts_g0):
                    s.wait_ge(XL.h, xl_need)
                    s.wait_ge(TS.h, ts_g0)
                    for dc in range(NDC):
                        ins = s.activation(Xsq(dc), Xf32(dc), AF.Square)
                    ins.then_inc(AS.h, 1)
                p.on("scalar", sq_f)
                AS.n += 1
                as_sq = AS.n

                def st_mm(t, xl_need=xl_need, xlg_need=xlg_need,
                          as_sq=as_sq, vs_g0=vs_g0):
                    t.wait_ge(XL.h, xl_need)
                    t.wait_ge(XLG.h, xlg_need)
                    t.wait_ge(AS.h, as_sq)
                    t.wait_ge(VS.h, vs_g0)
                    for dc in range(NDC):
                        t.matmul(PB[0:1, 0:512], ones[:, 0:1], Xbf(dc),
                                 start=(dc == 0), stop=(dc == NDC - 1))
                    for dc in range(NDC):
                        ins = t.matmul(PB[0:1, 512:1024], ones[:, 0:1], Xsq(dc),
                                       start=(dc == 0), stop=(dc == NDC - 1))
                    ins.then_inc(TS.h, 1)
                p.on("tensor", st_mm)
                TS.n += 1
                ts_stats = TS.n

                def rows_v(v, ts_stats=ts_stats):
                    v.wait_ge(TS.h, ts_stats)
                    v.tensor_scalar(drow(0), PB[0:1, 0:512], -1.0 / D, None, ALU.mult)
                    v.tensor_scalar(drow(2), PB[0:1, 512:1024], 1.0 / D, None, ALU.mult)
                    v.tensor_tensor(drow(3), drow(0), drow(0), ALU.mult)
                    ins = v.tensor_tensor(drow(3), drow(2), drow(3), ALU.subtract)
                    ins.then_inc(VS.h, 1)
                p.on("vector", rows_v)
                VS.n += 1
                vs_rows = VS.n

                def rows_s(s, vs_rows=vs_rows):
                    s.wait_ge(VS.h, vs_rows)
                    s.activation(drow(2), drow(3), AF.Ln, bias=EPS)
                    s.activation(drow(1), drow(2), AF.Exp, scale=-0.5)
                    ins = s.activation(drow(3), drow(2), AF.Exp, scale=-1.0)
                    ins.then_inc(AS.h, 1)
                p.on("scalar", rows_s)
                AS.n += 1
                as_r1 = AS.n

                def rows_v2(v, as_r1=as_r1):
                    v.wait_ge(AS.h, as_r1)
                    ins = v.tensor_scalar(drow(3), drow(3), -EPS, 1.0, ALU.mult, ALU.add)
                    ins.then_inc(VS.h, 1)
                p.on("vector", rows_v2)
                VS.n += 1
                vs_r2 = VS.n

                def rows_s2(s, vs_r2=vs_r2):
                    s.wait_ge(VS.h, vs_r2)
                    s.activation(drow(3), drow(3), AF.Ln, bias=EPS)
                    ins = s.activation(drow(3), drow(3), AF.Exp, scale=-0.5)
                    ins.then_inc(AS.h, 1)
                p.on("scalar", rows_s2)
                AS.n += 1
                as_r2 = AS.n

                def rows_v3(v, as_r2=as_r2):
                    v.wait_ge(AS.h, as_r2)
                    ins = v.tensor_tensor(drow(2), drow(1), drow(3), ALU.mult)
                    ins.then_inc(VS.h, 1)
                p.on("vector", rows_v3)
                VS.n += 1
                vs_r3 = VS.n

                def rows_st(sy, vs_r3=vs_r3, c0=c0, c1=c1):
                    sy.wait_ge(VS.h, vs_r3)
                    sy.dma_start(out=d_rows2[0:1, c0:c1], in_=drow(0)).then_inc(GL.h, 16)
                    sy.dma_start(out=d_rows2[1:2, c0:c1], in_=drow(1)).then_inc(GL.h, 16)
                p.on("sync", rows_st)
                GL.n += 32

                def bc_mm(t, vs_r3=vs_r3):
                    t.wait_ge(VS.h, vs_r3)
                    t.matmul(PB[:, 1024:1536], onesf[0:1, :], drow(0), start=True, stop=True)
                    ins = t.matmul(PB[:, 1536:2048], onesf[0:1, :], drow(2), start=True, stop=True)
                    ins.then_inc(TS.h, 1)
                p.on("tensor", bc_mm)
                TS.n += 1
                ts_bc = TS.n

                def xa_apply(v, ts_bc=ts_bc, tt=tt):
                    v.wait_ge(TS.h, ts_bc)
                    for dc in range(NDC):
                        t1 = Dfp[:, 1, 0:512]
                        v.tensor_tensor(t1, Xf32(dc), PB[:, 1024:1536], ALU.add)
                        ins = v.scalar_tensor_tensor(
                            lnA(dc, 3 + tt * 512, 3 + tt * 512 + 512),
                            t1, anw[:, dc:dc + 1], PB[:, 1536:2048], ALU.mult, ALU.mult)
                    ins.then_inc(VS.h, 1)
                p.on("vector", xa_apply)
                VS.n += 1

            ts_xstats = TS.n
            vs_xstats = VS.n
            as_xstats = AS.n
            gl_rows2 = GL.n

            def hx_ld(g, cch_done=cch_done, ts_xstats=ts_xstats, as_x=as_xstats):
                g.wait_ge(CCC.h, cch_done)
                g.wait_ge(TS.h, ts_xstats)
                g.wait_ge(AS.h, as_x)
                g.dma_start(out=Bf[:, 0:2048].rearrange("p (c m) -> p c m", m=128),
                            in_=d_ccho[0:D, :].rearrange("(c p) m -> p c m", p=128)
                            ).then_inc(HXG.h, 16)
            p.on("gpsimd", hx_ld)
            HXG.n += 16
            xhl_halo = HXG.n

            def hx_mask(v, xhl_halo=xhl_halo, vs0=VS.n):
                v.wait_ge(HXG.h, xhl_halo)
                v.wait_ge(CLD.h, CLD_T)
                for dc in range(NDC):
                    ins = v.tensor_scalar(
                        xh3[:, dc, 0:3],
                        Bf[:, 0:2048].rearrange("p (c m) -> p c m", m=128)[:, dc, 125:128],
                        hmask[:, 0:1], None, ALU.mult)
                ins.then_inc(VS.h, 1)
            p.on("vector", hx_mask)
            VS.n += 1
            vs_hx = VS.n

            def hx_sq(s, vs_hx=vs_hx):
                s.wait_ge(VS.h, vs_hx)
                s.activation(hsc[:, 0:48].rearrange("p (c t) -> p c t", t=3),
                             xh3[:, :, 0:3], AF.Square)
                ins = s.copy(hsc[:, 48:96].rearrange("p (c t) -> p c t", t=3),
                             xh3[:, :, 0:3])
                ins.then_inc(AS.h, 1)
            p.on("scalar", hx_sq)
            AS.n += 1
            as_hx = AS.n

            def hx_mm(t, as_hx=as_hx):
                t.wait_ge(AS.h, as_hx)
                for dc in range(NDC):
                    t.matmul(PB[0:1, 0:3], ones[:, 0:1],
                             hsc[:, 48 + dc * 3:48 + dc * 3 + 3],
                             start=(dc == 0), stop=(dc == NDC - 1))
                for dc in range(NDC):
                    ins = t.matmul(PB[0:1, 64:67], ones[:, 0:1],
                                   hsc[:, dc * 3:dc * 3 + 3],
                                   start=(dc == 0), stop=(dc == NDC - 1))
                ins.then_inc(TS.h, 1)
            p.on("tensor", hx_mm)
            TS.n += 1
            ts_hx = TS.n

            def hx_rows(v, ts_hx=ts_hx, vs0=VS.n):
                v.wait_ge(TS.h, ts_hx)
                v.tensor_scalar(drow(0)[:, 0:3], PB[0:1, 0:3], -1.0 / D, None,
                                ALU.mult).then_inc(VS.h, 1)
                v.wait_ge(VS.h, vs0 + 1)
                v.tensor_scalar(drow(2)[:, 0:3], PB[0:1, 64:67], 1.0 / D, None,
                                ALU.mult).then_inc(VS.h, 1)
                v.wait_ge(VS.h, vs0 + 2)
                v.tensor_tensor(drow(3)[:, 0:3], drow(0)[:, 0:3], drow(0)[:, 0:3],
                                ALU.mult).then_inc(VS.h, 1)
                v.wait_ge(VS.h, vs0 + 3)
                ins = v.tensor_tensor(drow(3)[:, 0:3], drow(2)[:, 0:3], drow(3)[:, 0:3],
                                      ALU.subtract)
                ins.then_inc(VS.h, 1)
            p.on("vector", hx_rows)
            VS.n += 4
            vs_hxr = VS.n

            def hx_rows_s(s, vs_hxr=vs_hxr):
                s.wait_ge(VS.h, vs_hxr)
                s.activation(drow(2)[:, 0:3], drow(3)[:, 0:3], AF.Ln, bias=EPS)
                s.activation(drow(1)[:, 0:3], drow(2)[:, 0:3], AF.Exp, scale=-0.5)
                ins = s.activation(drow(3)[:, 0:3], drow(2)[:, 0:3], AF.Exp, scale=-1.0)
                ins.then_inc(AS.h, 1)
            p.on("scalar", hx_rows_s)
            AS.n += 1
            as_hxr = AS.n

            def hx_rows_v2(v, as_hxr=as_hxr):
                v.wait_ge(AS.h, as_hxr)
                ins = v.tensor_scalar(drow(3)[:, 0:3], drow(3)[:, 0:3], -EPS, 1.0,
                                      ALU.mult, ALU.add)
                ins.then_inc(VS.h, 1)
            p.on("vector", hx_rows_v2)
            VS.n += 1
            vs_hxr2 = VS.n

            def hx_rows_s2(s, vs_hxr2=vs_hxr2):
                s.wait_ge(VS.h, vs_hxr2)
                s.activation(drow(3)[:, 0:3], drow(3)[:, 0:3], AF.Ln, bias=EPS)
                ins = s.activation(drow(3)[:, 0:3], drow(3)[:, 0:3], AF.Exp, scale=-0.5)
                ins.then_inc(AS.h, 1)
            p.on("scalar", hx_rows_s2)
            AS.n += 1
            as_hxr2 = AS.n

            def hx_rows_v3(v, as_hxr2=as_hxr2):
                v.wait_ge(AS.h, as_hxr2)
                ins = v.tensor_tensor(drow(2)[:, 0:3], drow(1)[:, 0:3], drow(3)[:, 0:3],
                                      ALU.mult)
                ins.then_inc(VS.h, 1)
            p.on("vector", hx_rows_v3)
            VS.n += 1
            vs_hxr3 = VS.n

            def hx_bc(t, vs_hxr3=vs_hxr3):
                t.wait_ge(VS.h, vs_hxr3)
                t.matmul(PB[:, 128:131], onesf[0:1, :], drow(0)[:, 0:3], start=True, stop=True)
                ins = t.matmul(PB[:, 192:195], onesf[0:1, :], drow(2)[:, 0:3],
                               start=True, stop=True)
                ins.then_inc(TS.h, 1)
            p.on("tensor", hx_bc)
            TS.n += 1
            ts_hxbc = TS.n

            def hxa_apply(v, ts_hxbc=ts_hxbc):
                v.wait_ge(TS.h, ts_hxbc)
                for dc in range(NDC):
                    t1 = Dfp[:, 1, 0:3]
                    v.tensor_tensor(t1, xh3[:, dc, 0:3], PB[:, 128:131], ALU.add)
                    ins = v.scalar_tensor_tensor(
                        lnA(dc, 0, 3), t1, anw[:, dc:dc + 1], PB[:, 192:195],
                        ALU.mult, ALU.mult)
                ins.then_inc(VS.h, 1)
            p.on("vector", hxa_apply)
            VS.n += 1
            vs_xa = VS.n

            if dev_dump:
                def dump_A(sy, vs_xa=vs_xa):
                    sy.wait_ge(VS.h, vs_xa)
                    sy.dma_start(out=d_dbgA, in_=A[:, :]).then_inc(SX.h, 16)
                p.on("sync", dump_A)
                SX.n += 16

            def preb(sl):
                base = (12288 + sl * 2051) if sl < 3 else (24576 + (sl - 3) * 2051)
                return B[:, base:base + 2051]

            DG = B[:, 20480:20992]
            osq = B[:, 20992:21504]
            s1h = Bf[:, 10752:11264]
            s2h = Bf[:, 11264:11776]

            def brow7():
                return Bf[0:1, 11776:12288]

            wring_gate = {}
            wp_k = [0]

            ts_corr_prev = [TS.n]

            def mk_qkv_units(h):
                units = []
                as_proj_d = {}
                vs_base = VS.n
                as_base = AS.n
                for ti, wsrc_d in enumerate((d_wq, d_wk, d_wv)):
                    kk = wp_k[0]
                    wp_k[0] += 1
                    slot = kk % 3
                    wp_need_d = {}

                    def u_ld(kk=kk, slot=slot, wsrc_d=wsrc_d, h=h, wp_need_d=wp_need_d):
                        ts_gate = wring_gate.get(kk - 3, ts_xstats)

                        def wp_ld(g, wsrc_d=wsrc_d, h=h, slot=slot, ts_gate=ts_gate,
                                  ws=WPs[slot], vs_first=(vs_xa if kk < 3 else 0)):
                            g.wait_ge(TS.h, ts_gate)
                            g.wait_ge(VS.h, vs_first)
                            g.dma_start(
                                out=B[:, slot * 4096:(slot + 1) * 4096].rearrange(
                                    "p (c m) -> p c m", m=256),
                                in_=wsrc_d.rearrange("(c p) m -> p c m", p=128)[
                                    :, :, h * 256:(h + 1) * 256]).then_inc(ws.h, 16)
                        p.on("gpsimd", wp_ld)
                        WPs[slot].n += 16
                        wp_need_d[0] = WPs[slot].n
                    units.append(u_ld)

                    sec_as = []
                    for hcb in range(2):
                        for tt in range(NTT):
                            gidx = hcb * 4 + tt

                            def u_pj(slot=slot, hcb=hcb, tt=tt, gidx=gidx, ti=ti,
                                     sec_as=sec_as, wp_need_d=wp_need_d):
                                pp = PA[:, (gidx % 2) * 512:(gidx % 2) * 512 + 512]
                                as_gate = sec_as[-2] if len(sec_as) >= 2 else as_base
                                vs_sec = vs_base if len(sec_as) < 2 else 0

                                def pj_mm(t, slot=slot, hcb=hcb, tt=tt, pp=pp,
                                          wp_need=wp_need_d[0], ws=WPs[slot],
                                          as_gate=as_gate, vs_sec=vs_sec):
                                    t.wait_ge(ws.h, wp_need)
                                    t.wait_ge(AS.h, as_gate)
                                    t.wait_ge(VS.h, max(vs_xa, vs_sec))
                                    for dc in range(NDC):
                                        ins = t.matmul(
                                            pp,
                                            B[:, slot * 4096 + dc * 256 + hcb * 128:
                                              slot * 4096 + dc * 256 + hcb * 128 + 128],
                                            lnA(dc, 3 + tt * 512, 3 + tt * 512 + 512),
                                            start=(dc == 0), stop=(dc == NDC - 1))
                                    ins.then_inc(TS.h, 1)
                                p.on("tensor", pj_mm)
                                TS.n += 1
                                ts_pj = TS.n

                                def pj_ev(s, ti=ti, hcb=hcb, tt=tt, pp=pp, ts_pj=ts_pj):
                                    s.wait_ge(TS.h, ts_pj)
                                    ins = s.copy(
                                        preb(ti * 2 + hcb)[:, 3 + tt * 512:3 + tt * 512 + 512],
                                        pp)
                                    ins.then_inc(AS.h, 1)
                                p.on("scalar", pj_ev)
                                AS.n += 1
                                sec_as.append(AS.n)
                            units.append(u_pj)

                    for hcb in range(2):
                        def u_pjh(slot=slot, hcb=hcb, ti=ti, wp_need_d=wp_need_d,
                                  kk=kk, last=(hcb == 1)):
                            def pjh_mm(t, slot=slot, hcb=hcb, wp_need=wp_need_d[0],
                                       ws=WPs[slot], vs_e=vs_base, as_e=AS.n):
                                t.wait_ge(ws.h, wp_need)
                                t.wait_ge(VS.h, vs_e)
                                t.wait_ge(AS.h, as_e)
                                for dc in range(NDC):
                                    ins = t.matmul(
                                        PA[:, 1024:1027],
                                        B[:, slot * 4096 + dc * 256 + hcb * 128:
                                          slot * 4096 + dc * 256 + hcb * 128 + 128],
                                        lnA(dc, 0, 3),
                                        start=(dc == 0), stop=(dc == NDC - 1))
                                ins.then_inc(TS.h, 1)
                            p.on("tensor", pjh_mm)
                            TS.n += 1
                            ts_pjh = TS.n

                            def pjh_ev(s, ti=ti, hcb=hcb, ts_pjh=ts_pjh):
                                s.wait_ge(TS.h, ts_pjh)
                                ins = s.copy(preb(ti * 2 + hcb)[:, 0:3], PA[:, 1024:1027])
                                ins.then_inc(AS.h, 1)
                            p.on("scalar", pjh_ev)
                            AS.n += 1
                            if last:
                                wring_gate[kk] = TS.n
                                as_proj_d[ti] = AS.n
                        units.append(u_pjh)
                return units, as_proj_d

            def emit_conv(h, as_proj_d):
                for ti in range(3):
                    cplane = 2 * ti
                    as_proj = as_proj_d[ti]
                    cv_as = []
                    for hcb in range(2):
                        vs_dgg = VS.n
                        ts_dgg = TS.n
                        os_dgg = OS.n

                        def dg_b(v, ti=ti, h=h, hcb=hcb, vs_dgg=vs_dgg, ts_dgg=ts_dgg,
                                 os_dgg=os_dgg):
                            v.wait_ge(VS.h, vs_dgg)
                            v.wait_ge(TS.h, ts_dgg)
                            v.wait_ge(OS.h, os_dgg)
                            for s_ in range(4):
                                ins = v.tensor_scalar(
                                    DG[:, s_ * 128:s_ * 128 + 128], eyef[:, :],
                                    cwt[:, ti, 2 * h + hcb, s_:s_ + 1], None, ALU.mult)
                            ins.then_inc(VS.h, 1)
                        p.on("vector", dg_b)
                        VS.n += 1
                        vs_dg = VS.n

                        for tt in range(NTT):
                            cgidx = hcb * 4 + tt
                            pc = PA[:, 1024 + (cgidx % 2) * 512:1024 + (cgidx % 2) * 512 + 512]
                            as_gate = cv_as[-2] if len(cv_as) >= 2 else AS.n
                            vs_sec = VS.n if len(cv_as) < 2 else 0

                            def cv_mm(t, ti=ti, hcb=hcb, tt=tt, pc=pc, vs_dg=vs_dg,
                                      as_proj=as_proj, as_gate=as_gate, vs_sec=vs_sec):
                                t.wait_ge(VS.h, max(vs_dg, vs_sec))
                                t.wait_ge(AS.h, max(as_proj, as_gate))
                                for s_ in range(4):
                                    ins = t.matmul(
                                        pc, DG[:, s_ * 128:s_ * 128 + 128],
                                        preb(ti * 2 + hcb)[:, tt * 512 + s_:tt * 512 + s_ + 512],
                                        start=(s_ == 0), stop=(s_ == 3))
                                ins.then_inc(TS.h, 1)
                            p.on("tensor", cv_mm)
                            TS.n += 1
                            ts_cv = TS.n

                            def cv_silu(s, cplane=cplane, hcb=hcb, tt=tt, pc=pc, ts_cv=ts_cv):
                                s.wait_ge(TS.h, ts_cv)
                                ins = s.activation(
                                    Cq[:, cplane + hcb, tt * 512:(tt + 1) * 512], pc, AF.Silu)
                                ins.then_inc(AS.h, 1)
                            p.on("scalar", cv_silu)
                            AS.n += 1
                            cv_as.append(AS.n)

            def fill(pend):
                if pend:
                    pend.pop(0)()

            pend = []
            as_proj_d = None
            as_proj_next = None
            for h in range(NH):
                if h == 0:
                    us0, as_proj_d = mk_qkv_units(0)
                    for u in us0:
                        u()
                emit_conv(h, as_proj_d)
                as_conv = AS.n

                vs_rope_gate = VS.n

                def rope_qk(v, as_conv=as_conv, vs_rope_gate=vs_rope_gate):
                    v.wait_ge(AS.h, as_conv)
                    v.wait_ge(VS.h, vs_rope_gate)
                    for pl in (0, 2):
                        x1 = Cq[:, pl, :]
                        x2 = Cq[:, pl + 1, :]
                        t1 = Dfb[:, 0, 0:2048]
                        t2 = Dfb[:, 0, 2048:4096]
                        t3 = Dfb[:, 1, 0:2048]
                        t4 = Dfb[:, 1, 2048:4096]
                        v.tensor_tensor(t1, x2, sinb[:, :], ALU.mult)
                        v.tensor_tensor(t2, x1, cosb[:, :], ALU.mult)
                        v.tensor_tensor(t3, x2, cosb[:, :], ALU.mult)
                        v.tensor_tensor(t4, x1, sinb[:, :], ALU.mult)
                        v.tensor_tensor(x1, t2, t1, ALU.subtract)
                        ins = v.tensor_tensor(x2, t3, t4, ALU.add)
                    ins.then_inc(VS.h, 1)
                p.on("vector", rope_qk)
                VS.n += 1
                vs_rope = VS.n

                if dev_dump and h == 0:
                    def dump_C(sy, vs_rope=vs_rope):
                        sy.wait_ge(VS.h, vs_rope)
                        sy.dma_start(out=d_dbgC, in_=Cq[:, :, :]).then_inc(SX.h, 16)
                    p.on("sync", dump_C)
                    SX.n += 16
                    def dump_C_gate(t, need=SX.n):
                        t.wait_ge(SX.h, need)
                    p.on("tensor", dump_C_gate)

                if h + 1 < NH:
                    pend, as_proj_next = mk_qkv_units(h + 1)
                else:
                    pend, as_proj_next = [], None
                for ci in range(NCH):
                    ts0, ts1 = ci * CC, (ci + 1) * CC
                    vs_b4 = b4_vs_gate[0]

                    def tr_mm(t, ci=ci, ts0=ts0, ts1=ts1, vs_b4=vs_b4, vs_rope=vs_rope,
                              vs_oev=b5_vs_gate[0], vs_del=b6_vs_gate[0]):
                        t.wait_ge(VS.h, max(vs_b4, vs_rope))
                        for dcb in range(2):
                            t.transpose(PBb[:, dcb * 128:(dcb + 1) * 128],
                                        Cq[:, 2 + dcb, ts0:ts1], eyeb[:, :])
                        for dcb in range(2):
                            t.transpose(PBb[:, 256 + dcb * 128:256 + (dcb + 1) * 128],
                                        Cq[:, 4 + dcb, ts0:ts1], eyeb[:, :])
                        for dcb in range(2):
                            ins = t.matmul(PB[:, 256:384], Cq[:, 2 + dcb, ts0:ts1],
                                           Cq[:, dcb, ts0:ts1],
                                           start=(dcb == 0), stop=(dcb == 1))
                        ins.then_inc(TS.h, 1)
                    p.on("tensor", tr_mm)
                    TS.n += 1
                    ts_tr = TS.n

                    def ret_ev(v, h=h, ci=ci, ts0=ts0, ts1=ts1, ts_tr=ts_tr):
                        v.wait_ge(TS.h, ts_tr)
                        v.tensor_scalar(ktb[:, :], PBb[:, 0:256], dkc[:, h:h + 1],
                                        None, ALU.mult)
                        v.tensor_copy(vtb[:, :], PBb[:, 256:512])
                        v.tensor_tensor(adt[:, :], PB[:, 256:384], dmT[:, h, :], ALU.mult)
                        for dcb in range(2):
                            ins = v.tensor_tensor(qtb[:, dcb * 128:(dcb + 1) * 128],
                                                  Cq[:, dcb, ts0:ts1], diT[:, h, :], ALU.mult)
                        ins.then_inc(VS.h, 1)
                    p.on("vector", ret_ev)
                    VS.n += 1
                    vs_retev = VS.n
                    b4_vs_gate[0] = vs_retev
                    fill(pend)

                    def o_mm(t, ci=ci, vs_retev=vs_retev, vs_oev=b5_vs_gate[0]):
                        t.wait_ge(VS.h, max(vs_retev, vs_oev))
                        for dvb in range(2):
                            ins = t.matmul(PB[:, 512 + dvb * 128:512 + (dvb + 1) * 128],
                                           vtb[:, dvb * 128:(dvb + 1) * 128], adt[:, :],
                                           start=(dvb == 0),
                                           stop=(ci == 0 and dvb == 1),
                                           skip_group_check=True)
                        if ci > 0:
                            for dvb in range(2):
                                for dkb in range(2):
                                    ins = t.matmul(
                                        PB[:, 512 + dvb * 128:512 + (dvb + 1) * 128],
                                        S_b[:, dkb, dvb * 128:(dvb + 1) * 128],
                                        qtb[:, dkb * 128:(dkb + 1) * 128],
                                        start=False, stop=(dvb == 1 and dkb == 1),
                                        skip_group_check=True)
                        ins.then_inc(TS.h, 1)
                    p.on("tensor", o_mm)
                    TS.n += 1
                    ts_o = TS.n
                    fill(pend)

                    def d_mm(t, vs_retev=vs_retev, vs_del=b6_vs_gate[0]):
                        t.wait_ge(VS.h, vs_del)
                        for dkb in range(2):
                            ins = t.matmul(PB[:, 1024 + dkb * 256:1024 + (dkb + 1) * 256],
                                           ktb[:, dkb * 128:(dkb + 1) * 128], vtb[:, :],
                                           start=True, stop=True)
                        ins.then_inc(TS.h, 1)
                    p.on("tensor", d_mm)
                    TS.n += 1
                    ts_d = TS.n

                    def o_ev(v, ci=ci, ts_o=ts_o):
                        v.wait_ge(TS.h, ts_o)
                        ins = v.tensor_copy(
                            Dfp[:, :, ci * 128:(ci + 1) * 128],
                            PB[:, 512:768].rearrange("p (b c) -> p b c", c=128))
                        ins.then_inc(VS.h, 1)
                    p.on("vector", o_ev)
                    VS.n += 1
                    b5_vs_gate[0] = VS.n

                    def s_upd(v, h=h, ci=ci, ts_d=ts_d, vs0=VS.n,
                              sd_prev=(SDs[(h + 1) % 2].n if ci == 0 else 0)):
                        v.wait_ge(TS.h, ts_d)
                        if ci == 0:
                            v.wait_ge(SDs[(h + 1) % 2].h, sd_prev)
                        for dkb in range(2):
                            if ci == 0:
                                ins = v.tensor_copy(
                                    S_f[:, dkb, :],
                                    PB[:, 1024 + dkb * 256:1024 + (dkb + 1) * 256])
                            else:
                                ins = v.scalar_tensor_tensor(
                                    S_f[:, dkb, :], S_f[:, dkb, :], GC[h],
                                    PB[:, 1024 + dkb * 256:1024 + (dkb + 1) * 256],
                                    ALU.mult, ALU.add)
                        ins.then_inc(VS.h, 1)
                        v.wait_ge(VS.h, vs0 + 1)
                        ins = v.tensor_copy(S_b[:, :, :], S_f[:, :, :])
                        ins.then_inc(VS.h, 1)
                    p.on("vector", s_upd)
                    VS.n += 2
                    b6_vs_gate[0] = VS.n

                while pend:
                    pend.pop(0)()
                vs_chunks = VS.n

                def s_st(sy, h=h, vs_chunks=vs_chunks):
                    sy.wait_ge(VS.h, vs_chunks)
                    sy.dma_start(
                        out=d_ccsi[h * DK:(h + 1) * DK, :].rearrange("(b p) v -> p b v", p=128),
                        in_=S_f[:, :, :]).then_inc(SDs[h % 2].h, 16)
                p.on("sync", s_st)
                SDs[h % 2].n += 16
                sd_need = SDs[h % 2].n

                def s_cc(g, h=h, sd_need=sd_need):
                    g.wait_ge(SDs[h % 2].h, sd_need)
                    g.collective_compute(
                        "AllGather", ALU.bypass, replica_groups=GROUPS,
                        ins=[d_ccsi[h * DK:(h + 1) * DK, :]],
                        outs=[d_ccso[h * 2 * DK:(h + 1) * 2 * DK, :]]).then_inc(CCSs[h].h, 1)
                p.on("gpsimd", s_cc)
                CCSs[h].n += 1

                def sg_ld(g, h=h, ts_corr_prev=ts_corr_prev[0]):
                    g.wait_ge(CCSs[h].h, 1)
                    g.wait_ge(TS.h, ts_corr_prev)
                    g.dma_start(out=Sg_b[:, :, :],
                                in_=d_ccso[h * 2 * DK:h * 2 * DK + DK, :].rearrange(
                                    "(b p) v -> p b v", p=128)).then_inc(SGL.h, 16)
                p.on("gpsimd", sg_ld)
                SGL.n += 16
                sgl_need = SGL.n

                for tt in range(NTT):
                    gl_gate = GL.n
                    vs_gate = VS.n

                    def cr_ld(sy, h=h, tt=tt, gl_gate=gl_gate, vs_gate=vs_gate):
                        sy.wait_ge(GL.h, gl_gate)
                        sy.wait_ge(VS.h, vs_gate)
                        sy.dma_start(out=brow7(),
                                     in_=d_corr[h:h + 1, tt * 512:(tt + 1) * 512]
                                     ).then_inc(GL.h, 16)
                    p.on("sync", cr_ld)
                    GL.n += 16
                    gl_need = GL.n

                    def cr_bc(t, gl_need=gl_need, as_e=AS.n, vs_e=VS.n):
                        t.wait_ge(GL.h, gl_need)
                        t.wait_ge(AS.h, as_e)
                        t.wait_ge(VS.h, vs_e)
                        ins = t.matmul(PB[:, 1536:2048], onesf[0:1, :], brow7(),
                                       start=True, stop=True)
                        ins.then_inc(TS.h, 1)
                    p.on("tensor", cr_bc)
                    TS.n += 1
                    ts_crbc = TS.n

                    def qq_mul(v, tt=tt, ts_crbc=ts_crbc, os_g=OS.n):
                        v.wait_ge(TS.h, ts_crbc)
                        v.wait_ge(OS.h, os_g)
                        for dkb in range(2):
                            ins = v.tensor_tensor(
                                Cq[:, 4, dkb * 512:(dkb + 1) * 512],
                                Cq[:, dkb, tt * 512:(tt + 1) * 512],
                                PB[:, 1536:2048], ALU.mult)
                        ins.then_inc(VS.h, 1)
                    p.on("vector", qq_mul)
                    VS.n += 1
                    vs_qq = VS.n
                    b7_read_ts[0] = TS.n

                    for dvb in range(2):
                        def cr_mm(t, dvb=dvb, vs_qq=vs_qq, sgl_need=sgl_need,
                                  vs_b5=b5_vs_gate[0]):
                            t.wait_ge(VS.h, max(vs_qq, vs_b5))
                            t.wait_ge(SGL.h, sgl_need)
                            for dkb in range(2):
                                ins = t.matmul(PB[:, 512:1024],
                                               Sg_b[:, dkb, dvb * 128:(dvb + 1) * 128],
                                               Cq[:, 4, dkb * 512:(dkb + 1) * 512],
                                               start=(dkb == 0), stop=(dkb == 1))
                            ins.then_inc(TS.h, 1)
                        p.on("tensor", cr_mm)
                        TS.n += 1
                        ts_cr = TS.n

                        def cr_add(v, dvb=dvb, tt=tt, ts_cr=ts_cr):
                            v.wait_ge(TS.h, ts_cr)
                            orr = Dfp[:, dvb, tt * 512:(tt + 1) * 512]
                            ins = v.tensor_tensor(orr, orr, PB[:, 512:1024], ALU.add)
                            ins.then_inc(VS.h, 1)
                        p.on("vector", cr_add)
                        VS.n += 1
                        b5_vs_gate[0] = VS.n
                ts_corr_prev[0] = TS.n

                if dev_dump and h == 0:
                    def dump_D(sy, vs_need=VS.n):
                        sy.wait_ge(VS.h, vs_need)
                        sy.dma_start(out=d_dbgD, in_=Dfp[:, :, :]).then_inc(SX.h, 16)
                        sy.dma_start(out=d_dbgS, in_=S_f[:, :, :]).then_inc(SX.h, 16)
                    p.on("sync", dump_D)
                    SX.n += 32
                    def dump_D_gate(v, need=SX.n):
                        v.wait_ge(SX.h, need)
                    p.on("vector", dump_D_gate)

                kk = wp_k[0]
                slot = kk % 3
                ts_gate = wring_gate.get(kk - 3, ts_xstats)

                def wg_ld(g, h=h, slot=slot, ts_gate=ts_gate, ws=WPs[slot]):
                    g.wait_ge(TS.h, ts_gate)
                    g.dma_start(
                        out=B[:, slot * 4096:(slot + 1) * 4096].rearrange(
                            "p (c m) -> p c m", m=256),
                        in_=d_wgt.rearrange("(c p) m -> p c m", p=128)[
                            :, :, h * 256:(h + 1) * 256]).then_inc(ws.h, 16)
                p.on("gpsimd", wg_ld)
                WPs[slot].n += 16
                wp_need = WPs[slot].n
                wp_k[0] += 1

                gt_as = []
                for tt in range(NTT):
                    as_gate = AS.n
                    vs_gate = VS.n

                    def gn_sq(s, tt=tt, as_gate=as_gate, vs_gate=vs_gate):
                        s.wait_ge(AS.h, as_gate)
                        s.wait_ge(VS.h, vs_gate)
                        ins = s.activation(osq[:, 0:512], Dfp[:, 0, tt * 512:(tt + 1) * 512],
                                           AF.Square)
                        ins.then_inc(AS.h, 1)
                    p.on("scalar", gn_sq)
                    AS.n += 1
                    as_sq0 = AS.n

                    def gn_mm0(t, as_sq0=as_sq0, ts_b7r=b7_read_ts[0]):
                        t.wait_ge(AS.h, as_sq0)
                        ins = t.matmul(PB[0:1, 1536:2048], ones[:, 0:1], osq[:, 0:512],
                                       start=True, stop=False)
                        ins.then_inc(TS.h, 1)
                    p.on("tensor", gn_mm0)
                    TS.n += 1
                    ts_gn0 = TS.n

                    def gn_sq1(s, tt=tt, ts_gn0=ts_gn0):
                        s.wait_ge(TS.h, ts_gn0)
                        ins = s.activation(osq[:, 0:512], Dfp[:, 1, tt * 512:(tt + 1) * 512],
                                           AF.Square)
                        ins.then_inc(AS.h, 1)
                    p.on("scalar", gn_sq1)
                    AS.n += 1
                    as_sq1 = AS.n

                    def gn_mm1(t, as_sq1=as_sq1):
                        t.wait_ge(AS.h, as_sq1)
                        ins = t.matmul(PB[0:1, 1536:2048], ones[:, 0:1], osq[:, 0:512],
                                       start=False, stop=True)
                        ins.then_inc(TS.h, 1)
                    p.on("tensor", gn_mm1)
                    TS.n += 1
                    ts_gn = TS.n

                    def gn_rows(s, ts_gn=ts_gn):
                        s.wait_ge(TS.h, ts_gn)
                        s.activation(brow7(), PB[0:1, 1536:2048], AF.Ln,
                                     bias=EPS, scale=1.0 / DK)
                        ins = s.activation(brow7(), brow7(), AF.Exp, scale=-0.5)
                        ins.then_inc(AS.h, 1)
                    p.on("scalar", gn_rows)
                    AS.n += 1
                    as_gnr = AS.n

                    def gn_bc(t, as_gnr=as_gnr, vs_e=VS.n):
                        t.wait_ge(AS.h, as_gnr)
                        t.wait_ge(VS.h, vs_e)
                        ins = t.matmul(PB[:, 1024:1536], onesf[0:1, :], brow7(),
                                       start=True, stop=True)
                        ins.then_inc(TS.h, 1)
                    p.on("tensor", gn_bc)
                    TS.n += 1
                    ts_gnbc = TS.n
                    b7_read_ts[0] = TS.n
                    b6_mm_ts[0] = TS.n

                    for dvb in range(2):
                        ggidx = tt * 2 + dvb
                        pp = PA[:, (ggidx % 2) * 512:(ggidx % 2) * 512 + 512]
                        as_gate2 = gt_as[-2] if len(gt_as) >= 2 else AS.n
                        vs_sec = VS.n if len(gt_as) < 2 else 0

                        def gp_mm(t, slot=slot, dvb=dvb, tt=tt, pp=pp, wp_need=wp_need,
                                  ws=WPs[slot], as_gate2=as_gate2, vs_xa=vs_xa,
                                  vs_sec=vs_sec):
                            t.wait_ge(ws.h, wp_need)
                            t.wait_ge(AS.h, as_gate2)
                            t.wait_ge(VS.h, vs_sec)
                            for dc in range(NDC):
                                ins = t.matmul(
                                    pp,
                                    B[:, slot * 4096 + dc * 256 + dvb * 128:
                                      slot * 4096 + dc * 256 + dvb * 128 + 128],
                                    lnA(dc, 3 + tt * 512, 3 + tt * 512 + 512),
                                    start=(dc == 0), stop=(dc == NDC - 1))
                            ins.then_inc(TS.h, 1)
                        p.on("tensor", gp_mm)
                        TS.n += 1
                        ts_gp = TS.n

                        def gate_silu(s, pp=pp, ts_gp=ts_gp):
                            s.wait_ge(TS.h, ts_gp)
                            ins = s.activation(s1h, pp, AF.Silu)
                            ins.then_inc(AS.h, 1)
                        p.on("scalar", gate_silu)
                        AS.n += 1
                        as_gs = AS.n
                        gt_as.append(as_gs)

                        def gate_mul(v, dvb=dvb, tt=tt, ts_gnbc=ts_gnbc, as_gs=as_gs,
                                     os_gate=OS.n, vs0=VS.n):
                            v.wait_ge(TS.h, ts_gnbc)
                            v.wait_ge(AS.h, as_gs)
                            v.wait_ge(OS.h, os_gate)
                            orr = Dfp[:, dvb, tt * 512:(tt + 1) * 512]
                            v.scalar_tensor_tensor(s2h, orr, gnw[:, dvb:dvb + 1],
                                                   PB[:, 1024:1536], ALU.mult,
                                                   ALU.mult).then_inc(VS.h, 1)
                            v.wait_ge(VS.h, vs0 + 1)
                            ins = v.tensor_tensor(Cq[:, 4 + dvb, tt * 512:(tt + 1) * 512],
                                                  s2h, s1h, ALU.mult)
                            ins.then_inc(VS.h, 1)
                        p.on("vector", gate_mul)
                        VS.n += 2
                        vs_gm = VS.n
                        b6_vs_gate[0] = VS.n
                        b6_read_vs[0] = VS.n

                        def o_st(sy, h=h, dvb=dvb, tt=tt, vs_gm=vs_gm):
                            sy.wait_ge(VS.h, vs_gm)
                            sy.dma_start(out=d_oscr[2 * h + dvb, :, tt * 512:(tt + 1) * 512],
                                         in_=Cq[:, 4 + dvb, tt * 512:(tt + 1) * 512]
                                         ).then_inc(OS.h, 16)
                        p.on("sync", o_st)
                        OS.n += 16

                wring_gate[kk] = TS.n
                as_proj_d = as_proj_next

            os_total = OS.n

            ts_hd = TS.n
            vs_hd = VS.n
            as_hd = AS.n

            def wo_ld(g, ts_hd=ts_hd, vs_hd=vs_hd, as_hd=as_hd):
                g.wait_ge(TS.h, ts_hd)
                g.wait_ge(VS.h, vs_hd)
                g.wait_ge(AS.h, as_hd)
                for hc in range(NDC):
                    g.dma_start(out=B[:, hc * 2048:(hc + 1) * 2048],
                                in_=d_wo[hc * 128:(hc + 1) * 128, :]).then_inc(WOL.h, 16)
            p.on("gpsimd", wo_ld)
            WOL.n += 16 * NDC
            wol_need = WOL.n

            hm_k = [0]
            for tt in range(NTT):
                vs_gate = VS.n
                ts_gate = TS.n
                as_gate = AS.n

                def ot_ld(sy, tt=tt, vs_gate=vs_gate, ts_gate=ts_gate, os_total=os_total,
                          first=(tt == 0)):
                    sy.wait_ge(TS.h, ts_gate)
                    sy.wait_ge(VS.h, vs_gate)
                    if first:
                        sy.wait_ge(OS.h, os_total)
                    sy.dma_start(out=A[:, 0:8192].rearrange("p (c m) -> p c m", m=512),
                                 in_=d_oscr.rearrange("c p t -> p c t")[
                                     :, :, tt * 512:(tt + 1) * 512]).then_inc(OL.h, 16)
                    for dc in range(NDC):
                        sy.dma_start(
                            out=Af[:, 4096 + dc * 512:4096 + (dc + 1) * 512],
                            in_=d_xscr.rearrange("(c p) t -> p c t", p=128)[
                                :, dc, tt * 512:(tt + 1) * 512]).then_inc(OL.h, 16)
                    sy.dma_start(out=drow(0), in_=d_rows2[0:1, tt * 512:(tt + 1) * 512]
                                 ).then_inc(OL.h, 16)
                    sy.dma_start(out=drow(1), in_=d_rows2[1:2, tt * 512:(tt + 1) * 512]
                                 ).then_inc(OL.h, 16)
                p.on("sync", ot_ld)
                OL.n += 16 * (1 + NDC + 2)
                ol_need = OL.n

                def wo_bc(t, ol_need=ol_need, vs_e=VS.n, as_e=AS.n):
                    t.wait_ge(OL.h, ol_need)
                    t.wait_ge(VS.h, vs_e)
                    t.wait_ge(AS.h, as_e)
                    t.matmul(PB[:, 1024:1536], onesf[0:1, :], drow(0), start=True, stop=True)
                    ins = t.matmul(PB[:, 1536:2048], onesf[0:1, :], drow(1),
                                   start=True, stop=True)
                    ins.then_inc(TS.h, 1)
                p.on("tensor", wo_bc)
                TS.n += 1
                ts_wobc = TS.n
                b7_read_ts[0] = TS.n

                for dc in range(NDC):
                    pp = PA[:, (dc % 2) * 512:(dc % 2) * 512 + 512]
                    vs_gate2 = wo_vs[-2] if len(wo_vs) >= 2 else 0

                    def wo_mm(t, dc=dc, pp=pp, wol_need=wol_need, ol_need=ol_need,
                              vs_gate2=vs_gate2, ts_wobc=ts_wobc):
                        t.wait_ge(WOL.h, wol_need)
                        t.wait_ge(OL.h, ol_need)
                        t.wait_ge(VS.h, vs_gate2)
                        for hc in range(NDC):
                            ins = t.matmul(pp, B[:, hc * 2048 + dc * 128:hc * 2048 + dc * 128 + 128],
                                           A[:, hc * 512:(hc + 1) * 512],
                                           start=(hc == 0), stop=(hc == NDC - 1))
                        ins.then_inc(TS.h, 1)
                    p.on("tensor", wo_mm)
                    TS.n += 1
                    ts_wo = TS.n

                    hs_ = hm_k[0] % 2
                    hm_gate = hm_st_gate[hs_]

                    def hmid_v(v, dc=dc, pp=pp, ts_wo=ts_wo, hs_=hs_, hm_gate=hm_gate):
                        v.wait_ge(TS.h, ts_wo)
                        v.wait_ge(HMS.h, hm_gate)
                        t1 = Af[:, 13312:13824]
                        v.tensor_tensor(t1, Af[:, 4096 + dc * 512:4096 + (dc + 1) * 512],
                                        PB[:, 1024:1536], ALU.add)
                        t2 = Af[:, 13824:14336]
                        v.tensor_tensor(t2, t1, PB[:, 1536:2048], ALU.mult)
                        ins = v.tensor_tensor(Af[:, 12288 + hs_ * 512:12288 + (hs_ + 1) * 512],
                                              t2, pp, ALU.add)
                        ins.then_inc(VS.h, 1)
                    p.on("vector", hmid_v)
                    VS.n += 1
                    wo_vs.append(VS.n)
                    vs_hm = VS.n

                    def hmid_sq(s, hs_=hs_, vs_hm=vs_hm):
                        s.wait_ge(VS.h, vs_hm)
                        ins = s.activation(A[:, 28672:29184],
                                           Af[:, 12288 + hs_ * 512:12288 + (hs_ + 1) * 512],
                                           AF.Square)
                        ins.then_inc(AS.h, 1)
                    p.on("scalar", hmid_sq)
                    AS.n += 1
                    as_hsq = AS.n

                    def ysq_mm(t, dc=dc, as_hsq=as_hsq):
                        t.wait_ge(AS.h, as_hsq)
                        ins = t.matmul(PB[0:1, 0:512], ones[:, 0:1], A[:, 28672:29184],
                                       start=(dc == 0), stop=(dc == NDC - 1))
                        ins.then_inc(TS.h, 1)
                    p.on("tensor", ysq_mm)
                    TS.n += 1

                    def hmid_st(sy, dc=dc, tt=tt, hs_=hs_, vs_hm=vs_hm):
                        sy.wait_ge(VS.h, vs_hm)
                        sy.dma_start(
                            out=d_hmid.rearrange("(c p) t -> p c t", p=128)[
                                :, dc, tt * 512:(tt + 1) * 512],
                            in_=Af[:, 12288 + hs_ * 512:12288 + (hs_ + 1) * 512]
                        ).then_inc(HMS.h, 16)
                    p.on("sync", hmid_st)
                    HMS.n += 16
                    hm_st_gate[hs_] = HMS.n
                    hm_k[0] += 1

                ts_ysq = TS.n

                def y_rows(s, ts_ysq=ts_ysq):
                    s.wait_ge(TS.h, ts_ysq)
                    s.activation(drow(2), PB[0:1, 0:512], AF.Ln, bias=EPS, scale=1.0 / D)
                    ins = s.activation(drow(2), drow(2), AF.Exp, scale=-0.5)
                    ins.then_inc(AS.h, 1)
                p.on("scalar", y_rows)
                AS.n += 1
                as_yr = AS.n

                def y_rows_st(sy, tt=tt, as_yr=as_yr):
                    sy.wait_ge(AS.h, as_yr)
                    sy.dma_start(out=d_rows2[2:3, tt * 512:(tt + 1) * 512],
                                 in_=drow(2)).then_inc(GL.h, 16)
                p.on("sync", y_rows_st)
                GL.n += 16

            hms_total = HMS.n
            gl_r3 = GL.n

            ts_yg = TS.n

            ym_k = [0]
            for tt in range(NTT):
                def yr_ld(sy, tt=tt, gl_r3=gl_r3, hms_total=hms_total, vg=VS.n, ag=AS.n):
                    sy.wait_ge(GL.h, gl_r3)
                    sy.wait_ge(VS.h, vg)
                    sy.wait_ge(AS.h, ag)
                    if tt == 0:
                        sy.wait_ge(HMS.h, hms_total)
                    sy.dma_start(out=drow(2), in_=d_rows2[2:3, tt * 512:(tt + 1) * 512]
                                 ).then_inc(OL.h, 16)
                p.on("sync", yr_ld)
                OL.n += 16
                ol_need = OL.n

                def yr_bc(t, ol_need=ol_need, vs_e=VS.n, as_e=AS.n):
                    t.wait_ge(OL.h, ol_need)
                    t.wait_ge(VS.h, vs_e)
                    t.wait_ge(AS.h, as_e)
                    ins = t.matmul(PB[:, 1536:2048], onesf[0:1, :], drow(2),
                                   start=True, stop=True)
                    ins.then_inc(TS.h, 1)
                p.on("tensor", yr_bc)
                TS.n += 1
                ts_yrbc = TS.n
                b7_read_ts[0] = TS.n

                for dc in range(NDC):
                    hs_ = ym_k[0] % 2
                    hml_gate = HMLs[hs_].n

                    def hm_ld(sy, dc=dc, tt=tt, hs_=hs_, vg=yld_vs[hs_], hml_gate=hml_gate):
                        sy.wait_ge(VS.h, vg)
                        sy.dma_start(
                            out=Af[:, hs_ * 512:(hs_ + 1) * 512],
                            in_=d_hmid.rearrange("(c p) t -> p c t", p=128)[
                                :, dc, tt * 512:(tt + 1) * 512]).then_inc(HMLs[hs_].h, 16)
                    p.on("sync", hm_ld)
                    HMLs[hs_].n += 16
                    hml_need = HMLs[hs_].n

                    def y_mul(v, dc=dc, tt=tt, hs_=hs_, hml_need=hml_need,
                              ts_yrbc=ts_yrbc, ts_yg=ts_yg):
                        v.wait_ge(HMLs[hs_].h, hml_need)
                        v.wait_ge(TS.h, max(ts_yrbc, ts_yg))
                        ins = v.scalar_tensor_tensor(
                            B[:, dc * 2048 + tt * 512:dc * 2048 + tt * 512 + 512],
                            Af[:, hs_ * 512:(hs_ + 1) * 512], mnw[:, dc:dc + 1],
                            PB[:, 1536:2048], ALU.mult, ALU.mult)
                        ins.then_inc(VS.h, 1)
                    p.on("vector", y_mul)
                    VS.n += 1
                    yld_vs[hs_] = VS.n
                    ym_k[0] += 1

            vs_y = VS.n

            s1z = Af[:, 8192:8704]
            s2z = Af[:, 8704:9216]
            zg_gate = {}
            for hc in range(NGC):
                slot = hc % 3
                ts_gate = zg_gate.get(hc - 3, TS.n if hc == 0 else 0)
                ts_gate2 = zg_gate.get(hc - 3, 0)

                def zw_ld(g, hc=hc, slot=slot, ts_gate2=ts_gate2, ws=WGs[slot],
                          vs_y=vs_y, first=(hc < 3)):
                    g.wait_ge(TS.h, ts_gate2)
                    if first:
                        g.wait_ge(VS.h, vs_y)
                    g.dma_start(
                        out=A[:, slot * 4096:slot * 4096 + 2048].rearrange(
                            "p (c m) -> p c m", m=128),
                        in_=d_wgate.rearrange("(c p) m -> p c m", p=128)[
                            :, :, hc * 128:(hc + 1) * 128]).then_inc(ws.h, 16)
                    g.dma_start(
                        out=A[:, slot * 4096 + 2048:slot * 4096 + 4096].rearrange(
                            "p (c m) -> p c m", m=128),
                        in_=d_wgate.rearrange("(c p) m -> p c m", p=128)[
                            :, :, GINTER + hc * 128:GINTER + (hc + 1) * 128]).then_inc(ws.h, 16)
                p.on("gpsimd", zw_ld)
                WGs[slot].n += 32
                wg_need = WGs[slot].n

                zj = hc % 2
                zs_gate = ZSs[zj].n

                for tt in range(NTT):
                    zgidx = hc * NTT + tt
                    pset = zgidx % 2
                    pg = PA[:, pset * 1024:pset * 1024 + 512]
                    pu = PA[:, pset * 1024 + 512:pset * 1024 + 1024]
                    ev_gate = ztail_ev[zgidx - 2] if zgidx >= 2 else (0, 0)

                    def z_mm(t, slot=slot, tt=tt, pg=pg, pu=pu, wg_need=wg_need,
                             ws=WGs[slot], ev_gate=ev_gate, vs_y=vs_y):
                        t.wait_ge(ws.h, wg_need)
                        t.wait_ge(VS.h, max(ev_gate[0], vs_y))
                        t.wait_ge(AS.h, ev_gate[1])
                        for dc in range(NDC):
                            t.matmul(pg, A[:, slot * 4096 + dc * 128:slot * 4096 + dc * 128 + 128],
                                     B[:, dc * 2048 + tt * 512:dc * 2048 + tt * 512 + 512],
                                     start=(dc == 0), stop=(dc == NDC - 1))
                        for dc in range(NDC):
                            ins = t.matmul(pu, A[:, slot * 4096 + 2048 + dc * 128:
                                                 slot * 4096 + 2048 + dc * 128 + 128],
                                           B[:, dc * 2048 + tt * 512:dc * 2048 + tt * 512 + 512],
                                           start=(dc == 0), stop=(dc == NDC - 1))
                        ins.then_inc(TS.h, 1)
                    p.on("tensor", z_mm)
                    TS.n += 1
                    ts_z = TS.n

                    def z_silu(s, pg=pg, ts_z=ts_z):
                        s.wait_ge(TS.h, ts_z)
                        ins = s.activation(s1z, pg, AF.Silu)
                        ins.then_inc(AS.h, 1)
                    p.on("scalar", z_silu)
                    AS.n += 1
                    as_z = AS.n

                    def z_mul(v, pu=pu, tt=tt, zj=zj, as_z=as_z, zs_gate=zs_gate):
                        v.wait_ge(AS.h, as_z)
                        v.wait_ge(ZSs[zj].h, zs_gate)
                        ins = v.tensor_tensor(
                            A[:, 12288 + zj * 2048 + tt * 512:12288 + zj * 2048 + tt * 512 + 512],
                            s1z, pu, ALU.mult)
                        ins.then_inc(VS.h, 1)
                    p.on("vector", z_mul)
                    VS.n += 1
                    ztail_ev.append((VS.n, as_z))

                vs_z = VS.n

                def z_st(sy, hc=hc, zj=zj, vs_z=vs_z):
                    sy.wait_ge(VS.h, vs_z)
                    sy.dma_start(out=d_zscr[hc, :, :],
                                 in_=A[:, 12288 + zj * 2048:12288 + (zj + 1) * 2048]
                                 ).then_inc(ZSs[zj].h, 16)
                p.on("sync", z_st)
                ZSs[zj].n += 16
                zg_gate[hc] = TS.n

            zs_tot = (ZSs[0].n, ZSs[1].n)

            for half in range(2):
                vs_gate = VS.n
                ts_gate = TS.n

                def xb_ld(sy, half=half, vs_gate=vs_gate, ts_gate=ts_gate):
                    sy.wait_ge(VS.h, vs_gate)
                    sy.wait_ge(TS.h, ts_gate)
                    for dcl in range(8):
                        sy.dma_start(
                            out=Bf[:, dcl * 2048:(dcl + 1) * 2048],
                            in_=d_xscr.rearrange("(c p) t -> p c t", p=128)[
                                :, half * 8 + dcl, :]).then_inc(XHL.h, 16)
                p.on("sync", xb_ld)
                XHL.n += 16 * 8
                xhl_need = XHL.n

                for tt in range(NTT):
                    ii = half * NTT + tt
                    zi = ii % 2
                    zl_gate = tb_zg[ii - 2] if ii >= 2 else 0

                    def zt_ld(sy, tt=tt, zi=zi, zl_gate=zl_gate, zs_tot=zs_tot,
                              first=(ii < 2), ts_g=TS.n):
                        sy.wait_ge(TS.h, zl_gate)
                        if first:
                            sy.wait_ge(ZSs[0].h, zs_tot[0])
                            sy.wait_ge(ZSs[1].h, zs_tot[1])
                            sy.wait_ge(TS.h, ts_g)
                        for hc in range(NGC):
                            sy.dma_start(
                                out=A[:, zi * 11264 + hc * 512:zi * 11264 + (hc + 1) * 512],
                                in_=d_zscr[hc, :, tt * 512:(tt + 1) * 512]
                            ).then_inc(ZLs[zi].h, 16)
                    p.on("sync", zt_ld)
                    ZLs[zi].n += 16 * NGC
                    zl_need = ZLs[zi].n

                    for dcl in range(8):
                        dc = half * 8 + dcl
                        wi = (ii * 8 + dcl) % 2
                        wn_gate = tb_wg[ii * 8 + dcl - 2] if ii * 8 + dcl >= 2 else 0

                        def wn_ld(g, dc=dc, wi=wi, wn_gate=wn_gate, vs_y=vs_y,
                                  first=(ii * 8 + dcl < 2)):
                            g.wait_ge(TS.h, wn_gate)
                            if first:
                                g.wait_ge(VS.h, vs_y)
                            g.dma_start(
                                out=A[:, 22528 + wi * 2816:22528 + (wi + 1) * 2816].rearrange(
                                    "p (c m) -> p c m", m=128),
                                in_=d_wdown.rearrange("(c p) m -> p c m", p=128)[
                                    :, :, dc * 128:(dc + 1) * 128]).then_inc(WNs[wi].h, 16)
                        p.on("gpsimd", wn_ld)
                        WNs[wi].n += 16
                        wn_need = WNs[wi].n

                        hs_ = (ii * 8 + dcl) % 2
                        hml_gate = HMLs[hs_].n

                        def hm2_ld(sy, dc=dc, tt=tt, hs_=hs_, vg=tbh_vs[hs_]):
                            sy.wait_ge(VS.h, vg)
                            sy.dma_start(
                                out=Af[:, 14080 + hs_ * 512:14080 + (hs_ + 1) * 512],
                                in_=d_hmid.rearrange("(c p) t -> p c t", p=128)[
                                    :, dc, tt * 512:(tt + 1) * 512]).then_inc(HMLs[hs_].h, 16)
                        p.on("sync", hm2_ld)
                        HMLs[hs_].n += 16
                        hml_need = HMLs[hs_].n

                        g3 = tb_g[0]
                        pp = PA[:, (g3 % 2) * 512:(g3 % 2) * 512 + 512]
                        ev_g = tb_ev[g3 - 2] if g3 >= 2 else 0

                        def wn_mm(t, wi=wi, zi=zi, pp=pp, wn_need=wn_need,
                                  zl_need=zl_need, ev_g=ev_g, xhl_need=xhl_need):
                            t.wait_ge(WNs[wi].h, wn_need)
                            t.wait_ge(ZLs[zi].h, zl_need)
                            t.wait_ge(VS.h, ev_g)
                            t.wait_ge(XHL.h, xhl_need)
                            for hc in range(NGC):
                                ins = t.matmul(pp,
                                               A[:, 22528 + wi * 2816 + hc * 128:
                                                 22528 + wi * 2816 + (hc + 1) * 128],
                                               A[:, zi * 11264 + hc * 512:
                                                 zi * 11264 + (hc + 1) * 512],
                                               start=(hc == 0), stop=(hc == NGC - 1))
                            ins.then_inc(TS.h, 1)
                        p.on("tensor", wn_mm)
                        TS.n += 1
                        ts_wn = TS.n

                        def x_fin(v, dcl=dcl, tt=tt, hs_=hs_, pp=pp, ts_wn=ts_wn,
                                  hml_need=hml_need):
                            v.wait_ge(TS.h, ts_wn)
                            v.wait_ge(HMLs[hs_].h, hml_need)
                            xs = Bf[:, dcl * 2048 + tt * 512:dcl * 2048 + tt * 512 + 512]
                            v.tensor_tensor(xs, xs, pp, ALU.add)
                            ins = v.tensor_tensor(
                                xs, xs, Af[:, 14080 + hs_ * 512:14080 + (hs_ + 1) * 512],
                                ALU.add)
                            ins.then_inc(VS.h, 1)
                        p.on("vector", x_fin)
                        VS.n += 1
                        tb_ev.append(VS.n)
                        tbh_vs[hs_] = VS.n
                        tb_g[0] += 1
                        tb_wg.append(TS.n)

                    tb_zg.append(TS.n)

                vs_done = VS.n

                p2out = d_xscr if 3 in phases else d_out

                def xb_st(sy, half=half, vs_done=vs_done, p2out=p2out):
                    sy.wait_ge(VS.h, vs_done)
                    for dcl in range(8):
                        sy.dma_start(
                            out=p2out.rearrange("(c p) t -> p c t", p=128)[
                                :, half * 8 + dcl, :],
                            in_=Bf[:, dcl * 2048:(dcl + 1) * 2048]).then_inc(SX.h, 16)
                p.on("sync", xb_st)
                SX.n += 16 * 8

        p2h_as = []
        p2c_as = []
        p2g_as = []
        b7_gate = [0]
        b7_read_ts = [0]
        b4_vs_gate = [0]
        b5_vs_gate = [0]
        b6_vs_gate = [0]
        b6_mm_ts = [0]
        b6_read_vs = [0]
        osn_gate = [0]
        wo_vs = []
        hm_st_gate = [0, 0]
        yld_vs = [0, 0]
        ztail_ev = []
        tb_zg = []
        tb_wg = []
        tb_ev = []
        tbh_vs = [0, 0]
        tb_g = [0]

        if 1 in phases:
            emit_ffn(0)
        if 2 in phases:
            emit_attn()
        if 3 in phases:
            emit_ffn(1)

        sx_total = SX.n

        def fin(sy, sx_total=sx_total):
            sy.wait_ge(SX.h, sx_total)
        p.on("sync", fin)

        p.replay()
    return nc


def _host_constants(core):
    b, half = core // 2, core % 2
    t0 = half * T
    pos = np.arange(t0, t0 + T, dtype=np.float64)
    invf = 10000.0 ** (-np.arange(128, dtype=np.float64) / 128.0)
    ang = pos[None, :] * invf[:, None]
    cosd = np.cos(ang).astype(np.float32)
    sind = np.sin(ang).astype(np.float32)
    gam = np.array(GAMMA, dtype=np.float64)
    i = np.arange(CC, dtype=np.float64)
    dmT = np.zeros((8 * 128, 128), np.float32)
    diT = np.zeros((8 * 128, 128), np.float32)
    for h in range(NH):
        diff = i[None, :] - i[:, None]
        m = np.where(diff >= 0, gam[h] ** np.maximum(diff, 0), 0.0) * SCL
        dmT[h * 128:(h + 1) * 128, :] = m.astype(np.float32)
        diT[h * 128:(h + 1) * 128, :] = np.broadcast_to(
            (gam[h] ** (i + 1) * SCL)[None, :], (128, 128)).astype(np.float32)
    dkcd = np.zeros((128, 8), np.float32)
    for h in range(NH):
        dkcd[:, h] = (gam[h] ** (CC - 1 - i)).astype(np.float32)
    corrd = np.zeros((8, T), np.float32)
    if half == 1:
        tloc = np.arange(T, dtype=np.float64)
        for h in range(NH):
            corrd[h] = (gam[h] ** (tloc + 1) * SCL).astype(np.float32)
    hmaskd = np.full((128, 1), 1.0 if half == 1 else 0.0, np.float32)
    return cosd, sind, dmT, diT, dkcd, corrd, hmaskd


def prepare_in_maps(inputs):
    x = np.asarray(inputs["L0_attn_hidden_states"], np.float32).reshape(4 * 4096, D)
    eye = np.eye(128, dtype=np.float32)
    shared = {
        "wq": np.ascontiguousarray(inputs["L0_attn_Wq"], np.float32),
        "wk": np.ascontiguousarray(inputs["L0_attn_Wk"], np.float32),
        "wv": np.ascontiguousarray(inputs["L0_attn_Wv"], np.float32),
        "wgt": np.ascontiguousarray(inputs["L0_attn_Wg"], np.float32),
        "wo": np.ascontiguousarray(inputs["L0_attn_Wo"], np.float32),
        "wgate": np.ascontiguousarray(inputs["L0_attn_W_gate"], np.float32),
        "wdown": np.ascontiguousarray(inputs["L0_attn_W_down"], np.float32),
        "rwT": np.ascontiguousarray(np.asarray(inputs["L0_ffn_router_w"], np.float32).T),
        "wg8": np.ascontiguousarray(inputs["L0_ffn_wg"], np.float32),
        "wu8": np.ascontiguousarray(inputs["L0_ffn_wu"], np.float32),
        "wd8": np.ascontiguousarray(inputs["L0_ffn_wd"], np.float32),
        "sg": np.ascontiguousarray(inputs["L0_ffn_sg"], np.float32),
        "su": np.ascontiguousarray(inputs["L0_ffn_su"], np.float32),
        "sd": np.ascontiguousarray(inputs["L0_ffn_sd"], np.float32),
        "anw": np.ascontiguousarray(
            np.asarray(inputs["L0_attn_attn_norm_w"], np.float32).reshape(D, 1)),
        "mnw": np.ascontiguousarray(
            np.asarray(inputs["L0_attn_mlp_norm_w"], np.float32).reshape(D, 1)),
        "gnw": np.ascontiguousarray(
            np.asarray(inputs["L0_attn_gnorm_w"], np.float32).reshape(DK, 1)),
        "cw": np.ascontiguousarray(np.stack(
            [inputs["L0_attn_conv_wq"], inputs["L0_attn_conv_wk"],
             inputs["L0_attn_conv_wv"]]).astype(np.float32)),
        "eyed": eye,
    }
    in_maps = []
    for core in range(8):
        b, half = core // 2, core % 2
        r0 = b * 4096 + half * T
        xT = np.zeros((D, T + 3), np.float32)
        xT[:, 3:] = x[r0:r0 + T].T
        if half == 1:
            xT[:, 0:3] = x[r0 - 3:r0].T
        cosd, sind, dmT, diT, dkcd, corrd, hmaskd = _host_constants(core)
        m = dict(shared)
        m.update({"xT": np.ascontiguousarray(xT), "cosd": cosd, "sind": sind,
                  "dmTd": dmT, "diTd": diT, "dkcd": dkcd, "corrd": corrd,
                  "hmaskd": hmaskd})
        in_maps.append(m)
    return in_maps


def kernel(**inputs):
    in_maps = prepare_in_maps(inputs)
    nc = build_nc()
    res = run_bass_kernel_spmd(nc, in_maps, core_ids=list(range(8)))
    out = np.empty((4, 4096, D), np.float32)
    for core in range(8):
        b, half = core // 2, core % 2
        out[b, half * T:(half + 1) * T, :] = res.results[core]["out_fm"].T
    return out


# version 06 (final)
# speedup vs baseline: 1.1090x; 1.1090x over previous
import numpy as np
import ml_dtypes
from contextlib import ExitStack

import concourse.bass as bass
import concourse.mybir as mybir
from concourse.bass_utils import run_bass_kernel_spmd

F32 = mybir.dt.float32
BF16 = mybir.dt.bfloat16
AF = mybir.ActivationFunctionType
ALU = mybir.AluOpType
AX = mybir.AxisListType
bf16 = ml_dtypes.bfloat16

D = 2048
T = 2048
NDC = 16
NTT = 4
TTW = 512
NH = 8
DK = 256
CC = 128
NCH = T // CC
NVE = 10
NHC = 11
HINTER = 1408
GINTER = 2816
NGC = 22
EPS = 1e-5
SCL = DK ** -0.5
GAMMA = [1.0 - 2.0 ** (-5.0 - h) for h in range(NH)]
GC = [g ** CC for g in GAMMA]


class Sem:
    def __init__(self, nc, name, ctx):
        self.h = ctx.enter_context(nc.semaphore(name))
        self.n = 0


class Prog:
    ENGINES = ("tensor", "vector", "scalar", "gpsimd", "sync")

    def __init__(self, nc):
        self.nc = nc
        self.steps = {e: [] for e in self.ENGINES}

    def on(self, engine, fn):
        self.steps[engine].append(fn)

    def replay(self):
        with self.nc.Block() as block:
            for eng_name in self.ENGINES:
                fns = self.steps[eng_name]
                if not fns:
                    continue

                def body(eng, fns=fns):
                    for f in fns:
                        f(eng)

                getattr(block, eng_name)(body)


def build_nc(phases=(1, 2, 3), dev_p1_out=False, dev_dump=False, dev_scr=False):
    nc = bass.Bass("TRN2", target_bir_lowering=False, detect_race_conditions=False)
    SCR_KIND = "ExternalOutput" if (dev_dump or dev_scr) else "Internal"

    d_xT = nc.dram_tensor("xT", [D, T + 3], F32, kind="ExternalInput").ap()
    d_wq = nc.dram_tensor("wq", [D, D], F32, kind="ExternalInput").ap()
    d_wk = nc.dram_tensor("wk", [D, D], F32, kind="ExternalInput").ap()
    d_wv = nc.dram_tensor("wv", [D, D], F32, kind="ExternalInput").ap()
    d_wgt = nc.dram_tensor("wgt", [D, D], F32, kind="ExternalInput").ap()
    d_wo = nc.dram_tensor("wo", [D, D], F32, kind="ExternalInput").ap()
    d_wgate = nc.dram_tensor("wgate", [D, 2 * GINTER], F32, kind="ExternalInput").ap()
    d_wdown = nc.dram_tensor("wdown", [GINTER, D], F32, kind="ExternalInput").ap()
    d_rwT = nc.dram_tensor("rwT", [D, 8], F32, kind="ExternalInput").ap()
    d_wg8 = nc.dram_tensor("wg8", [8, D, HINTER], F32, kind="ExternalInput").ap()
    d_wu8 = nc.dram_tensor("wu8", [8, D, HINTER], F32, kind="ExternalInput").ap()
    d_wd8 = nc.dram_tensor("wd8", [8, HINTER, D], F32, kind="ExternalInput").ap()
    d_sg = nc.dram_tensor("sg", [D, GINTER], F32, kind="ExternalInput").ap()
    d_su = nc.dram_tensor("su", [D, GINTER], F32, kind="ExternalInput").ap()
    d_sd = nc.dram_tensor("sd", [GINTER, D], F32, kind="ExternalInput").ap()
    d_anw = nc.dram_tensor("anw", [D, 1], F32, kind="ExternalInput").ap()
    d_mnw = nc.dram_tensor("mnw", [D, 1], F32, kind="ExternalInput").ap()
    d_gnw = nc.dram_tensor("gnw", [DK, 1], F32, kind="ExternalInput").ap()
    d_cw = nc.dram_tensor("cw", [3, D, 4], F32, kind="ExternalInput").ap()
    d_cos = nc.dram_tensor("cosd", [128, T], F32, kind="ExternalInput").ap()
    d_sin = nc.dram_tensor("sind", [128, T], F32, kind="ExternalInput").ap()
    d_dmT = nc.dram_tensor("dmTd", [8 * 128, 128], F32, kind="ExternalInput").ap()
    d_diT = nc.dram_tensor("diTd", [8 * 128, 128], F32, kind="ExternalInput").ap()
    d_corr = nc.dram_tensor("corrd", [8, T], F32, kind="ExternalInput").ap()
    d_dkc = nc.dram_tensor("dkcd", [128, 8], F32, kind="ExternalInput").ap()
    d_eye = nc.dram_tensor("eyed", [128, 128], F32, kind="ExternalInput").ap()
    d_hmask = nc.dram_tensor("hmaskd", [128, 1], F32, kind="ExternalInput").ap()

    d_out = nc.dram_tensor("out_fm", [D, T], F32, kind="ExternalOutput").ap()

    d_xscr = nc.dram_tensor("xscr", [D, T], F32, kind=SCR_KIND).ap()
    d_hmid = nc.dram_tensor("hmidscr", [D, T], F32, kind=SCR_KIND).ap()
    d_hscr = nc.dram_tensor("hscr", [NVE * NHC, 128, T], BF16, kind="Internal").ap()
    d_oscr = nc.dram_tensor("oscr", [NDC, 128, T], BF16, kind=SCR_KIND).ap()
    d_zscr = nc.dram_tensor("zscr", [NGC, 128, T], BF16, kind=SCR_KIND).ap()
    d_grows = nc.dram_tensor("growsscr", [8, T], F32, kind="Internal").ap()
    d_rows2 = nc.dram_tensor("rows2scr", [3, T], F32, kind="Internal").ap()
    t_cchi = nc.dram_tensor("cchi", [D, 128], F32, kind="Internal")
    t_ccho = nc.dram_tensor("ccho", [2 * D, 128], F32, kind="Internal")
    t_ccsi = nc.dram_tensor("ccsi", [NH * DK, DK], F32, kind="Internal")
    t_ccso = nc.dram_tensor("ccso", [NH * 2 * DK, DK], F32, kind="Internal")
    d_cchi, d_ccho = t_cchi.ap(), t_ccho.ap()
    d_ccsi, d_ccso = t_ccsi.ap(), t_ccso.ap()
    if dev_dump:
        d_dbgA = nc.dram_tensor("dbgA", [128, 33792], BF16, kind="ExternalOutput").ap()
        d_dbgC = nc.dram_tensor("dbgC", [128, 6, 2048], BF16, kind="ExternalOutput").ap()
        d_dbgD = nc.dram_tensor("dbgD", [128, 2, 2048], F32, kind="ExternalOutput").ap()
        d_dbgS = nc.dram_tensor("dbgS", [128, 2, DK], F32, kind="ExternalOutput").ap()

    GROUPS = [[0, 1], [2, 3], [4, 5], [6, 7]]

    with ExitStack() as ctx:
        ec = ctx.enter_context
        A = ec(nc.sbuf_tensor("A", [128, 33792], BF16))
        B = ec(nc.sbuf_tensor("B", [128, 32768], BF16))
        Cq = ec(nc.sbuf_tensor("Cq", [128, 6, 2048], BF16))
        Dfp = ec(nc.sbuf_tensor("Dfp", [128, 2, 2048], F32))
        cosb = ec(nc.sbuf_tensor("cosb", [128, T], BF16))
        sinb = ec(nc.sbuf_tensor("sinb", [128, T], BF16))
        S_f = ec(nc.sbuf_tensor("S_f", [128, 2, DK], F32))
        S_b = ec(nc.sbuf_tensor("S_b", [128, 2, DK], BF16))
        Sg_b = ec(nc.sbuf_tensor("Sg_b", [128, 2, DK], BF16))
        dmT = ec(nc.sbuf_tensor("dmT", [128, 8, 128], BF16))
        diT = ec(nc.sbuf_tensor("diT", [128, 8, 128], BF16))
        dkc = ec(nc.sbuf_tensor("dkc_sb", [128, 8], F32))
        anw = ec(nc.sbuf_tensor("anw_sb", [128, 16], F32))
        mnw = ec(nc.sbuf_tensor("mnw_sb", [128, 16], F32))
        gnw = ec(nc.sbuf_tensor("gnw_sb", [128, 2], F32))
        cwt = ec(nc.sbuf_tensor("cwt", [128, 3, 16, 4], F32))
        rwT = ec(nc.sbuf_tensor("rwT_sb", [128, 16, 8], BF16))
        eyef = ec(nc.sbuf_tensor("eyef", [128, 128], F32))
        eyeb = ec(nc.sbuf_tensor("eyeb", [128, 128], BF16))
        ones = ec(nc.sbuf_tensor("ones", [128, 128], BF16))
        onesf = ec(nc.sbuf_tensor("onesf", [128, 128], F32))
        cst_eps = ec(nc.sbuf_tensor("cst_eps", [128, 1], F32))
        nc.const_aps.register(F32, EPS, cst_eps[:, 0:1])
        hmask = ec(nc.sbuf_tensor("hmask_sb", [128, 1], F32))
        tp2 = ec(nc.sbuf_tensor("tp2", [128, 64], F32))
        xh3 = ec(nc.sbuf_tensor("xh3", [128, 16, 4], F32))
        hsc = ec(nc.sbuf_tensor("hsc", [128, 96], BF16))
        ktb = ec(nc.sbuf_tensor("ktb", [128, 256], BF16))
        vtb = ec(nc.sbuf_tensor("vtb", [128, 256], BF16))
        adt = ec(nc.sbuf_tensor("adt", [128, 128], BF16))
        qtb = ec(nc.sbuf_tensor("qtb", [128, 256], BF16))

        Bf = B.bitcast(F32)
        Af = A.bitcast(F32)
        Dfb = Dfp.bitcast(BF16)

        PA = ec(nc.psum_tensor("PA", [128, 2048], F32))
        PB = ec(nc.psum_tensor("PB", [128, 2048], F32))
        PBb = PB.bitcast(BF16)

        TS = Sem(nc, "TS", ctx)
        VS = Sem(nc, "VS", ctx)
        AS = Sem(nc, "AS", ctx)
        CLD = Sem(nc, "CLD", ctx)
        XL = Sem(nc, "XL", ctx)
        XLG = Sem(nc, "XLG", ctx)
        HXG = Sem(nc, "HXG", ctx)
        CCC = Sem(nc, "CCC", ctx)
        GL = Sem(nc, "GL", ctx)
        WL = [Sem(nc, f"WL{i}", ctx) for i in range(4)]
        HSs = [Sem(nc, f"HS{i}", ctx) for i in range(2)]
        XHL = Sem(nc, "XHL", ctx)
        HLs = [Sem(nc, f"HL{i}", ctx) for i in range(2)]
        WDs = [Sem(nc, f"WD{i}", ctx) for i in range(2)]
        SX = Sem(nc, "SX", ctx)
        WPs = [Sem(nc, f"WP{i}", ctx) for i in range(3)]
        OS = Sem(nc, "OS", ctx)
        OL = Sem(nc, "OL", ctx)
        WOL = Sem(nc, "WOL", ctx)
        HMS = Sem(nc, "HMS", ctx)
        HMLs = [Sem(nc, f"HML{i}", ctx) for i in range(2)]
        WGs = [Sem(nc, f"WG{i}", ctx) for i in range(3)]
        ZSs = [Sem(nc, f"ZS{i}", ctx) for i in range(2)]
        ZLs = [Sem(nc, f"ZL{i}", ctx) for i in range(2)]
        WNs = [Sem(nc, f"WN{i}", ctx) for i in range(2)]
        CCH = Sem(nc, "CCH", ctx)
        CCSs = [Sem(nc, f"CCS{i}", ctx) for i in range(NH)]
        SDs = [Sem(nc, f"SD{i}", ctx) for i in range(2)]
        SGL = Sem(nc, "SGL", ctx)

        p = Prog(nc)

        def boot(g):
            g.dma_start(out=cosb[:, :], in_=d_cos[:, :]).then_inc(CLD.h, 16)
            g.dma_start(out=sinb[:, :], in_=d_sin[:, :]).then_inc(CLD.h, 16)
            g.dma_start(out=dmT[:, :, :],
                        in_=d_dmT.rearrange("(h p) c -> p h c", p=128)).then_inc(CLD.h, 16)
            g.dma_start(out=diT[:, :, :],
                        in_=d_diT.rearrange("(h p) c -> p h c", p=128)).then_inc(CLD.h, 16)
            g.dma_start(out=dkc[:, :], in_=d_dkc[:, :]).then_inc(CLD.h, 16)
            g.dma_start(out=anw[:, :],
                        in_=d_anw.rearrange("(c p) one -> p (c one)", p=128),
                        allow_slow_non_contiguous=True).then_inc(CLD.h, 16)
            g.dma_start(out=mnw[:, :],
                        in_=d_mnw.rearrange("(c p) one -> p (c one)", p=128),
                        allow_slow_non_contiguous=True).then_inc(CLD.h, 16)
            g.dma_start(out=gnw[:, :],
                        in_=d_gnw.rearrange("(c p) one -> p (c one)", p=128),
                        allow_slow_non_contiguous=True).then_inc(CLD.h, 16)
            g.dma_start(out=cwt[:, :, :, :],
                        in_=d_cw.rearrange("k (c p) s -> p k c s", p=128),
                        allow_slow_non_contiguous=True).then_inc(CLD.h, 16)
            g.dma_start(out=rwT[:, :, :],
                        in_=d_rwT.rearrange("(c p) e -> p c e", p=128)).then_inc(CLD.h, 16)
            g.dma_start(out=eyef[:, :], in_=d_eye[:, :]).then_inc(CLD.h, 16)
            g.dma_start(out=eyeb[:, :], in_=d_eye[:, :]).then_inc(CLD.h, 16)
            g.dma_start(out=hmask[:, :], in_=d_hmask[:, :]).then_inc(CLD.h, 16)
        p.on("gpsimd", boot)
        CLD.n = 13 * 16
        CLD_T = CLD.n

        def boot_ms(v):
            v.memset(ones[:, :], 1.0)
            v.memset(onesf[:, :], 1.0)
            ins = v.memset(cst_eps[:, :], EPS)
            ins.then_inc(VS.h, 1)
        p.on("vector", boot_ms)
        VS.n += 1

        def drow(k):
            return Dfp[0:1, 0, k * 512:(k + 1) * 512]

        def lnA(dc, c0, c1):
            return A[:, dc * 2112 + c0:dc * 2112 + c1]

        def emit_ffn(call):
            if call == 0:
                def in_ap(dc, c0, c1):
                    return d_xT.rearrange("(c p) t -> p c t", p=128)[:, dc, 3 + c0:3 + c1]
            else:
                def in_ap(dc, c0, c1):
                    return d_xscr.rearrange("(c p) t -> p c t", p=128)[:, dc, c0:c1]
            out_d = (d_out if (call == 1 or dev_p1_out) else d_xscr)

            def Xf32(dc):
                return Bf[:, 8192 + dc * 512:8192 + (dc + 1) * 512]

            def Xbf(dc):
                return B[:, dc * 512:(dc + 1) * 512]

            def Xsq(dc):
                return B[:, 8192 + dc * 512:8192 + (dc + 1) * 512]

            sx_gate = SX.n
            for tt in range(NTT):
                c0, c1 = tt * TTW, (tt + 1) * TTW
                vs_g0, ts_g0, as_g0 = VS.n, TS.n, AS.n

                def ld_f(sy, vs_g0=vs_g0, ts_g0=ts_g0, as_g0=as_g0, c0=c0, c1=c1,
                         sx_gate=sx_gate):
                    sy.wait_ge(VS.h, vs_g0)
                    sy.wait_ge(TS.h, ts_g0)
                    sy.wait_ge(AS.h, as_g0)
                    sy.wait_ge(SX.h, sx_gate)
                    for dc in range(0, NDC, 2):
                        sy.dma_start(out=Xf32(dc), in_=in_ap(dc, c0, c1)).then_inc(XL.h, 16)
                p.on("sync", ld_f)

                def ld_f2(s, vs_g0=vs_g0, ts_g0=ts_g0, as_g0=as_g0, c0=c0, c1=c1,
                          sx_gate=sx_gate):
                    s.wait_ge(VS.h, vs_g0)
                    s.wait_ge(TS.h, ts_g0)
                    s.wait_ge(SX.h, sx_gate)
                    for dc in range(1, NDC, 2):
                        s.dma_start(out=Xf32(dc), in_=in_ap(dc, c0, c1)).then_inc(XL.h, 16)
                p.on("scalar", ld_f2)
                XL.n += 16 * NDC

                xl_need = XL.n

                def cast_b(v, xl_need=xl_need, ts_g0=ts_g0, sx_gate=sx_gate):
                    v.wait_ge(XL.h, xl_need)
                    v.wait_ge(TS.h, ts_g0)
                    v.wait_ge(SX.h, sx_gate)
                    for dc in range(NDC):
                        ins = v.tensor_copy(Xbf(dc), Xf32(dc))
                    ins.then_inc(VS.h, 1)
                p.on("vector", cast_b)
                VS.n += 1
                vs_cast = VS.n

                def sq_f(s, xl_need=xl_need, ts_g0=ts_g0):
                    s.wait_ge(XL.h, xl_need)
                    s.wait_ge(TS.h, ts_g0)
                    s.wait_ge(CLD.h, CLD_T)
                    for dc in range(NDC):
                        ins = s.activation(Xsq(dc), Xf32(dc), AF.Square)
                    ins.then_inc(AS.h, 1)
                p.on("scalar", sq_f)
                AS.n += 1
                as_sq = AS.n

                def st_mm(t, xl_need=xl_need, vs_cast=vs_cast,
                          as_sq=as_sq):
                    t.wait_ge(XL.h, xl_need)
                    t.wait_ge(AS.h, as_sq)
                    t.wait_ge(CLD.h, CLD_T)
                    t.wait_ge(VS.h, vs_cast)
                    for dc in range(NDC):
                        t.matmul(PB[0:1, 0:512], ones[:, 0:1], Xbf(dc),
                                 start=(dc == 0), stop=(dc == NDC - 1))
                    for dc in range(NDC):
                        ins = t.matmul(PB[0:1, 512:1024], ones[:, 0:1], Xsq(dc),
                                       start=(dc == 0), stop=(dc == NDC - 1))
                    ins.then_inc(TS.h, 1)
                p.on("tensor", st_mm)
                TS.n += 1
                ts_stats = TS.n

                def rows_v(v, ts_stats=ts_stats):
                    v.wait_ge(TS.h, ts_stats)
                    v.tensor_scalar(drow(0), PB[0:1, 0:512], -1.0 / D, None, ALU.mult)
                    v.tensor_scalar(drow(2), PB[0:1, 512:1024], 1.0 / D, None, ALU.mult)
                    v.tensor_tensor(drow(3), drow(0), drow(0), ALU.mult)
                    ins = v.tensor_tensor(drow(2), drow(2), drow(3), ALU.subtract)
                    ins.then_inc(VS.h, 1)
                p.on("vector", rows_v)
                VS.n += 1
                vs_rows = VS.n

                def rows_s(s, vs_rows=vs_rows):
                    s.wait_ge(VS.h, vs_rows)
                    s.activation(drow(3), drow(2), AF.Ln, bias=EPS)
                    ins = s.activation(drow(1), drow(3), AF.Exp, scale=-0.5)
                    ins.then_inc(AS.h, 1)
                p.on("scalar", rows_s)
                AS.n += 1
                as_rows = AS.n

                def bc_mm(t, as_rows=as_rows):
                    t.wait_ge(AS.h, as_rows)
                    t.matmul(PB[:, 1024:1536], onesf[0:1, :], drow(0), start=True, stop=True)
                    ins = t.matmul(PB[:, 1536:2048], onesf[0:1, :], drow(1), start=True, stop=True)
                    ins.then_inc(TS.h, 1)
                p.on("tensor", bc_mm)
                TS.n += 1
                ts_bc = TS.n

                def ln_apply(v, ts_bc=ts_bc, tt=tt):
                    v.wait_ge(TS.h, ts_bc)
                    for dc in range(NDC):
                        t1 = Dfp[:, 1, 0:512]
                        v.tensor_tensor(t1, Xf32(dc), PB[:, 1024:1536], ALU.add)
                        ins = v.tensor_tensor(lnA(dc, tt * 512, tt * 512 + 512),
                                              t1, PB[:, 1536:2048], ALU.mult)
                    ins.then_inc(VS.h, 1)
                p.on("vector", ln_apply)
                VS.n += 1
                vs_ln = VS.n

            vs_ln_all = VS.n

            def emit_top2_unit(tt, tc):
                def rl_mm(t, tt=tt, tc=tc, vs_ln=vs_ln_all, as_prev=AS.n):
                    t.wait_ge(VS.h, vs_ln)
                    t.wait_ge(AS.h, as_prev)
                    for dc in range(NDC):
                        ins = t.matmul(PB[:, 0:8],
                                       lnA(dc, tt * 512 + tc * 128, tt * 512 + tc * 128 + 128),
                                       rwT[:, dc, :], start=(dc == 0), stop=(dc == NDC - 1))
                    ins.then_inc(TS.h, 1)
                p.on("tensor", rl_mm)
                TS.n += 1
                ts_rl = TS.n

                vs0 = VS.n

                def top2_v(v, ts_rl=ts_rl, vs0=vs0):
                    v.wait_ge(TS.h, ts_rl)
                    v.reduce_max(tp2[:, 8:9], PB[:, 0:8], axis=AX.X).then_inc(VS.h, 1)
                    v.wait_ge(VS.h, vs0 + 1)
                    v.tensor_scalar(tp2[:, 9:10], tp2[:, 8:9], -1.0, None,
                                    ALU.mult).then_inc(VS.h, 1)
                p.on("vector", top2_v)
                VS.n += 2
                vs_t2a = VS.n

                def top2_s(s, vs_t2a=vs_t2a):
                    s.wait_ge(VS.h, vs_t2a)
                    ins = s.activation(tp2[:, 16:24], PB[:, 0:8], AF.Exp, bias=tp2[:, 9:10])
                    ins.then_inc(AS.h, 1)
                p.on("scalar", top2_s)
                AS.n += 1
                as_t2 = AS.n
                vs0 = VS.n

                def top2_v3(v, as_t2=as_t2, vs0=vs0):
                    v.wait_ge(AS.h, as_t2)
                    E = tp2[:, 16:24]
                    v.tensor_scalar(tp2[:, 32:40], E, 1.0, None, ALU.is_lt).then_inc(VS.h, 1)
                    v.wait_ge(VS.h, vs0 + 1)
                    v.tensor_tensor(tp2[:, 24:32], E, tp2[:, 32:40], ALU.mult).then_inc(VS.h, 1)
                    v.wait_ge(VS.h, vs0 + 2)
                    v.reduce_max(tp2[:, 15:16], tp2[:, 24:32], axis=AX.X).then_inc(VS.h, 1)
                    v.wait_ge(VS.h, vs0 + 3)
                    v.tensor_scalar(tp2[:, 48:56], E, tp2[:, 15:16], None,
                                    ALU.is_ge).then_inc(VS.h, 1)
                    v.wait_ge(VS.h, vs0 + 4)
                    v.tensor_scalar(tp2[:, 56:57], tp2[:, 15:16], 1.0 + 1e-20, None,
                                    ALU.add).then_inc(VS.h, 1)
                p.on("vector", top2_v3)
                VS.n += 5
                vs_t2c = VS.n

                def top2_s3(s, vs_t2c=vs_t2c):
                    s.wait_ge(VS.h, vs_t2c)
                    s.activation(tp2[:, 57:58], tp2[:, 56:57], AF.Ln)
                    ins = s.activation(tp2[:, 58:59], tp2[:, 57:58], AF.Exp, scale=-1.0)
                    ins.then_inc(AS.h, 1)
                p.on("scalar", top2_s3)
                AS.n += 1
                as_t2c = AS.n
                vs0 = VS.n

                def top2_v4(v, as_t2c=as_t2c, vs0=vs0):
                    v.wait_ge(AS.h, as_t2c)
                    gk = tp2[:, 40:48]
                    v.tensor_tensor(gk, tp2[:, 16:24], tp2[:, 48:56], ALU.mult).then_inc(VS.h, 1)
                    v.wait_ge(VS.h, vs0 + 1)
                    v.tensor_scalar(gk, gk, tp2[:, 58:59], None, ALU.mult).then_inc(VS.h, 1)
                    v.wait_ge(VS.h, vs0 + 2)
                    ins = v.tensor_scalar(gk, gk, 0.5, None, ALU.mult)
                    ins.then_inc(VS.h, 1)
                p.on("vector", top2_v4)
                VS.n += 3
                vs_t2d = VS.n

                def gt_tr(t, vs_t2d=vs_t2d):
                    t.wait_ge(VS.h, vs_t2d)
                    ins = t.transpose(PB[0:8, 128:256], tp2[:, 40:48], eyef[:, :])
                    ins.then_inc(TS.h, 1)
                p.on("tensor", gt_tr)
                TS.n += 1
                ts_gt = TS.n

                def gt_ev(v, ts_gt=ts_gt, gl_g=GL.n):
                    v.wait_ge(TS.h, ts_gt)
                    v.wait_ge(GL.h, gl_g)
                    ins = v.tensor_copy(Dfp[0:8, 1, 512:640], PB[0:8, 128:256])
                    ins.then_inc(VS.h, 1)
                p.on("vector", gt_ev)
                VS.n += 1
                vs_gt = VS.n

                def gt_st(sy, vs_gt=vs_gt, tt=tt, tc=tc):
                    sy.wait_ge(VS.h, vs_gt)
                    sy.dma_start(
                        out=d_grows[:, tt * 512 + tc * 128:tt * 512 + tc * 128 + 128],
                        in_=Dfp[0:8, 1, 512:640]).then_inc(GL.h, 16)
                p.on("sync", gt_st)
                GL.n += 16

            vs_stage0 = VS.n
            ts_stage0 = TS.n
            as_stage0 = AS.n

            def wslot(s, half, dc):
                base = s * 4096 + half * 2048 + dc * 128
                return B[:, base:base + 128]

            def Gsb(c0, c1):
                return Bf[:, 8192 + c0:8192 + c1]

            s1 = Bf[:, 10240:10752]
            s2 = Bf[:, 10752:11264]

            def hslot(j):
                return B[:, 22528 + j * 512:22528 + (j + 1) * 512]

            def wsrc(ve):
                if ve < 8:
                    return (d_wg8[ve], d_wu8[ve])
                if ve == 8:
                    return (d_sg[:, 0:HINTER], d_su[:, 0:HINTER])
                return (d_sg[:, HINTER:GINTER], d_su[:, HINTER:GINTER])

            slot_gate = {}
            p1_ev = []
            gidx = 0
            units_left = [(tt_, tc_) for tt_ in range(NTT) for tc_ in range(4)]
            for kpos, ve in enumerate((8, 9, 0, 1, 2, 3, 4, 5, 6, 7)):
                src_g, src_u = wsrc(ve)
                if ve < 8:
                    for tt in range(NTT):
                        vs_gate = VS.n
                        ts_gate = TS.n

                        def g_ld(sy, ve=ve, tt=tt, vs_gate=vs_gate, ts_gate=ts_gate,
                                 gl_gate=GL.n):
                            sy.wait_ge(GL.h, gl_gate)
                            sy.wait_ge(VS.h, vs_gate)
                            sy.wait_ge(TS.h, ts_gate)
                            sy.dma_start(out=Dfp[0:1, 1, 0:512],
                                         in_=d_grows[ve:ve + 1, tt * 512:(tt + 1) * 512]
                                         ).then_inc(GL.h, 16)
                        p.on("sync", g_ld)
                        GL.n += 16
                        gl_need = GL.n

                        def g_bc(t, gl_need=gl_need, vs_gate=vs_gate, as_g=AS.n):
                            t.wait_ge(GL.h, gl_need)
                            t.wait_ge(VS.h, vs_gate)
                            t.wait_ge(AS.h, as_g)
                            ins = t.matmul(PB[:, 0:512], onesf[0:1, :], Dfp[0:1, 1, 0:512],
                                           start=True, stop=True)
                            ins.then_inc(TS.h, 1)
                        p.on("tensor", g_bc)
                        TS.n += 1
                        ts_gbc = TS.n

                        def g_ev(v, ts_gbc=ts_gbc, tt=tt):
                            v.wait_ge(TS.h, ts_gbc)
                            ins = v.tensor_copy(Gsb(tt * 512, (tt + 1) * 512), PB[:, 0:512])
                            ins.then_inc(VS.h, 1)
                        p.on("vector", g_ev)
                        VS.n += 1
                else:
                    vs_gate = VS.n

                    def g_half(v, vs_gate=vs_gate, as_g=as_stage0, ts_g=ts_stage0):
                        v.wait_ge(VS.h, vs_gate)
                        v.wait_ge(AS.h, as_g)
                        v.wait_ge(TS.h, ts_g)
                        ins = v.memset(Gsb(0, 2048), 0.5)
                        ins.then_inc(VS.h, 1)
                    p.on("vector", g_half)
                    VS.n += 1
                vs_g = VS.n

                for hc in range(NHC):
                    k = kpos * NHC + hc
                    slot = k % 4
                    ts_gate = slot_gate.get(k - 4, ts_stage0)

                    def w_ld(g, src_g=src_g, src_u=src_u, hc=hc, slot=slot,
                             ts_gate=ts_gate, ws=WL[slot]):
                        g.wait_ge(TS.h, ts_gate)
                        g.dma_start(
                            out=B[:, slot * 4096:slot * 4096 + 2048].rearrange(
                                "p (c m) -> p c m", m=128),
                            in_=src_g.rearrange("(c p) m -> p c m", p=128)[
                                :, :, hc * 128:(hc + 1) * 128]).then_inc(ws.h, 16)
                        g.dma_start(
                            out=B[:, slot * 4096 + 2048:slot * 4096 + 4096].rearrange(
                                "p (c m) -> p c m", m=128),
                            in_=src_u.rearrange("(c p) m -> p c m", p=128)[
                                :, :, hc * 128:(hc + 1) * 128]).then_inc(ws.h, 16)
                    p.on("gpsimd", w_ld)
                    WL[slot].n += 32
                    wl_need = WL[slot].n

                    for tt in range(NTT):
                        pset = gidx % 2
                        pg = PA[:, pset * 1024:pset * 1024 + 512]
                        pu = PA[:, pset * 1024 + 512:pset * 1024 + 1024]
                        ev_gate = p1_ev[gidx - 2] if gidx >= 2 else (0, 0)

                        def mm_g(t, hc=hc, tt=tt, slot=slot, pg=pg, pu=pu,
                                 wl_need=wl_need, ws=WL[slot], ev_gate=ev_gate, vs_g=vs_g):
                            t.wait_ge(ws.h, wl_need)
                            t.wait_ge(VS.h, max(ev_gate[0], vs_g))
                            t.wait_ge(AS.h, ev_gate[1])
                            for dc in range(NDC):
                                t.matmul(pg, wslot(slot, 0, dc),
                                         lnA(dc, tt * 512, tt * 512 + 512),
                                         start=(dc == 0), stop=(dc == NDC - 1))
                            for dc in range(NDC):
                                ins = t.matmul(pu, wslot(slot, 1, dc),
                                               lnA(dc, tt * 512, tt * 512 + 512),
                                               start=(dc == 0), stop=(dc == NDC - 1))
                            ins.then_inc(TS.h, 1)
                        p.on("tensor", mm_g)
                        TS.n += 1
                        ts_mm = TS.n

                        def h_silu(s, pg=pg, ts_mm=ts_mm):
                            s.wait_ge(TS.h, ts_mm)
                            ins = s.activation(s1, pg, AF.Silu)
                            ins.then_inc(AS.h, 1)
                        p.on("scalar", h_silu)
                        AS.n += 1
                        as_h = AS.n

                        hj = gidx % 2
                        hs_gate = HSs[hj].n

                        def h_mul(v, pu=pu, as_h=as_h, tt=tt, hj=hj, hs_gate=hs_gate):
                            v.wait_ge(AS.h, as_h)
                            v.wait_ge(HSs[hj].h, hs_gate)
                            v.tensor_tensor(s2, s1, pu, ALU.mult)
                            ins = v.tensor_tensor(hslot(hj), s2,
                                                  Gsb(tt * 512, (tt + 1) * 512), ALU.mult)
                            ins.then_inc(VS.h, 1)
                        p.on("vector", h_mul)
                        VS.n += 1
                        vs_h = VS.n
                        p1_ev.append((vs_h, as_h))

                        def h_st(sy, kh=ve * NHC + hc, tt=tt, hj=hj, vs_h=vs_h):
                            sy.wait_ge(VS.h, vs_h)
                            sy.dma_start(out=d_hscr[kh, :, tt * 512:(tt + 1) * 512],
                                         in_=hslot(hj)).then_inc(HSs[hj].h, 16)
                        p.on("sync", h_st)
                        HSs[hj].n += 16
                        gidx += 1
                        if kpos < 2 and units_left and gidx % 4 == 0:
                            emit_top2_unit(*units_left.pop(0))

                    slot_gate[k] = TS.n
                if kpos == 1:
                    while units_left:
                        emit_top2_unit(*units_left.pop(0))

            hs_tot = (HSs[0].n, HSs[1].n)
            ts_pass1 = TS.n
            vs_pass1 = VS.n
            as_pass1 = AS.n

            def h2slot(j, hc):
                return A[:, j * 5632 + hc * 512:j * 5632 + (hc + 1) * 512]

            def wdbuf(j, hc, dcl):
                base = 11264 + j * 11264 + hc * 1024 + dcl * 128
                return A[:, base:base + 128]

            def xhalf(dcl, c0, c1):
                return Bf[:, dcl * 2048 + c0:dcl * 2048 + c1]

            p2_ev = []
            p2_hg = []
            p2_wg = []
            g2 = 0
            for half in range(2):
                vs_gate = VS.n
                ts_gate = TS.n
                as_gate = AS.n

                def xh_ld(sy, half=half, vs_gate=vs_gate, ts_gate=ts_gate, as_gate=as_gate,
                          sx_gate=sx_gate):
                    sy.wait_ge(VS.h, vs_gate)
                    sy.wait_ge(TS.h, ts_gate)
                    sy.wait_ge(AS.h, as_gate)
                    sy.wait_ge(SX.h, sx_gate)
                    for dcl in range(8):
                        sy.dma_start(out=Bf[:, dcl * 2048:(dcl + 1) * 2048],
                                     in_=in_ap(half * 8 + dcl, 0, T)).then_inc(XHL.h, 16)
                p.on("sync", xh_ld)
                XHL.n += 16 * 8
                xhl_need = XHL.n

                for ve in range(NVE):
                    gi = half * NVE + ve
                    wj = gi % 2
                    if ve < 8:
                        wd_src = d_wd8[ve].rearrange("(c p) m -> p c m", p=128)[
                            :, :, half * 1024:(half + 1) * 1024]
                    elif ve == 8:
                        wd_src = d_sd[0:HINTER, :].rearrange("(c p) m -> p c m", p=128)[
                            :, :, half * 1024:(half + 1) * 1024]
                    else:
                        wd_src = d_sd[HINTER:GINTER, :].rearrange("(c p) m -> p c m", p=128)[
                            :, :, half * 1024:(half + 1) * 1024]
                    ts_gate2 = p2_wg[gi - 2] if gi >= 2 else ts_pass1

                    def wd_ld(g, wd_src=wd_src, wj=wj, ts_gate2=ts_gate2,
                              hs_tot=hs_tot, first=(gi == 0)):
                        g.wait_ge(TS.h, ts_gate2)
                        if first:
                            g.wait_ge(HSs[0].h, hs_tot[0])
                            g.wait_ge(HSs[1].h, hs_tot[1])
                        g.dma_start(
                            out=A[:, 11264 + wj * 11264:11264 + (wj + 1) * 11264].rearrange(
                                "p (c m) -> p c m", m=1024),
                            in_=wd_src).then_inc(WDs[wj].h, 16)
                    p.on("gpsimd", wd_ld)
                    WDs[wj].n += 16
                    wd_need = WDs[wj].n

                    for tt in range(NTT):
                        ii = gi * NTT + tt
                        hi = ii % 2
                        hl_gate = p2_hg[ii - 2] if ii >= 2 else ts_pass1

                        def h_ld(sy, ve=ve, tt=tt, hi=hi, hl_gate=hl_gate,
                                 hs_tot=hs_tot, first=(ii < 2)):
                            sy.wait_ge(TS.h, hl_gate)
                            if first:
                                sy.wait_ge(HSs[0].h, hs_tot[0])
                                sy.wait_ge(HSs[1].h, hs_tot[1])
                            for hc in range(NHC):
                                sy.dma_start(out=h2slot(hi, hc),
                                             in_=d_hscr[ve * NHC + hc, :,
                                                        tt * 512:(tt + 1) * 512]
                                             ).then_inc(HLs[hi].h, 16)
                        p.on("sync", h_ld)
                        HLs[hi].n += 16 * NHC
                        hl_need = HLs[hi].n

                        for dcl in range(8):
                            pb_ = PA[:, (g2 % 4) * 512:(g2 % 4) * 512 + 512]
                            ev_g = p2_ev[g2 - 4] if g2 >= 4 else 0

                            def wd_mm(t, wj=wj, hi=hi, dcl=dcl, pb_=pb_,
                                      wd_need=wd_need, hl_need=hl_need,
                                      ev_g=ev_g, xhl_need=xhl_need):
                                t.wait_ge(WDs[wj].h, wd_need)
                                t.wait_ge(HLs[hi].h, hl_need)
                                t.wait_ge(VS.h, ev_g)
                                t.wait_ge(XHL.h, xhl_need)
                                for hc in range(NHC):
                                    ins = t.matmul(pb_, wdbuf(wj, hc, dcl), h2slot(hi, hc),
                                                   start=(hc == 0), stop=(hc == NHC - 1))
                                ins.then_inc(TS.h, 1)
                            p.on("tensor", wd_mm)
                            TS.n += 1
                            ts_wd = TS.n

                            def x_acc(v, dcl=dcl, tt=tt, pb_=pb_, ts_wd=ts_wd):
                                v.wait_ge(TS.h, ts_wd)
                                xs = xhalf(dcl, tt * 512, (tt + 1) * 512)
                                ins = v.tensor_tensor(xs, xs, pb_, ALU.add)
                                ins.then_inc(VS.h, 1)
                            p.on("vector", x_acc)
                            VS.n += 1
                            p2_ev.append(VS.n)
                            g2 += 1

                        p2_hg.append(TS.n)
                    p2_wg.append(TS.n)

                vs_done = VS.n

                def xh_st(sy, half=half, vs_done=vs_done):
                    sy.wait_ge(VS.h, vs_done)
                    for dcl in range(8):
                        sy.dma_start(
                            out=out_d.rearrange("(c p) t -> p c t", p=128)[:, half * 8 + dcl, :],
                            in_=Bf[:, dcl * 2048:(dcl + 1) * 2048]).then_inc(SX.h, 16)
                p.on("sync", xh_st)
                SX.n += 16 * 8

        def emit_attn():
            sx_p1 = SX.n
            def cch_dma(sy, sx_p1=sx_p1):
                sy.wait_ge(SX.h, sx_p1)
                sy.dma_start(out=d_cchi[:, :], in_=d_xscr[:, 1920:2048]).then_inc(CCH.h, 16)
            p.on("sync", cch_dma)
            CCH.n += 16

            def cch_cc(g, need=CCH.n):
                g.wait_ge(CCH.h, need)
                g.collective_compute("AllGather", ALU.bypass, replica_groups=GROUPS,
                                     ins=[d_cchi], outs=[d_ccho]).then_inc(CCC.h, 1)
            p.on("gpsimd", cch_cc)
            CCC.n += 1
            cch_done = CCC.n

            def Xf32(dc):
                return Bf[:, 8192 + dc * 512:8192 + (dc + 1) * 512]

            def Xbf(dc):
                return B[:, dc * 512:(dc + 1) * 512]

            def Xsq(dc):
                return B[:, 8192 + dc * 512:8192 + (dc + 1) * 512]

            for tt in range(NTT):
                c0, c1 = tt * TTW, (tt + 1) * TTW
                vs_g0, ts_g0, as_g0 = VS.n, TS.n, AS.n

                def ld_f(sy, vs_g0=vs_g0, ts_g0=ts_g0, as_g0=as_g0, c0=c0, c1=c1,
                         sx_p1=sx_p1):
                    sy.wait_ge(VS.h, vs_g0)
                    sy.wait_ge(TS.h, ts_g0)
                    sy.wait_ge(AS.h, as_g0)
                    sy.wait_ge(SX.h, sx_p1)
                    for dc in range(0, NDC, 2):
                        sy.dma_start(
                            out=Xf32(dc),
                            in_=d_xscr.rearrange("(c p) t -> p c t", p=128)[:, dc, c0:c1]
                        ).then_inc(XL.h, 16)
                p.on("sync", ld_f)

                def ld_f2(s, vs_g0=vs_g0, ts_g0=ts_g0, c0=c0, c1=c1, sx_p1=sx_p1):
                    s.wait_ge(VS.h, vs_g0)
                    s.wait_ge(TS.h, ts_g0)
                    s.wait_ge(SX.h, sx_p1)
                    for dc in range(1, NDC, 2):
                        s.dma_start(
                            out=Xf32(dc),
                            in_=d_xscr.rearrange("(c p) t -> p c t", p=128)[:, dc, c0:c1]
                        ).then_inc(XL.h, 16)
                p.on("scalar", ld_f2)
                XL.n += 16 * NDC

                xl_need = XL.n

                def cast_b(v, xl_need=xl_need, ts_g0=ts_g0, sx_p1=sx_p1):
                    v.wait_ge(XL.h, xl_need)
                    v.wait_ge(TS.h, ts_g0)
                    v.wait_ge(SX.h, sx_p1)
                    for dc in range(NDC):
                        ins = v.tensor_copy(Xbf(dc), Xf32(dc))
                    ins.then_inc(VS.h, 1)
                p.on("vector", cast_b)
                VS.n += 1
                vs_cast = VS.n

                def sq_f(s, xl_need=xl_need, ts_g0=ts_g0):
                    s.wait_ge(XL.h, xl_need)
                    s.wait_ge(TS.h, ts_g0)
                    for dc in range(NDC):
                        ins = s.activation(Xsq(dc), Xf32(dc), AF.Square)
                    ins.then_inc(AS.h, 1)
                p.on("scalar", sq_f)
                AS.n += 1
                as_sq = AS.n

                def st_mm(t, xl_need=xl_need, vs_cast=vs_cast, as_sq=as_sq):
                    t.wait_ge(XL.h, xl_need)
                    t.wait_ge(AS.h, as_sq)
                    t.wait_ge(VS.h, vs_cast)
                    for dc in range(NDC):
                        t.matmul(PB[0:1, 0:512], ones[:, 0:1], Xbf(dc),
                                 start=(dc == 0), stop=(dc == NDC - 1))
                    for dc in range(NDC):
                        ins = t.matmul(PB[0:1, 512:1024], ones[:, 0:1], Xsq(dc),
                                       start=(dc == 0), stop=(dc == NDC - 1))
                    ins.then_inc(TS.h, 1)
                p.on("tensor", st_mm)
                TS.n += 1
                ts_stats = TS.n

                def rows_v(v, ts_stats=ts_stats):
                    v.wait_ge(TS.h, ts_stats)
                    v.tensor_scalar(drow(0), PB[0:1, 0:512], -1.0 / D, None, ALU.mult)
                    v.tensor_scalar(drow(2), PB[0:1, 512:1024], 1.0 / D, None, ALU.mult)
                    v.tensor_tensor(drow(3), drow(0), drow(0), ALU.mult)
                    ins = v.tensor_tensor(drow(3), drow(2), drow(3), ALU.subtract)
                    ins.then_inc(VS.h, 1)
                p.on("vector", rows_v)
                VS.n += 1
                vs_rows = VS.n

                def rows_s(s, vs_rows=vs_rows):
                    s.wait_ge(VS.h, vs_rows)
                    s.activation(drow(2), drow(3), AF.Ln, bias=EPS)
                    s.activation(drow(1), drow(2), AF.Exp, scale=-0.5)
                    ins = s.activation(drow(3), drow(2), AF.Exp, scale=-1.0)
                    ins.then_inc(AS.h, 1)
                p.on("scalar", rows_s)
                AS.n += 1
                as_r1 = AS.n

                def rows_v2(v, as_r1=as_r1):
                    v.wait_ge(AS.h, as_r1)
                    ins = v.tensor_scalar(drow(3), drow(3), -EPS, 1.0, ALU.mult, ALU.add)
                    ins.then_inc(VS.h, 1)
                p.on("vector", rows_v2)
                VS.n += 1
                vs_r2 = VS.n

                def rows_s2(s, vs_r2=vs_r2):
                    s.wait_ge(VS.h, vs_r2)
                    s.activation(drow(3), drow(3), AF.Ln, bias=EPS)
                    ins = s.activation(drow(3), drow(3), AF.Exp, scale=-0.5)
                    ins.then_inc(AS.h, 1)
                p.on("scalar", rows_s2)
                AS.n += 1
                as_r2 = AS.n

                def rows_v3(v, as_r2=as_r2):
                    v.wait_ge(AS.h, as_r2)
                    ins = v.tensor_tensor(drow(2), drow(1), drow(3), ALU.mult)
                    ins.then_inc(VS.h, 1)
                p.on("vector", rows_v3)
                VS.n += 1
                vs_r3 = VS.n

                def rows_st(sy, vs_r3=vs_r3, c0=c0, c1=c1):
                    sy.wait_ge(VS.h, vs_r3)
                    sy.dma_start(out=d_rows2[0:1, c0:c1], in_=drow(0)).then_inc(GL.h, 16)
                    sy.dma_start(out=d_rows2[1:2, c0:c1], in_=drow(1)).then_inc(GL.h, 16)
                p.on("sync", rows_st)
                GL.n += 32

                def bc_mm(t, vs_r3=vs_r3):
                    t.wait_ge(VS.h, vs_r3)
                    t.matmul(PB[:, 1024:1536], onesf[0:1, :], drow(0), start=True, stop=True)
                    ins = t.matmul(PB[:, 1536:2048], onesf[0:1, :], drow(2), start=True, stop=True)
                    ins.then_inc(TS.h, 1)
                p.on("tensor", bc_mm)
                TS.n += 1
                ts_bc = TS.n

                def xa_apply(v, ts_bc=ts_bc, tt=tt):
                    v.wait_ge(TS.h, ts_bc)
                    for dc in range(NDC):
                        t1 = Dfp[:, 1, 0:512]
                        v.tensor_tensor(t1, Xf32(dc), PB[:, 1024:1536], ALU.add)
                        ins = v.scalar_tensor_tensor(
                            lnA(dc, 3 + tt * 512, 3 + tt * 512 + 512),
                            t1, anw[:, dc:dc + 1], PB[:, 1536:2048], ALU.mult, ALU.mult)
                    ins.then_inc(VS.h, 1)
                p.on("vector", xa_apply)
                VS.n += 1

            ts_xstats = TS.n
            vs_xstats = VS.n
            as_xstats = AS.n
            gl_rows2 = GL.n

            def hx_ld(g, cch_done=cch_done, ts_xstats=ts_xstats, as_x=as_xstats):
                g.wait_ge(CCC.h, cch_done)
                g.wait_ge(TS.h, ts_xstats)
                g.wait_ge(AS.h, as_x)
                g.dma_start(out=Bf[:, 0:2048].rearrange("p (c m) -> p c m", m=128),
                            in_=d_ccho[0:D, :].rearrange("(c p) m -> p c m", p=128)
                            ).then_inc(HXG.h, 16)
            p.on("gpsimd", hx_ld)
            HXG.n += 16
            xhl_halo = HXG.n

            def hx_mask(v, xhl_halo=xhl_halo, vs0=VS.n):
                v.wait_ge(HXG.h, xhl_halo)
                v.wait_ge(CLD.h, CLD_T)
                for dc in range(NDC):
                    ins = v.tensor_scalar(
                        xh3[:, dc, 0:3],
                        Bf[:, 0:2048].rearrange("p (c m) -> p c m", m=128)[:, dc, 125:128],
                        hmask[:, 0:1], None, ALU.mult)
                ins.then_inc(VS.h, 1)
            p.on("vector", hx_mask)
            VS.n += 1
            vs_hx = VS.n

            def hx_sq(s, vs_hx=vs_hx):
                s.wait_ge(VS.h, vs_hx)
                s.activation(hsc[:, 0:48].rearrange("p (c t) -> p c t", t=3),
                             xh3[:, :, 0:3], AF.Square)
                ins = s.copy(hsc[:, 48:96].rearrange("p (c t) -> p c t", t=3),
                             xh3[:, :, 0:3])
                ins.then_inc(AS.h, 1)
            p.on("scalar", hx_sq)
            AS.n += 1
            as_hx = AS.n

            def hx_mm(t, as_hx=as_hx):
                t.wait_ge(AS.h, as_hx)
                for dc in range(NDC):
                    t.matmul(PB[0:1, 0:3], ones[:, 0:1],
                             hsc[:, 48 + dc * 3:48 + dc * 3 + 3],
                             start=(dc == 0), stop=(dc == NDC - 1))
                for dc in range(NDC):
                    ins = t.matmul(PB[0:1, 64:67], ones[:, 0:1],
                                   hsc[:, dc * 3:dc * 3 + 3],
                                   start=(dc == 0), stop=(dc == NDC - 1))
                ins.then_inc(TS.h, 1)
            p.on("tensor", hx_mm)
            TS.n += 1
            ts_hx = TS.n

            def hx_rows(v, ts_hx=ts_hx, vs0=VS.n):
                v.wait_ge(TS.h, ts_hx)
                v.tensor_scalar(drow(0)[:, 0:3], PB[0:1, 0:3], -1.0 / D, None,
                                ALU.mult).then_inc(VS.h, 1)
                v.wait_ge(VS.h, vs0 + 1)
                v.tensor_scalar(drow(2)[:, 0:3], PB[0:1, 64:67], 1.0 / D, None,
                                ALU.mult).then_inc(VS.h, 1)
                v.wait_ge(VS.h, vs0 + 2)
                v.tensor_tensor(drow(3)[:, 0:3], drow(0)[:, 0:3], drow(0)[:, 0:3],
                                ALU.mult).then_inc(VS.h, 1)
                v.wait_ge(VS.h, vs0 + 3)
                ins = v.tensor_tensor(drow(3)[:, 0:3], drow(2)[:, 0:3], drow(3)[:, 0:3],
                                      ALU.subtract)
                ins.then_inc(VS.h, 1)
            p.on("vector", hx_rows)
            VS.n += 4
            vs_hxr = VS.n

            def hx_rows_s(s, vs_hxr=vs_hxr):
                s.wait_ge(VS.h, vs_hxr)
                s.activation(drow(2)[:, 0:3], drow(3)[:, 0:3], AF.Ln, bias=EPS)
                s.activation(drow(1)[:, 0:3], drow(2)[:, 0:3], AF.Exp, scale=-0.5)
                ins = s.activation(drow(3)[:, 0:3], drow(2)[:, 0:3], AF.Exp, scale=-1.0)
                ins.then_inc(AS.h, 1)
            p.on("scalar", hx_rows_s)
            AS.n += 1
            as_hxr = AS.n

            def hx_rows_v2(v, as_hxr=as_hxr):
                v.wait_ge(AS.h, as_hxr)
                ins = v.tensor_scalar(drow(3)[:, 0:3], drow(3)[:, 0:3], -EPS, 1.0,
                                      ALU.mult, ALU.add)
                ins.then_inc(VS.h, 1)
            p.on("vector", hx_rows_v2)
            VS.n += 1
            vs_hxr2 = VS.n

            def hx_rows_s2(s, vs_hxr2=vs_hxr2):
                s.wait_ge(VS.h, vs_hxr2)
                s.activation(drow(3)[:, 0:3], drow(3)[:, 0:3], AF.Ln, bias=EPS)
                ins = s.activation(drow(3)[:, 0:3], drow(3)[:, 0:3], AF.Exp, scale=-0.5)
                ins.then_inc(AS.h, 1)
            p.on("scalar", hx_rows_s2)
            AS.n += 1
            as_hxr2 = AS.n

            def hx_rows_v3(v, as_hxr2=as_hxr2):
                v.wait_ge(AS.h, as_hxr2)
                ins = v.tensor_tensor(drow(2)[:, 0:3], drow(1)[:, 0:3], drow(3)[:, 0:3],
                                      ALU.mult)
                ins.then_inc(VS.h, 1)
            p.on("vector", hx_rows_v3)
            VS.n += 1
            vs_hxr3 = VS.n

            def hx_bc(t, vs_hxr3=vs_hxr3):
                t.wait_ge(VS.h, vs_hxr3)
                t.matmul(PB[:, 128:131], onesf[0:1, :], drow(0)[:, 0:3], start=True, stop=True)
                ins = t.matmul(PB[:, 192:195], onesf[0:1, :], drow(2)[:, 0:3],
                               start=True, stop=True)
                ins.then_inc(TS.h, 1)
            p.on("tensor", hx_bc)
            TS.n += 1
            ts_hxbc = TS.n

            def hxa_apply(v, ts_hxbc=ts_hxbc):
                v.wait_ge(TS.h, ts_hxbc)
                for dc in range(NDC):
                    t1 = Dfp[:, 1, 0:3]
                    v.tensor_tensor(t1, xh3[:, dc, 0:3], PB[:, 128:131], ALU.add)
                    ins = v.scalar_tensor_tensor(
                        lnA(dc, 0, 3), t1, anw[:, dc:dc + 1], PB[:, 192:195],
                        ALU.mult, ALU.mult)
                ins.then_inc(VS.h, 1)
            p.on("vector", hxa_apply)
            VS.n += 1
            vs_xa = VS.n

            if dev_dump:
                def dump_A(sy, vs_xa=vs_xa):
                    sy.wait_ge(VS.h, vs_xa)
                    sy.dma_start(out=d_dbgA, in_=A[:, :]).then_inc(SX.h, 16)
                p.on("sync", dump_A)
                SX.n += 16

            def preb(sl):
                base = (12288 + sl * 2051) if sl < 3 else (24576 + (sl - 3) * 2051)
                return B[:, base:base + 2051]

            DG = B[:, 20480:20992]
            osq = B[:, 20992:21504]
            s1h = Bf[:, 10752:11264]
            s2h = Bf[:, 11264:11776]

            def brow7():
                return Bf[0:1, 11776:12288]

            wring_gate = {}
            wp_k = [0]

            ts_corr_prev = [TS.n]

            def mk_qkv_units(h):
                units = []
                as_proj_d = {}
                vs_base = VS.n
                as_base = AS.n
                for ti, wsrc_d in enumerate((d_wq, d_wk, d_wv)):
                    kk = wp_k[0]
                    wp_k[0] += 1
                    slot = kk % 3
                    wp_need_d = {}

                    def u_ld(kk=kk, slot=slot, wsrc_d=wsrc_d, h=h, wp_need_d=wp_need_d):
                        ts_gate = wring_gate.get(kk - 3, ts_xstats)

                        def wp_ld(g, wsrc_d=wsrc_d, h=h, slot=slot, ts_gate=ts_gate,
                                  ws=WPs[slot], vs_first=(vs_xa if kk < 3 else 0)):
                            g.wait_ge(TS.h, ts_gate)
                            g.wait_ge(VS.h, vs_first)
                            g.dma_start(
                                out=B[:, slot * 4096:(slot + 1) * 4096].rearrange(
                                    "p (c m) -> p c m", m=256),
                                in_=wsrc_d.rearrange("(c p) m -> p c m", p=128)[
                                    :, :, h * 256:(h + 1) * 256]).then_inc(ws.h, 16)
                        p.on("gpsimd", wp_ld)
                        WPs[slot].n += 16
                        wp_need_d[0] = WPs[slot].n
                    units.append(u_ld)

                    sec_as = []
                    for hcb in range(2):
                        for tt in range(NTT):
                            gidx = hcb * 4 + tt

                            def u_pj(slot=slot, hcb=hcb, tt=tt, gidx=gidx, ti=ti,
                                     sec_as=sec_as, wp_need_d=wp_need_d):
                                pp = PA[:, (gidx % 2) * 512:(gidx % 2) * 512 + 512]
                                as_gate = sec_as[-2] if len(sec_as) >= 2 else as_base
                                vs_sec = vs_base if len(sec_as) < 2 else 0

                                def pj_mm(t, slot=slot, hcb=hcb, tt=tt, pp=pp,
                                          wp_need=wp_need_d[0], ws=WPs[slot],
                                          as_gate=as_gate, vs_sec=vs_sec):
                                    t.wait_ge(ws.h, wp_need)
                                    t.wait_ge(AS.h, as_gate)
                                    t.wait_ge(VS.h, max(vs_xa, vs_sec))
                                    for dc in range(NDC):
                                        ins = t.matmul(
                                            pp,
                                            B[:, slot * 4096 + dc * 256 + hcb * 128:
                                              slot * 4096 + dc * 256 + hcb * 128 + 128],
                                            lnA(dc, 3 + tt * 512, 3 + tt * 512 + 512),
                                            start=(dc == 0), stop=(dc == NDC - 1))
                                    ins.then_inc(TS.h, 1)
                                p.on("tensor", pj_mm)
                                TS.n += 1
                                ts_pj = TS.n

                                def pj_ev(s, ti=ti, hcb=hcb, tt=tt, pp=pp, ts_pj=ts_pj):
                                    s.wait_ge(TS.h, ts_pj)
                                    ins = s.copy(
                                        preb(ti * 2 + hcb)[:, 3 + tt * 512:3 + tt * 512 + 512],
                                        pp)
                                    ins.then_inc(AS.h, 1)
                                p.on("scalar", pj_ev)
                                AS.n += 1
                                sec_as.append(AS.n)
                            units.append(u_pj)

                    for hcb in range(2):
                        def u_pjh(slot=slot, hcb=hcb, ti=ti, wp_need_d=wp_need_d,
                                  kk=kk, last=(hcb == 1)):
                            def pjh_mm(t, slot=slot, hcb=hcb, wp_need=wp_need_d[0],
                                       ws=WPs[slot], vs_e=vs_base, as_e=AS.n):
                                t.wait_ge(ws.h, wp_need)
                                t.wait_ge(VS.h, vs_e)
                                t.wait_ge(AS.h, as_e)
                                for dc in range(NDC):
                                    ins = t.matmul(
                                        PA[:, 1024:1027],
                                        B[:, slot * 4096 + dc * 256 + hcb * 128:
                                          slot * 4096 + dc * 256 + hcb * 128 + 128],
                                        lnA(dc, 0, 3),
                                        start=(dc == 0), stop=(dc == NDC - 1))
                                ins.then_inc(TS.h, 1)
                            p.on("tensor", pjh_mm)
                            TS.n += 1
                            ts_pjh = TS.n

                            def pjh_ev(s, ti=ti, hcb=hcb, ts_pjh=ts_pjh):
                                s.wait_ge(TS.h, ts_pjh)
                                ins = s.copy(preb(ti * 2 + hcb)[:, 0:3], PA[:, 1024:1027])
                                ins.then_inc(AS.h, 1)
                            p.on("scalar", pjh_ev)
                            AS.n += 1
                            if last:
                                wring_gate[kk] = TS.n
                                as_proj_d[ti] = AS.n
                        units.append(u_pjh)
                return units, as_proj_d

            def emit_conv(h, as_proj_d):
                for ti in range(3):
                    cplane = 2 * ti
                    as_proj = as_proj_d[ti]
                    cv_as = []
                    for hcb in range(2):
                        vs_dgg = VS.n
                        ts_dgg = TS.n
                        os_dgg = OS.n

                        def dg_b(v, ti=ti, h=h, hcb=hcb, vs_dgg=vs_dgg, ts_dgg=ts_dgg,
                                 os_dgg=os_dgg):
                            v.wait_ge(VS.h, vs_dgg)
                            v.wait_ge(TS.h, ts_dgg)
                            v.wait_ge(OS.h, os_dgg)
                            for s_ in range(4):
                                ins = v.tensor_scalar(
                                    DG[:, s_ * 128:s_ * 128 + 128], eyef[:, :],
                                    cwt[:, ti, 2 * h + hcb, s_:s_ + 1], None, ALU.mult)
                            ins.then_inc(VS.h, 1)
                        p.on("vector", dg_b)
                        VS.n += 1
                        vs_dg = VS.n

                        for tt in range(NTT):
                            cgidx = hcb * 4 + tt
                            pc = PA[:, 1024 + (cgidx % 2) * 512:1024 + (cgidx % 2) * 512 + 512]
                            as_gate = cv_as[-2] if len(cv_as) >= 2 else AS.n
                            vs_sec = VS.n if len(cv_as) < 2 else 0

                            def cv_mm(t, ti=ti, hcb=hcb, tt=tt, pc=pc, vs_dg=vs_dg,
                                      as_proj=as_proj, as_gate=as_gate, vs_sec=vs_sec):
                                t.wait_ge(VS.h, max(vs_dg, vs_sec))
                                t.wait_ge(AS.h, max(as_proj, as_gate))
                                for s_ in range(4):
                                    ins = t.matmul(
                                        pc, DG[:, s_ * 128:s_ * 128 + 128],
                                        preb(ti * 2 + hcb)[:, tt * 512 + s_:tt * 512 + s_ + 512],
                                        start=(s_ == 0), stop=(s_ == 3))
                                ins.then_inc(TS.h, 1)
                            p.on("tensor", cv_mm)
                            TS.n += 1
                            ts_cv = TS.n

                            def cv_silu(s, cplane=cplane, hcb=hcb, tt=tt, pc=pc, ts_cv=ts_cv):
                                s.wait_ge(TS.h, ts_cv)
                                ins = s.activation(
                                    Cq[:, cplane + hcb, tt * 512:(tt + 1) * 512], pc, AF.Silu)
                                ins.then_inc(AS.h, 1)
                            p.on("scalar", cv_silu)
                            AS.n += 1
                            cv_as.append(AS.n)

            def fill(pend):
                if pend:
                    pend.pop(0)()

            pend = []
            as_proj_d = None
            as_proj_next = None
            for h in range(NH):
                if h == 0:
                    us0, as_proj_d = mk_qkv_units(0)
                    for u in us0:
                        u()
                emit_conv(h, as_proj_d)
                as_conv = AS.n

                vs_rope_gate = VS.n

                def rope_qk(v, as_conv=as_conv, vs_rope_gate=vs_rope_gate):
                    v.wait_ge(AS.h, as_conv)
                    v.wait_ge(VS.h, vs_rope_gate)
                    for pl in (0, 2):
                        x1 = Cq[:, pl, :]
                        x2 = Cq[:, pl + 1, :]
                        t1 = Dfb[:, 0, 0:2048]
                        t2 = Dfb[:, 0, 2048:4096]
                        t3 = Dfb[:, 1, 0:2048]
                        t4 = Dfb[:, 1, 2048:4096]
                        v.tensor_tensor(t1, x2, sinb[:, :], ALU.mult)
                        v.tensor_tensor(t2, x1, cosb[:, :], ALU.mult)
                        v.tensor_tensor(t3, x2, cosb[:, :], ALU.mult)
                        v.tensor_tensor(t4, x1, sinb[:, :], ALU.mult)
                        v.tensor_tensor(x1, t2, t1, ALU.subtract)
                        ins = v.tensor_tensor(x2, t3, t4, ALU.add)
                    ins.then_inc(VS.h, 1)
                p.on("vector", rope_qk)
                VS.n += 1
                vs_rope = VS.n

                if dev_dump and h == 0:
                    def dump_C(sy, vs_rope=vs_rope):
                        sy.wait_ge(VS.h, vs_rope)
                        sy.dma_start(out=d_dbgC, in_=Cq[:, :, :]).then_inc(SX.h, 16)
                    p.on("sync", dump_C)
                    SX.n += 16
                    def dump_C_gate(t, need=SX.n):
                        t.wait_ge(SX.h, need)
                    p.on("tensor", dump_C_gate)

                if h + 1 < NH:
                    pend, as_proj_next = mk_qkv_units(h + 1)
                else:
                    pend, as_proj_next = [], None
                for _ in range(3):
                    fill(pend)
                for ci in range(NCH):
                    ts0, ts1 = ci * CC, (ci + 1) * CC
                    vs_b4 = b4_vs_gate[0]

                    def tr_mm(t, ci=ci, ts0=ts0, ts1=ts1, vs_b4=vs_b4, vs_rope=vs_rope,
                              vs_oev=b5_vs_gate[0], vs_del=b6_vs_gate[0]):
                        t.wait_ge(VS.h, max(vs_b4, vs_rope))
                        for dcb in range(2):
                            t.transpose(PBb[:, dcb * 128:(dcb + 1) * 128],
                                        Cq[:, 2 + dcb, ts0:ts1], eyeb[:, :])
                        for dcb in range(2):
                            t.transpose(PBb[:, 256 + dcb * 128:256 + (dcb + 1) * 128],
                                        Cq[:, 4 + dcb, ts0:ts1], eyeb[:, :])
                        for dcb in range(2):
                            ins = t.matmul(PB[:, 256:384], Cq[:, 2 + dcb, ts0:ts1],
                                           Cq[:, dcb, ts0:ts1],
                                           start=(dcb == 0), stop=(dcb == 1))
                        ins.then_inc(TS.h, 1)
                    p.on("tensor", tr_mm)
                    TS.n += 1
                    ts_tr = TS.n

                    def ret_ev(v, h=h, ci=ci, ts0=ts0, ts1=ts1, ts_tr=ts_tr):
                        v.wait_ge(TS.h, ts_tr)
                        v.tensor_scalar(ktb[:, :], PBb[:, 0:256], dkc[:, h:h + 1],
                                        None, ALU.mult)
                        v.tensor_copy(vtb[:, :], PBb[:, 256:512])
                        v.tensor_tensor(adt[:, :], PB[:, 256:384], dmT[:, h, :], ALU.mult)
                        for dcb in range(2):
                            ins = v.tensor_tensor(qtb[:, dcb * 128:(dcb + 1) * 128],
                                                  Cq[:, dcb, ts0:ts1], diT[:, h, :], ALU.mult)
                        ins.then_inc(VS.h, 1)
                    p.on("vector", ret_ev)
                    VS.n += 1
                    vs_retev = VS.n
                    b4_vs_gate[0] = vs_retev
                    fill(pend)

                    def o_mm(t, ci=ci, vs_retev=vs_retev, vs_oev=b5_vs_gate[0]):
                        t.wait_ge(VS.h, max(vs_retev, vs_oev))
                        for dvb in range(2):
                            ins = t.matmul(PB[:, 512 + dvb * 128:512 + (dvb + 1) * 128],
                                           vtb[:, dvb * 128:(dvb + 1) * 128], adt[:, :],
                                           start=(dvb == 0),
                                           stop=(ci == 0 and dvb == 1),
                                           skip_group_check=True)
                        if ci > 0:
                            for dvb in range(2):
                                for dkb in range(2):
                                    ins = t.matmul(
                                        PB[:, 512 + dvb * 128:512 + (dvb + 1) * 128],
                                        S_b[:, dkb, dvb * 128:(dvb + 1) * 128],
                                        qtb[:, dkb * 128:(dkb + 1) * 128],
                                        start=False, stop=(dvb == 1 and dkb == 1),
                                        skip_group_check=True)
                        ins.then_inc(TS.h, 1)
                    p.on("tensor", o_mm)
                    TS.n += 1
                    ts_o = TS.n

                    def d_mm(t, vs_retev=vs_retev, vs_del=b6_vs_gate[0]):
                        t.wait_ge(VS.h, vs_del)
                        for dkb in range(2):
                            ins = t.matmul(PB[:, 1024 + dkb * 256:1024 + (dkb + 1) * 256],
                                           ktb[:, dkb * 128:(dkb + 1) * 128], vtb[:, :],
                                           start=True, stop=True)
                        ins.then_inc(TS.h, 1)
                    p.on("tensor", d_mm)
                    TS.n += 1
                    ts_d = TS.n

                    def o_ev(v, ci=ci, ts_o=ts_o):
                        v.wait_ge(TS.h, ts_o)
                        ins = v.tensor_copy(
                            Dfp[:, :, ci * 128:(ci + 1) * 128],
                            PB[:, 512:768].rearrange("p (b c) -> p b c", c=128))
                        ins.then_inc(VS.h, 1)
                    p.on("vector", o_ev)
                    VS.n += 1
                    b5_vs_gate[0] = VS.n

                    def s_upd(v, h=h, ci=ci, ts_d=ts_d, vs0=VS.n,
                              sd_prev=(SDs[(h + 1) % 2].n if ci == 0 else 0)):
                        v.wait_ge(TS.h, ts_d)
                        if ci == 0:
                            v.wait_ge(SDs[(h + 1) % 2].h, sd_prev)
                        for dkb in range(2):
                            if ci == 0:
                                ins = v.tensor_copy(
                                    S_f[:, dkb, :],
                                    PB[:, 1024 + dkb * 256:1024 + (dkb + 1) * 256])
                            else:
                                ins = v.scalar_tensor_tensor(
                                    S_f[:, dkb, :], S_f[:, dkb, :], GC[h],
                                    PB[:, 1024 + dkb * 256:1024 + (dkb + 1) * 256],
                                    ALU.mult, ALU.add)
                        ins.then_inc(VS.h, 1)
                        v.wait_ge(VS.h, vs0 + 1)
                        ins = v.tensor_copy(S_b[:, :, :], S_f[:, :, :])
                        ins.then_inc(VS.h, 1)
                    p.on("vector", s_upd)
                    VS.n += 2
                    b6_vs_gate[0] = VS.n

                vs_chunks = VS.n

                def s_st(sy, h=h, vs_chunks=vs_chunks):
                    sy.wait_ge(VS.h, vs_chunks)
                    sy.dma_start(
                        out=d_ccsi[h * DK:(h + 1) * DK, :].rearrange("(b p) v -> p b v", p=128),
                        in_=S_f[:, :, :]).then_inc(SDs[h % 2].h, 16)
                p.on("sync", s_st)
                SDs[h % 2].n += 16
                sd_need = SDs[h % 2].n

                def s_cc(g, h=h, sd_need=sd_need):
                    g.wait_ge(SDs[h % 2].h, sd_need)
                    g.collective_compute(
                        "AllGather", ALU.bypass, replica_groups=GROUPS,
                        ins=[d_ccsi[h * DK:(h + 1) * DK, :]],
                        outs=[d_ccso[h * 2 * DK:(h + 1) * 2 * DK, :]]).then_inc(CCSs[h].h, 1)
                p.on("gpsimd", s_cc)
                CCSs[h].n += 1

                def sg_ld(g, h=h, ts_corr_prev=ts_corr_prev[0]):
                    g.wait_ge(CCSs[h].h, 1)
                    g.wait_ge(TS.h, ts_corr_prev)
                    g.dma_start(out=Sg_b[:, :, :],
                                in_=d_ccso[h * 2 * DK:h * 2 * DK + DK, :].rearrange(
                                    "(b p) v -> p b v", p=128)).then_inc(SGL.h, 16)
                p.on("gpsimd", sg_ld)
                SGL.n += 16
                sgl_need = SGL.n
                fill(pend)
                fill(pend)

                for tt in range(NTT):
                    gl_gate = GL.n
                    vs_gate = VS.n

                    def cr_ld(sy, h=h, tt=tt, gl_gate=gl_gate, vs_gate=vs_gate):
                        sy.wait_ge(GL.h, gl_gate)
                        sy.wait_ge(VS.h, vs_gate)
                        sy.dma_start(out=brow7(),
                                     in_=d_corr[h:h + 1, tt * 512:(tt + 1) * 512]
                                     ).then_inc(GL.h, 16)
                    p.on("sync", cr_ld)
                    GL.n += 16
                    gl_need = GL.n

                    def cr_bc(t, gl_need=gl_need, as_e=AS.n, vs_e=VS.n):
                        t.wait_ge(GL.h, gl_need)
                        t.wait_ge(AS.h, as_e)
                        t.wait_ge(VS.h, vs_e)
                        ins = t.matmul(PB[:, 1536:2048], onesf[0:1, :], brow7(),
                                       start=True, stop=True)
                        ins.then_inc(TS.h, 1)
                    p.on("tensor", cr_bc)
                    TS.n += 1
                    ts_crbc = TS.n
                    fill(pend)

                    def qq_mul(v, tt=tt, ts_crbc=ts_crbc, os_g=OS.n):
                        v.wait_ge(TS.h, ts_crbc)
                        v.wait_ge(OS.h, os_g)
                        for dkb in range(2):
                            ins = v.tensor_tensor(
                                Cq[:, 4, dkb * 512:(dkb + 1) * 512],
                                Cq[:, dkb, tt * 512:(tt + 1) * 512],
                                PB[:, 1536:2048], ALU.mult)
                        ins.then_inc(VS.h, 1)
                    p.on("vector", qq_mul)
                    VS.n += 1
                    vs_qq = VS.n
                    b7_read_ts[0] = TS.n

                    for dvb in range(2):
                        def cr_mm(t, dvb=dvb, vs_qq=vs_qq, sgl_need=sgl_need,
                                  vs_b5=b5_vs_gate[0]):
                            t.wait_ge(VS.h, max(vs_qq, vs_b5))
                            t.wait_ge(SGL.h, sgl_need)
                            for dkb in range(2):
                                ins = t.matmul(PB[:, 512:1024],
                                               Sg_b[:, dkb, dvb * 128:(dvb + 1) * 128],
                                               Cq[:, 4, dkb * 512:(dkb + 1) * 512],
                                               start=(dkb == 0), stop=(dkb == 1))
                            ins.then_inc(TS.h, 1)
                        p.on("tensor", cr_mm)
                        TS.n += 1
                        ts_cr = TS.n
                        fill(pend)

                        def cr_add(v, dvb=dvb, tt=tt, ts_cr=ts_cr):
                            v.wait_ge(TS.h, ts_cr)
                            orr = Dfp[:, dvb, tt * 512:(tt + 1) * 512]
                            ins = v.tensor_tensor(orr, orr, PB[:, 512:1024], ALU.add)
                            ins.then_inc(VS.h, 1)
                        p.on("vector", cr_add)
                        VS.n += 1
                        b5_vs_gate[0] = VS.n
                ts_corr_prev[0] = TS.n
                while pend:
                    pend.pop(0)()

                if dev_dump and h == 0:
                    def dump_D(sy, vs_need=VS.n):
                        sy.wait_ge(VS.h, vs_need)
                        sy.dma_start(out=d_dbgD, in_=Dfp[:, :, :]).then_inc(SX.h, 16)
                        sy.dma_start(out=d_dbgS, in_=S_f[:, :, :]).then_inc(SX.h, 16)
                    p.on("sync", dump_D)
                    SX.n += 32
                    def dump_D_gate(v, need=SX.n):
                        v.wait_ge(SX.h, need)
                    p.on("vector", dump_D_gate)

                kk = wp_k[0]
                slot = kk % 3
                ts_gate = wring_gate.get(kk - 3, ts_xstats)

                def wg_ld(g, h=h, slot=slot, ts_gate=ts_gate, ws=WPs[slot]):
                    g.wait_ge(TS.h, ts_gate)
                    g.dma_start(
                        out=B[:, slot * 4096:(slot + 1) * 4096].rearrange(
                            "p (c m) -> p c m", m=256),
                        in_=d_wgt.rearrange("(c p) m -> p c m", p=128)[
                            :, :, h * 256:(h + 1) * 256]).then_inc(ws.h, 16)
                p.on("gpsimd", wg_ld)
                WPs[slot].n += 16
                wp_need = WPs[slot].n
                wp_k[0] += 1

                gt_as = []
                for tt in range(NTT):
                    as_gate = AS.n
                    vs_gate = VS.n

                    def gn_sq(s, tt=tt, as_gate=as_gate, vs_gate=vs_gate):
                        s.wait_ge(AS.h, as_gate)
                        s.wait_ge(VS.h, vs_gate)
                        ins = s.activation(osq[:, 0:512], Dfp[:, 0, tt * 512:(tt + 1) * 512],
                                           AF.Square)
                        ins.then_inc(AS.h, 1)
                    p.on("scalar", gn_sq)
                    AS.n += 1
                    as_sq0 = AS.n

                    def gn_mm0(t, as_sq0=as_sq0, ts_b7r=b7_read_ts[0]):
                        t.wait_ge(AS.h, as_sq0)
                        ins = t.matmul(PB[0:1, 1536:2048], ones[:, 0:1], osq[:, 0:512],
                                       start=True, stop=False)
                        ins.then_inc(TS.h, 1)
                    p.on("tensor", gn_mm0)
                    TS.n += 1
                    ts_gn0 = TS.n

                    def gn_sq1(s, tt=tt, ts_gn0=ts_gn0):
                        s.wait_ge(TS.h, ts_gn0)
                        ins = s.activation(osq[:, 0:512], Dfp[:, 1, tt * 512:(tt + 1) * 512],
                                           AF.Square)
                        ins.then_inc(AS.h, 1)
                    p.on("scalar", gn_sq1)
                    AS.n += 1
                    as_sq1 = AS.n

                    def gn_mm1(t, as_sq1=as_sq1):
                        t.wait_ge(AS.h, as_sq1)
                        ins = t.matmul(PB[0:1, 1536:2048], ones[:, 0:1], osq[:, 0:512],
                                       start=False, stop=True)
                        ins.then_inc(TS.h, 1)
                    p.on("tensor", gn_mm1)
                    TS.n += 1
                    ts_gn = TS.n

                    def gn_rows(s, ts_gn=ts_gn):
                        s.wait_ge(TS.h, ts_gn)
                        s.activation(brow7(), PB[0:1, 1536:2048], AF.Ln,
                                     bias=EPS, scale=1.0 / DK)
                        ins = s.activation(brow7(), brow7(), AF.Exp, scale=-0.5)
                        ins.then_inc(AS.h, 1)
                    p.on("scalar", gn_rows)
                    AS.n += 1
                    as_gnr = AS.n

                    def gn_bc(t, as_gnr=as_gnr, vs_e=VS.n):
                        t.wait_ge(AS.h, as_gnr)
                        t.wait_ge(VS.h, vs_e)
                        ins = t.matmul(PB[:, 1024:1536], onesf[0:1, :], brow7(),
                                       start=True, stop=True)
                        ins.then_inc(TS.h, 1)
                    p.on("tensor", gn_bc)
                    TS.n += 1
                    ts_gnbc = TS.n
                    b7_read_ts[0] = TS.n
                    b6_mm_ts[0] = TS.n

                    for dvb in range(2):
                        ggidx = tt * 2 + dvb
                        pp = PA[:, (ggidx % 2) * 512:(ggidx % 2) * 512 + 512]
                        as_gate2 = gt_as[-2] if len(gt_as) >= 2 else AS.n
                        vs_sec = VS.n if len(gt_as) < 2 else 0

                        def gp_mm(t, slot=slot, dvb=dvb, tt=tt, pp=pp, wp_need=wp_need,
                                  ws=WPs[slot], as_gate2=as_gate2, vs_xa=vs_xa,
                                  vs_sec=vs_sec):
                            t.wait_ge(ws.h, wp_need)
                            t.wait_ge(AS.h, as_gate2)
                            t.wait_ge(VS.h, vs_sec)
                            for dc in range(NDC):
                                ins = t.matmul(
                                    pp,
                                    B[:, slot * 4096 + dc * 256 + dvb * 128:
                                      slot * 4096 + dc * 256 + dvb * 128 + 128],
                                    lnA(dc, 3 + tt * 512, 3 + tt * 512 + 512),
                                    start=(dc == 0), stop=(dc == NDC - 1))
                            ins.then_inc(TS.h, 1)
                        p.on("tensor", gp_mm)
                        TS.n += 1
                        ts_gp = TS.n

                        def gate_silu(s, pp=pp, ts_gp=ts_gp):
                            s.wait_ge(TS.h, ts_gp)
                            ins = s.activation(s1h, pp, AF.Silu)
                            ins.then_inc(AS.h, 1)
                        p.on("scalar", gate_silu)
                        AS.n += 1
                        as_gs = AS.n
                        gt_as.append(as_gs)

                        def gate_mul(v, dvb=dvb, tt=tt, ts_gnbc=ts_gnbc, as_gs=as_gs,
                                     os_gate=OS.n, vs0=VS.n):
                            v.wait_ge(TS.h, ts_gnbc)
                            v.wait_ge(AS.h, as_gs)
                            v.wait_ge(OS.h, os_gate)
                            orr = Dfp[:, dvb, tt * 512:(tt + 1) * 512]
                            v.scalar_tensor_tensor(s2h, orr, gnw[:, dvb:dvb + 1],
                                                   PB[:, 1024:1536], ALU.mult,
                                                   ALU.mult).then_inc(VS.h, 1)
                            v.wait_ge(VS.h, vs0 + 1)
                            ins = v.tensor_tensor(Cq[:, 4 + dvb, tt * 512:(tt + 1) * 512],
                                                  s2h, s1h, ALU.mult)
                            ins.then_inc(VS.h, 1)
                        p.on("vector", gate_mul)
                        VS.n += 2
                        vs_gm = VS.n
                        b6_vs_gate[0] = VS.n
                        b6_read_vs[0] = VS.n

                        def o_st(sy, h=h, dvb=dvb, tt=tt, vs_gm=vs_gm):
                            sy.wait_ge(VS.h, vs_gm)
                            sy.dma_start(out=d_oscr[2 * h + dvb, :, tt * 512:(tt + 1) * 512],
                                         in_=Cq[:, 4 + dvb, tt * 512:(tt + 1) * 512]
                                         ).then_inc(OS.h, 16)
                        p.on("sync", o_st)
                        OS.n += 16

                wring_gate[kk] = TS.n
                as_proj_d = as_proj_next

            os_total = OS.n

            ts_hd = TS.n
            vs_hd = VS.n
            as_hd = AS.n

            def wo_ld(g, ts_hd=ts_hd, vs_hd=vs_hd, as_hd=as_hd):
                g.wait_ge(TS.h, ts_hd)
                g.wait_ge(VS.h, vs_hd)
                g.wait_ge(AS.h, as_hd)
                for hc in range(NDC):
                    g.dma_start(out=B[:, hc * 2048:(hc + 1) * 2048],
                                in_=d_wo[hc * 128:(hc + 1) * 128, :]).then_inc(WOL.h, 16)
            p.on("gpsimd", wo_ld)
            WOL.n += 16 * NDC
            wol_need = WOL.n

            hm_k = [0]
            for tt in range(NTT):
                vs_gate = VS.n
                ts_gate = TS.n
                as_gate = AS.n

                def ot_ld(sy, tt=tt, vs_gate=vs_gate, ts_gate=ts_gate, os_total=os_total,
                          first=(tt == 0)):
                    sy.wait_ge(TS.h, ts_gate)
                    sy.wait_ge(VS.h, vs_gate)
                    if first:
                        sy.wait_ge(OS.h, os_total)
                    sy.dma_start(out=A[:, 0:8192].rearrange("p (c m) -> p c m", m=512),
                                 in_=d_oscr.rearrange("c p t -> p c t")[
                                     :, :, tt * 512:(tt + 1) * 512]).then_inc(OL.h, 16)
                    for dc in range(NDC):
                        sy.dma_start(
                            out=Af[:, 4096 + dc * 512:4096 + (dc + 1) * 512],
                            in_=d_xscr.rearrange("(c p) t -> p c t", p=128)[
                                :, dc, tt * 512:(tt + 1) * 512]).then_inc(OL.h, 16)
                    sy.dma_start(out=drow(0), in_=d_rows2[0:1, tt * 512:(tt + 1) * 512]
                                 ).then_inc(OL.h, 16)
                    sy.dma_start(out=drow(1), in_=d_rows2[1:2, tt * 512:(tt + 1) * 512]
                                 ).then_inc(OL.h, 16)
                p.on("sync", ot_ld)
                OL.n += 16 * (1 + NDC + 2)
                ol_need = OL.n

                def wo_bc(t, ol_need=ol_need, vs_e=VS.n, as_e=AS.n):
                    t.wait_ge(OL.h, ol_need)
                    t.wait_ge(VS.h, vs_e)
                    t.wait_ge(AS.h, as_e)
                    t.matmul(PB[:, 1024:1536], onesf[0:1, :], drow(0), start=True, stop=True)
                    ins = t.matmul(PB[:, 1536:2048], onesf[0:1, :], drow(1),
                                   start=True, stop=True)
                    ins.then_inc(TS.h, 1)
                p.on("tensor", wo_bc)
                TS.n += 1
                ts_wobc = TS.n
                b7_read_ts[0] = TS.n

                for dc in range(NDC):
                    pp = PA[:, (dc % 2) * 512:(dc % 2) * 512 + 512]
                    vs_gate2 = wo_vs[-2] if len(wo_vs) >= 2 else 0

                    def wo_mm(t, dc=dc, pp=pp, wol_need=wol_need, ol_need=ol_need,
                              vs_gate2=vs_gate2, ts_wobc=ts_wobc):
                        t.wait_ge(WOL.h, wol_need)
                        t.wait_ge(OL.h, ol_need)
                        t.wait_ge(VS.h, vs_gate2)
                        for hc in range(NDC):
                            ins = t.matmul(pp, B[:, hc * 2048 + dc * 128:hc * 2048 + dc * 128 + 128],
                                           A[:, hc * 512:(hc + 1) * 512],
                                           start=(hc == 0), stop=(hc == NDC - 1))
                        ins.then_inc(TS.h, 1)
                    p.on("tensor", wo_mm)
                    TS.n += 1
                    ts_wo = TS.n

                    hs_ = hm_k[0] % 2
                    hm_gate = hm_st_gate[hs_]

                    def hmid_v(v, dc=dc, pp=pp, ts_wo=ts_wo, hs_=hs_, hm_gate=hm_gate):
                        v.wait_ge(TS.h, ts_wo)
                        v.wait_ge(HMS.h, hm_gate)
                        t1 = Af[:, 13312:13824]
                        v.tensor_tensor(t1, Af[:, 4096 + dc * 512:4096 + (dc + 1) * 512],
                                        PB[:, 1024:1536], ALU.add)
                        t2 = Af[:, 13824:14336]
                        v.tensor_tensor(t2, t1, PB[:, 1536:2048], ALU.mult)
                        ins = v.tensor_tensor(Af[:, 12288 + hs_ * 512:12288 + (hs_ + 1) * 512],
                                              t2, pp, ALU.add)
                        ins.then_inc(VS.h, 1)
                    p.on("vector", hmid_v)
                    VS.n += 1
                    wo_vs.append(VS.n)
                    vs_hm = VS.n

                    def hmid_sq(s, hs_=hs_, vs_hm=vs_hm):
                        s.wait_ge(VS.h, vs_hm)
                        ins = s.activation(A[:, 28672:29184],
                                           Af[:, 12288 + hs_ * 512:12288 + (hs_ + 1) * 512],
                                           AF.Square)
                        ins.then_inc(AS.h, 1)
                    p.on("scalar", hmid_sq)
                    AS.n += 1
                    as_hsq = AS.n

                    def ysq_mm(t, dc=dc, as_hsq=as_hsq):
                        t.wait_ge(AS.h, as_hsq)
                        ins = t.matmul(PB[0:1, 0:512], ones[:, 0:1], A[:, 28672:29184],
                                       start=(dc == 0), stop=(dc == NDC - 1))
                        ins.then_inc(TS.h, 1)
                    p.on("tensor", ysq_mm)
                    TS.n += 1

                    def hmid_st(sy, dc=dc, tt=tt, hs_=hs_, vs_hm=vs_hm):
                        sy.wait_ge(VS.h, vs_hm)
                        sy.dma_start(
                            out=d_hmid.rearrange("(c p) t -> p c t", p=128)[
                                :, dc, tt * 512:(tt + 1) * 512],
                            in_=Af[:, 12288 + hs_ * 512:12288 + (hs_ + 1) * 512]
                        ).then_inc(HMS.h, 16)
                    p.on("sync", hmid_st)
                    HMS.n += 16
                    hm_st_gate[hs_] = HMS.n
                    hm_k[0] += 1

                ts_ysq = TS.n

                def y_rows(s, ts_ysq=ts_ysq):
                    s.wait_ge(TS.h, ts_ysq)
                    s.activation(drow(2), PB[0:1, 0:512], AF.Ln, bias=EPS, scale=1.0 / D)
                    ins = s.activation(drow(2), drow(2), AF.Exp, scale=-0.5)
                    ins.then_inc(AS.h, 1)
                p.on("scalar", y_rows)
                AS.n += 1
                as_yr = AS.n

                def y_rows_st(sy, tt=tt, as_yr=as_yr):
                    sy.wait_ge(AS.h, as_yr)
                    sy.dma_start(out=d_rows2[2:3, tt * 512:(tt + 1) * 512],
                                 in_=drow(2)).then_inc(GL.h, 16)
                p.on("sync", y_rows_st)
                GL.n += 16

            hms_total = HMS.n
            gl_r3 = GL.n

            ts_yg = TS.n

            ym_k = [0]
            for tt in range(NTT):
                def yr_ld(sy, tt=tt, gl_r3=gl_r3, hms_total=hms_total, vg=VS.n, ag=AS.n):
                    sy.wait_ge(GL.h, gl_r3)
                    sy.wait_ge(VS.h, vg)
                    sy.wait_ge(AS.h, ag)
                    if tt == 0:
                        sy.wait_ge(HMS.h, hms_total)
                    sy.dma_start(out=drow(2), in_=d_rows2[2:3, tt * 512:(tt + 1) * 512]
                                 ).then_inc(OL.h, 16)
                p.on("sync", yr_ld)
                OL.n += 16
                ol_need = OL.n

                def yr_bc(t, ol_need=ol_need, vs_e=VS.n, as_e=AS.n):
                    t.wait_ge(OL.h, ol_need)
                    t.wait_ge(VS.h, vs_e)
                    t.wait_ge(AS.h, as_e)
                    ins = t.matmul(PB[:, 1536:2048], onesf[0:1, :], drow(2),
                                   start=True, stop=True)
                    ins.then_inc(TS.h, 1)
                p.on("tensor", yr_bc)
                TS.n += 1
                ts_yrbc = TS.n
                b7_read_ts[0] = TS.n

                for dc in range(NDC):
                    hs_ = ym_k[0] % 2
                    hml_gate = HMLs[hs_].n

                    def hm_ld(sy, dc=dc, tt=tt, hs_=hs_, vg=yld_vs[hs_], hml_gate=hml_gate):
                        sy.wait_ge(VS.h, vg)
                        sy.dma_start(
                            out=Af[:, hs_ * 512:(hs_ + 1) * 512],
                            in_=d_hmid.rearrange("(c p) t -> p c t", p=128)[
                                :, dc, tt * 512:(tt + 1) * 512]).then_inc(HMLs[hs_].h, 16)
                    p.on("sync", hm_ld)
                    HMLs[hs_].n += 16
                    hml_need = HMLs[hs_].n

                    def y_mul(v, dc=dc, tt=tt, hs_=hs_, hml_need=hml_need,
                              ts_yrbc=ts_yrbc, ts_yg=ts_yg):
                        v.wait_ge(HMLs[hs_].h, hml_need)
                        v.wait_ge(TS.h, max(ts_yrbc, ts_yg))
                        ins = v.scalar_tensor_tensor(
                            B[:, dc * 2048 + tt * 512:dc * 2048 + tt * 512 + 512],
                            Af[:, hs_ * 512:(hs_ + 1) * 512], mnw[:, dc:dc + 1],
                            PB[:, 1536:2048], ALU.mult, ALU.mult)
                        ins.then_inc(VS.h, 1)
                    p.on("vector", y_mul)
                    VS.n += 1
                    yld_vs[hs_] = VS.n
                    ym_k[0] += 1

            vs_y = VS.n

            s1z = Af[:, 8192:8704]
            s2z = Af[:, 8704:9216]
            zg_gate = {}
            for hc in range(NGC):
                slot = hc % 3
                ts_gate = zg_gate.get(hc - 3, TS.n if hc == 0 else 0)
                ts_gate2 = zg_gate.get(hc - 3, 0)

                def zw_ld(g, hc=hc, slot=slot, ts_gate2=ts_gate2, ws=WGs[slot],
                          vs_y=vs_y, first=(hc < 3)):
                    g.wait_ge(TS.h, ts_gate2)
                    if first:
                        g.wait_ge(VS.h, vs_y)
                    g.dma_start(
                        out=A[:, slot * 4096:slot * 4096 + 2048].rearrange(
                            "p (c m) -> p c m", m=128),
                        in_=d_wgate.rearrange("(c p) m -> p c m", p=128)[
                            :, :, hc * 128:(hc + 1) * 128]).then_inc(ws.h, 16)
                    g.dma_start(
                        out=A[:, slot * 4096 + 2048:slot * 4096 + 4096].rearrange(
                            "p (c m) -> p c m", m=128),
                        in_=d_wgate.rearrange("(c p) m -> p c m", p=128)[
                            :, :, GINTER + hc * 128:GINTER + (hc + 1) * 128]).then_inc(ws.h, 16)
                p.on("gpsimd", zw_ld)
                WGs[slot].n += 32
                wg_need = WGs[slot].n

                zj = hc % 2
                zs_gate = ZSs[zj].n

                for tt in range(NTT):
                    zgidx = hc * NTT + tt
                    pset = zgidx % 2
                    pg = PA[:, pset * 1024:pset * 1024 + 512]
                    pu = PA[:, pset * 1024 + 512:pset * 1024 + 1024]
                    ev_gate = ztail_ev[zgidx - 2] if zgidx >= 2 else (0, 0)

                    def z_mm(t, slot=slot, tt=tt, pg=pg, pu=pu, wg_need=wg_need,
                             ws=WGs[slot], ev_gate=ev_gate, vs_y=vs_y):
                        t.wait_ge(ws.h, wg_need)
                        t.wait_ge(VS.h, max(ev_gate[0], vs_y))
                        t.wait_ge(AS.h, ev_gate[1])
                        for dc in range(NDC):
                            t.matmul(pg, A[:, slot * 4096 + dc * 128:slot * 4096 + dc * 128 + 128],
                                     B[:, dc * 2048 + tt * 512:dc * 2048 + tt * 512 + 512],
                                     start=(dc == 0), stop=(dc == NDC - 1))
                        for dc in range(NDC):
                            ins = t.matmul(pu, A[:, slot * 4096 + 2048 + dc * 128:
                                                 slot * 4096 + 2048 + dc * 128 + 128],
                                           B[:, dc * 2048 + tt * 512:dc * 2048 + tt * 512 + 512],
                                           start=(dc == 0), stop=(dc == NDC - 1))
                        ins.then_inc(TS.h, 1)
                    p.on("tensor", z_mm)
                    TS.n += 1
                    ts_z = TS.n

                    def z_silu(s, pg=pg, ts_z=ts_z):
                        s.wait_ge(TS.h, ts_z)
                        ins = s.activation(s1z, pg, AF.Silu)
                        ins.then_inc(AS.h, 1)
                    p.on("scalar", z_silu)
                    AS.n += 1
                    as_z = AS.n

                    def z_mul(v, pu=pu, tt=tt, zj=zj, as_z=as_z, zs_gate=zs_gate):
                        v.wait_ge(AS.h, as_z)
                        v.wait_ge(ZSs[zj].h, zs_gate)
                        ins = v.tensor_tensor(
                            A[:, 12288 + zj * 2048 + tt * 512:12288 + zj * 2048 + tt * 512 + 512],
                            s1z, pu, ALU.mult)
                        ins.then_inc(VS.h, 1)
                    p.on("vector", z_mul)
                    VS.n += 1
                    ztail_ev.append((VS.n, as_z))

                vs_z = VS.n

                def z_st(sy, hc=hc, zj=zj, vs_z=vs_z):
                    sy.wait_ge(VS.h, vs_z)
                    sy.dma_start(out=d_zscr[hc, :, :],
                                 in_=A[:, 12288 + zj * 2048:12288 + (zj + 1) * 2048]
                                 ).then_inc(ZSs[zj].h, 16)
                p.on("sync", z_st)
                ZSs[zj].n += 16
                zg_gate[hc] = TS.n

            zs_tot = (ZSs[0].n, ZSs[1].n)

            for half in range(2):
                vs_gate = VS.n
                ts_gate = TS.n

                def xb_ld(sy, half=half, vs_gate=vs_gate, ts_gate=ts_gate):
                    sy.wait_ge(VS.h, vs_gate)
                    sy.wait_ge(TS.h, ts_gate)
                    for dcl in range(8):
                        sy.dma_start(
                            out=Bf[:, dcl * 2048:(dcl + 1) * 2048],
                            in_=d_xscr.rearrange("(c p) t -> p c t", p=128)[
                                :, half * 8 + dcl, :]).then_inc(XHL.h, 16)
                p.on("sync", xb_ld)
                XHL.n += 16 * 8
                xhl_need = XHL.n

                for tt in range(NTT):
                    ii = half * NTT + tt
                    zi = ii % 2
                    zl_gate = tb_zg[ii - 2] if ii >= 2 else 0

                    def zt_ld(sy, tt=tt, zi=zi, zl_gate=zl_gate, zs_tot=zs_tot,
                              first=(ii < 2), ts_g=TS.n):
                        sy.wait_ge(TS.h, zl_gate)
                        if first:
                            sy.wait_ge(ZSs[0].h, zs_tot[0])
                            sy.wait_ge(ZSs[1].h, zs_tot[1])
                            sy.wait_ge(TS.h, ts_g)
                        for hc in range(NGC):
                            sy.dma_start(
                                out=A[:, zi * 11264 + hc * 512:zi * 11264 + (hc + 1) * 512],
                                in_=d_zscr[hc, :, tt * 512:(tt + 1) * 512]
                            ).then_inc(ZLs[zi].h, 16)
                    p.on("sync", zt_ld)
                    ZLs[zi].n += 16 * NGC
                    zl_need = ZLs[zi].n

                    for dcl in range(8):
                        dc = half * 8 + dcl
                        wi = (ii * 8 + dcl) % 2
                        wn_gate = tb_wg[ii * 8 + dcl - 2] if ii * 8 + dcl >= 2 else 0

                        def wn_ld(g, dc=dc, wi=wi, wn_gate=wn_gate, vs_y=vs_y,
                                  first=(ii * 8 + dcl < 2)):
                            g.wait_ge(TS.h, wn_gate)
                            if first:
                                g.wait_ge(VS.h, vs_y)
                            g.dma_start(
                                out=A[:, 22528 + wi * 2816:22528 + (wi + 1) * 2816].rearrange(
                                    "p (c m) -> p c m", m=128),
                                in_=d_wdown.rearrange("(c p) m -> p c m", p=128)[
                                    :, :, dc * 128:(dc + 1) * 128]).then_inc(WNs[wi].h, 16)
                        p.on("gpsimd", wn_ld)
                        WNs[wi].n += 16
                        wn_need = WNs[wi].n

                        hs_ = (ii * 8 + dcl) % 2
                        hml_gate = HMLs[hs_].n

                        def hm2_ld(sy, dc=dc, tt=tt, hs_=hs_, vg=tbh_vs[hs_]):
                            sy.wait_ge(VS.h, vg)
                            sy.dma_start(
                                out=Af[:, 14080 + hs_ * 512:14080 + (hs_ + 1) * 512],
                                in_=d_hmid.rearrange("(c p) t -> p c t", p=128)[
                                    :, dc, tt * 512:(tt + 1) * 512]).then_inc(HMLs[hs_].h, 16)
                        p.on("sync", hm2_ld)
                        HMLs[hs_].n += 16
                        hml_need = HMLs[hs_].n

                        g3 = tb_g[0]
                        pp = PA[:, (g3 % 4) * 512:(g3 % 4) * 512 + 512]
                        ev_g = tb_ev[g3 - 4] if g3 >= 4 else 0

                        def wn_mm(t, wi=wi, zi=zi, pp=pp, wn_need=wn_need,
                                  zl_need=zl_need, ev_g=ev_g, xhl_need=xhl_need):
                            t.wait_ge(WNs[wi].h, wn_need)
                            t.wait_ge(ZLs[zi].h, zl_need)
                            t.wait_ge(VS.h, ev_g)
                            t.wait_ge(XHL.h, xhl_need)
                            for hc in range(NGC):
                                ins = t.matmul(pp,
                                               A[:, 22528 + wi * 2816 + hc * 128:
                                                 22528 + wi * 2816 + (hc + 1) * 128],
                                               A[:, zi * 11264 + hc * 512:
                                                 zi * 11264 + (hc + 1) * 512],
                                               start=(hc == 0), stop=(hc == NGC - 1))
                            ins.then_inc(TS.h, 1)
                        p.on("tensor", wn_mm)
                        TS.n += 1
                        ts_wn = TS.n

                        def x_fin(v, dcl=dcl, tt=tt, hs_=hs_, pp=pp, ts_wn=ts_wn,
                                  hml_need=hml_need):
                            v.wait_ge(TS.h, ts_wn)
                            v.wait_ge(HMLs[hs_].h, hml_need)
                            xs = Bf[:, dcl * 2048 + tt * 512:dcl * 2048 + tt * 512 + 512]
                            v.tensor_tensor(xs, xs, pp, ALU.add)
                            ins = v.tensor_tensor(
                                xs, xs, Af[:, 14080 + hs_ * 512:14080 + (hs_ + 1) * 512],
                                ALU.add)
                            ins.then_inc(VS.h, 1)
                        p.on("vector", x_fin)
                        VS.n += 1
                        tb_ev.append(VS.n)
                        tbh_vs[hs_] = VS.n
                        tb_g[0] += 1
                        tb_wg.append(TS.n)

                    tb_zg.append(TS.n)

                vs_done = VS.n

                p2out = d_xscr if 3 in phases else d_out

                def xb_st(sy, half=half, vs_done=vs_done, p2out=p2out):
                    sy.wait_ge(VS.h, vs_done)
                    for dcl in range(8):
                        sy.dma_start(
                            out=p2out.rearrange("(c p) t -> p c t", p=128)[
                                :, half * 8 + dcl, :],
                            in_=Bf[:, dcl * 2048:(dcl + 1) * 2048]).then_inc(SX.h, 16)
                p.on("sync", xb_st)
                SX.n += 16 * 8

        p2h_as = []
        p2c_as = []
        p2g_as = []
        b7_gate = [0]
        b7_read_ts = [0]
        b4_vs_gate = [0]
        b5_vs_gate = [0]
        b6_vs_gate = [0]
        b6_mm_ts = [0]
        b6_read_vs = [0]
        osn_gate = [0]
        wo_vs = []
        hm_st_gate = [0, 0]
        yld_vs = [0, 0]
        ztail_ev = []
        tb_zg = []
        tb_wg = []
        tb_ev = []
        tbh_vs = [0, 0]
        tb_g = [0]

        if 1 in phases:
            emit_ffn(0)
        if 2 in phases:
            emit_attn()
        if 3 in phases:
            emit_ffn(1)

        sx_total = SX.n

        def fin(sy, sx_total=sx_total):
            sy.wait_ge(SX.h, sx_total)
        p.on("sync", fin)

        p.replay()
    return nc


def _host_constants(core):
    b, half = core // 2, core % 2
    t0 = half * T
    pos = np.arange(t0, t0 + T, dtype=np.float64)
    invf = 10000.0 ** (-np.arange(128, dtype=np.float64) / 128.0)
    ang = pos[None, :] * invf[:, None]
    cosd = np.cos(ang).astype(np.float32)
    sind = np.sin(ang).astype(np.float32)
    gam = np.array(GAMMA, dtype=np.float64)
    i = np.arange(CC, dtype=np.float64)
    dmT = np.zeros((8 * 128, 128), np.float32)
    diT = np.zeros((8 * 128, 128), np.float32)
    for h in range(NH):
        diff = i[None, :] - i[:, None]
        m = np.where(diff >= 0, gam[h] ** np.maximum(diff, 0), 0.0) * SCL
        dmT[h * 128:(h + 1) * 128, :] = m.astype(np.float32)
        diT[h * 128:(h + 1) * 128, :] = np.broadcast_to(
            (gam[h] ** (i + 1) * SCL)[None, :], (128, 128)).astype(np.float32)
    dkcd = np.zeros((128, 8), np.float32)
    for h in range(NH):
        dkcd[:, h] = (gam[h] ** (CC - 1 - i)).astype(np.float32)
    corrd = np.zeros((8, T), np.float32)
    if half == 1:
        tloc = np.arange(T, dtype=np.float64)
        for h in range(NH):
            corrd[h] = (gam[h] ** (tloc + 1) * SCL).astype(np.float32)
    hmaskd = np.full((128, 1), 1.0 if half == 1 else 0.0, np.float32)
    return cosd, sind, dmT, diT, dkcd, corrd, hmaskd


def prepare_in_maps(inputs):
    x = np.asarray(inputs["L0_attn_hidden_states"], np.float32).reshape(4 * 4096, D)
    eye = np.eye(128, dtype=np.float32)
    shared = {
        "wq": np.ascontiguousarray(inputs["L0_attn_Wq"], np.float32),
        "wk": np.ascontiguousarray(inputs["L0_attn_Wk"], np.float32),
        "wv": np.ascontiguousarray(inputs["L0_attn_Wv"], np.float32),
        "wgt": np.ascontiguousarray(inputs["L0_attn_Wg"], np.float32),
        "wo": np.ascontiguousarray(inputs["L0_attn_Wo"], np.float32),
        "wgate": np.ascontiguousarray(inputs["L0_attn_W_gate"], np.float32),
        "wdown": np.ascontiguousarray(inputs["L0_attn_W_down"], np.float32),
        "rwT": np.ascontiguousarray(np.asarray(inputs["L0_ffn_router_w"], np.float32).T),
        "wg8": np.ascontiguousarray(inputs["L0_ffn_wg"], np.float32),
        "wu8": np.ascontiguousarray(inputs["L0_ffn_wu"], np.float32),
        "wd8": np.ascontiguousarray(inputs["L0_ffn_wd"], np.float32),
        "sg": np.ascontiguousarray(inputs["L0_ffn_sg"], np.float32),
        "su": np.ascontiguousarray(inputs["L0_ffn_su"], np.float32),
        "sd": np.ascontiguousarray(inputs["L0_ffn_sd"], np.float32),
        "anw": np.ascontiguousarray(
            np.asarray(inputs["L0_attn_attn_norm_w"], np.float32).reshape(D, 1)),
        "mnw": np.ascontiguousarray(
            np.asarray(inputs["L0_attn_mlp_norm_w"], np.float32).reshape(D, 1)),
        "gnw": np.ascontiguousarray(
            np.asarray(inputs["L0_attn_gnorm_w"], np.float32).reshape(DK, 1)),
        "cw": np.ascontiguousarray(np.stack(
            [inputs["L0_attn_conv_wq"], inputs["L0_attn_conv_wk"],
             inputs["L0_attn_conv_wv"]]).astype(np.float32)),
        "eyed": eye,
    }
    in_maps = []
    for core in range(8):
        b, half = core // 2, core % 2
        r0 = b * 4096 + half * T
        xT = np.zeros((D, T + 3), np.float32)
        xT[:, 3:] = x[r0:r0 + T].T
        if half == 1:
            xT[:, 0:3] = x[r0 - 3:r0].T
        cosd, sind, dmT, diT, dkcd, corrd, hmaskd = _host_constants(core)
        m = dict(shared)
        m.update({"xT": np.ascontiguousarray(xT), "cosd": cosd, "sind": sind,
                  "dmTd": dmT, "diTd": diT, "dkcd": dkcd, "corrd": corrd,
                  "hmaskd": hmaskd})
        in_maps.append(m)
    return in_maps


def kernel(**inputs):
    in_maps = prepare_in_maps(inputs)
    nc = build_nc()
    res = run_bass_kernel_spmd(nc, in_maps, core_ids=list(range(8)))
    out = np.empty((4, 4096, D), np.float32)
    for core in range(8):
        b, half = core // 2, core % 2
        out[b, half * T:(half + 1) * T, :] = res.results[core]["out_fm"].T
    return out
